# version 18
# baseline (speedup 1.0000x reference)
import math
import numpy as np
import ml_dtypes
import concourse.bass as bass
import concourse.mybir as mybir
from concourse.bass_utils import run_bass_kernel_spmd
from concourse.alu_op_type import AluOpType as OP
from concourse import dve_ops
from bass_rust import AP as RAP
from contextlib import ExitStack

F32 = mybir.dt.float32
BF16 = mybir.dt.bfloat16
I32 = mybir.dt.int32
I16 = mybir.dt.int16
AF = mybir.ActivationFunctionType
AX = mybir.AxisListType

NCORES = 8
D = 1024
DC = D // 128
E, FF = 8, 2816
FC = FF // 128
DQK, DV = 128, 1024
NB, MAXD = 32, 128
LN_EPS = 1e-5
ATT_EPS = 1e-6
RMS_EPS = 1e-6
ACT_EPS = 1e-5
MU = math.sqrt(0.5)
STD = math.sqrt(0.25 * math.pi)
FGROUPS = [2] * 11
BIG = 65536.0


class Ctx:

    def __init__(self, nc, sems, mksem):
        self.nc = nc
        self.eng = {"T": nc.tensor, "V": nc.vector, "S": nc.scalar,
                    "G": nc.gpsimd, "Q": nc.sync}
        self.sems = sems
        self.mksem = mksem
        self.count = {k: 0 for k in "TVSGQ"}
        self.ilog = {k: [] for k in "TVSGQ"}
        self.incs = {k: [] for k in "TVSGQ"}
        self.semval = {k: 0 for k in "TVSGQ"}
        self.dmaval = {}
        self.wm = {k: {} for k in "TVSGQ"}
        self.lastc = {k: None for k in "TVSGQ"}
        self.bufs = {}

    def _prod_sem(self, prod):
        kind, idx = prod
        if kind.startswith("D:"):
            return (kind, idx)
        for (i2, v2) in self.incs[kind]:
            if i2 >= idx:
                return (kind, v2)
        v = self.semval[kind] + 1
        self.semval[kind] = v
        self.ilog[kind][idx].then_inc(self.sems[kind], 1)
        self.incs[kind].append((idx, v))
        return (kind, v)

    def _wait(self, ename, semkey, val):
        if self.wm[ename].get(semkey, 0) >= val:
            return
        self.eng[ename].wait_ge(self.sems[semkey], val)
        self.wm[ename][semkey] = val

    def _collect(self, ename, reads, writes, needed):
        for b in reads:
            s = self.bufs.get(b)
            if s and s["w"] is not None:
                p = s["w"]
                if ename == "T" and p[0] == "T":
                    continue
                k, v = self._prod_sem(p)
                needed[k] = max(needed.get(k, 0), v)
        for b in writes:
            s = self.bufs.get(b)
            if s is None:
                continue
            deps = list(s["r"])
            if s["w"] is not None:
                deps.append(s["w"])
            for p in deps:
                if ename == "T" and p[0] == "T":
                    continue
                k, v = self._prod_sem(p)
                needed[k] = max(needed.get(k, 0), v)

    def emit(self, ename, fn, reads=(), writes=(), accum=False):
        needed = {}
        self._collect(ename, reads, () if accum else writes, needed)
        for k in sorted(needed):
            self._wait(ename, k, needed[k])
        instr = fn(self.eng[ename])
        idx = self.count[ename]
        self.count[ename] += 1
        self.ilog[ename].append(instr)
        self.lastc[ename] = idx
        prod = (ename, idx)
        for b in writes:
            if accum and b in self.bufs:
                self.bufs[b]["w"] = prod
            else:
                self.bufs[b] = {"w": prod, "r": []}
        for b in reads:
            self.bufs.setdefault(b, {"w": None, "r": []})["r"].append(prod)
        return instr

    def dma(self, qname, out, in_, reads=(), writes=(), slot=None):
        if slot is None:
            slot = writes[0] if writes else "store"
        semkey = "D:" + slot
        if semkey not in self.sems:
            self.sems[semkey] = self.mksem(slot)
            self.dmaval[semkey] = 0
        needed = {}
        self._collect(qname, reads, writes, needed)
        for k in sorted(needed):
            self._wait(qname, k, needed[k])
        instr = self.eng[qname].dma_start(out=out, in_=in_)
        idx = self.count[qname]
        self.count[qname] += 1
        self.ilog[qname].append(instr)
        self.dmaval[semkey] += 16
        v = self.dmaval[semkey]
        instr.then_inc(self.sems[semkey], 16)
        prod = (semkey, v)
        for b in writes:
            self.bufs[b] = {"w": prod, "r": []}
        for b in reads:
            self.bufs.setdefault(b, {"w": None, "r": []})["r"].append(prod)
        return instr

    def barrier(self):
        for k in "TVSGQ":
            if self.lastc[k] is not None:
                self._prod_sem((k, self.lastc[k]))
        targets = [(k, self.semval[k]) for k in "TVSGQ" if self.semval[k] > 0]
        targets += [(k, v) for k, v in self.dmaval.items() if v > 0]
        for ename in "TVSGQ":
            for k, v in targets:
                self._wait(ename, k, v)
        self.bufs = {}


def weave(gens):
    gens = [g for g in gens if g is not None]
    while gens:
        alive = []
        for g in gens:
            try:
                next(g)
                alive.append(g)
            except StopIteration:
                continue
        gens = alive


def c_hi_lo(x):
    hi = np.float32(x)
    lo = np.float32(x - float(hi))
    return float(hi), float(lo)


def build_program(nc, T):
    TC = T // 128
    NIT = T // 512
    NTB = T // 512
    S = T
    NBE = (S + 768) // 128

    dt = lambda name, shape, dty, kind: nc.dram_tensor(name, shape, dty, kind=kind).ap()
    x_in = dt("x_in", [T, D], F32, "ExternalInput")
    Wqkv = dt("L0_Wqkv", [D, 3 * D], F32, "ExternalInput")
    Wout = dt("L0_Wout", [D, D], F32, "ExternalInput")
    bout = dt("L0_bout", [D], F32, "ExternalInput")
    router_w = dt("L0_router", [D, E], F32, "ExternalInput")
    gate_p = dt("L0_gate", [E, D, FF], F32, "ExternalInput")
    up_p = dt("L0_up", [E, D, FF], F32, "ExternalInput")
    down_p = dt("L0_down", [E, FF, D], F32, "ExternalInput")
    wqk = dt("L1_wqk", [D, DQK], F32, "ExternalInput")
    bqk = dt("L1_bqk", [DQK], F32, "ExternalInput")
    gamma = dt("L1_gamma", [2, DQK], F32, "ExternalInput")
    beta = dt("L1_beta", [2, DQK], F32, "ExternalInput")
    wv = dt("L1_wv", [D, DV], F32, "ExternalInput")
    bv = dt("L1_bv", [DV], F32, "ExternalInput")
    rel_bias = dt("L1_rel", [NB], F32, "ExternalInput")
    w_int8 = dt("L1_w8", [D, D], I32, "ExternalInput")
    wscale = dt("L1_ws", [1], F32, "ExternalInput")
    bias2 = dt("L1_bias", [D], F32, "ExternalInput")
    ident = dt("c_ident", [128, 128], F32, "ExternalInput")
    jrev = dt("c_jrev", [128, 128], F32, "ExternalInput")
    ones1 = dt("c_ones1", [1, 128], F32, "ExternalInput")
    onesk = dt("c_onesk", [128, 1], BF16, "ExternalInput")
    iota8 = dt("c_iota8", [128, E], F32, "ExternalInput")
    iotaB = dt("c_iotaB", [128, E], F32, "ExternalInput")
    sel2 = dt("c_sel2", [2, 128], F32, "ExternalInput")
    sel16 = dt("c_sel16", [16, 8 * 128], BF16, "ExternalInput")
    cmasks = dt("c_masks", [128, 4 * 512], BF16, "ExternalInput")
    ohmat = dt("c_oh", [NB, S + 768], F32, "ExternalInput")
    x_out = dt("x_out", [T, D], F32, "ExternalOutput")
    bias_ext = dt("s_bias", [S + 768], F32, "Internal")
    gcache = dt("s_gcache", [E, D, FF], BF16, "Internal")
    ucache = dt("s_ucache", [E, D, FF], BF16, "Internal")
    dcache = dt("s_dcache", [E, FF, D], BF16, "Internal")
    qkvcache = dt("s_qkvc", [D, 3 * D], BF16, "Internal")
    ocache = dt("s_oc", [D, D], BF16, "Internal")
    vcache = dt("s_vc", [D, DV], BF16, "Internal")
    qkcache = dt("s_qkc", [D, DQK], BF16, "Internal")
    rcache = dt("s_rc", [D, E], BF16, "Internal")

    with ExitStack() as stk:
        sems = {}
        for k in ["T", "V", "S", "G"]:
            sems[k] = stk.enter_context(nc.semaphore("sem_" + k))
        mksem = lambda slot: stk.enter_context(nc.semaphore("dsem_" + slot.replace(":", "_")))
        ctx = Ctx(nc, sems, mksem)

        def SB(st, name, shape, dty):
            return st.enter_context(nc.sbuf_tensor(name, shape, dty))

        x = SB(stk, "x", [128, TC * 1024], F32)
        xnT_stk = stk.enter_context(ExitStack())
        xnT = xnT_stk.enter_context(nc.sbuf_tensor("xnT", [128, DC * T], BF16,
                                                   side="right"))
        s_tok = SB(stk, "s_tok", [128, TC * 1024], BF16)
        m_st = SB(stk, "m_st", [128, TC], F32)
        r_st = SB(stk, "r_st", [128, TC], F32)
        ew = SB(stk, "ew", [128, TC * E], F32)
        c_id = SB(stk, "c_id", [128, 128], F32)
        c_o1 = SB(stk, "c_o1", [1, 128], F32)
        c_ok = SB(stk, "c_ok", [128, 1], BF16)
        xn_f = SB(stk, "xn_f", [128, 1024], F32)
        lndump = SB(stk, "lndump", [128, 1024], F32)
        ssum = SB(stk, "ssum", [128, 1], F32)
        ssq = SB(stk, "ssq", [128, 1], F32)
        varv = SB(stk, "varv", [128, 1], F32)
        P = [stk.enter_context(nc.psum_tensor("P%d" % i, [128, 1024], F32))
             for i in range(4)]
        PKa = lambda i, h: "P%d:%d" % (i, h)
        PKf = lambda i: ["P%d:0" % i, "P%d:1" % i]
        PSL = lambda i, s: "P%d:0s%d" % (i, s)
        PSLall = lambda i: [PSL(i, s) for s in range(4)]

        blk = stk.enter_context(nc.Block())

        ctx.dma("Q", c_id[:], ident, writes=["c_id"])
        ctx.dma("Q", c_o1[:], ones1, writes=["c_o1"])
        ctx.dma("G", c_ok[:], onesk, writes=["c_ok"])
        ctx.dma("Q", x[:].rearrange("p (tc d) -> p tc d", d=1024),
                x_in.rearrange("(tc p) d -> p tc d", p=128),
                writes=["x%d" % t for t in range(TC)], slot="xld")
        ctx._wait("G", "D:xld", 16)
        ctx.dma("G", rcache, router_w, writes=["rcache"], slot="rcw")
        ctx.dma("G", qkvcache, Wqkv, writes=["qkvcache"], slot="qkvcw")
        ctx.dma("G", ocache, Wout, writes=["ocache"], slot="ocw")
        ctx.dma("G", qkcache, wqk, writes=["qkcache"], slot="qkcw")
        ctx.dma("G", vcache, wv, writes=["vcache"], slot="vcw")
        for e_ in range(E):
            ctx.dma("G", gcache[e_], gate_p[e_], writes=["gcache"], slot="gcw")
            ctx.dma("G", ucache[e_], up_p[e_], writes=["ucache"], slot="ucw")
            ctx.dma("G", dcache[e_], down_p[e_], writes=["dcache"], slot="dcw")

        def bcast_row(st, name, row_ap, n, pi):
            t = SB(st, name, [128, n], F32)
            bcast_row_into(t, st, name, row_ap, n, pi)
            return t

        def bcast_row_into(t, st, name, row_ap, n, pi):
            rrow = SB(st, name + "_r", [1, n], F32)
            ctx.dma("Q", rrow[:], row_ap, writes=[name + "_r"])
            for h in range(0, n, 512):
                w = min(512, n - h)
                pb = PKa(pi, (h % 1024) // 512)
                ctx.emit("T", lambda t_: t_.matmul(
                    out=P[pi][0:128, h % 1024:h % 1024 + w], lhsT=c_o1[:],
                    rhs=rrow[0:1, h:h + w], start=True, stop=True),
                    reads=["c_o1", name + "_r"], writes=[pb])
                ctx.emit("V", lambda v: v.tensor_copy(
                    out=t[:, h:h + w], in_=P[pi][0:128, h % 1024:h % 1024 + w]),
                    reads=[pb], writes=[name])
            return t

        var16 = SB(stk, "var16", [128, TC], F32)
        stot_a = SB(stk, "stot_a", [128, TC], F32)
        cq2_a = SB(stk, "cq2_a", [128, TC], F32)
        amx_a = SB(stk, "amx_a", [128, TC], F32)
        ct_a = SB(stk, "ct_a", [128, TC], F32)
        wsb2 = SB(stk, "wsb2", [128, 1], F32)

        def ln_stats():
            ctx.emit("V", lambda v: v.tensor_reduce(
                out=m_st[:], in_=x[:].rearrange("p (tc d) -> p tc d", d=1024),
                axis=AX.X, op=OP.add),
                reads=["x%d" % t for t in range(TC)], writes=["m_st"])
            for tc in range(TC):
                ctx.emit("S", lambda s: s.activation(
                    out=lndump[:], in_=x[:, tc * 1024:(tc + 1) * 1024],
                    func=AF.Square, accum_out=r_st[:, tc:tc + 1]),
                    reads=["x%d" % tc], writes=["lndump", "r_st"])
            ctx.emit("V", lambda v: v.tensor_scalar(
                out=m_st[:], in0=m_st[:], scalar1=1.0 / 1024.0, scalar2=None,
                op0=OP.mult), reads=["m_st"], writes=["m_st"])
            ctx.emit("V", lambda v: v.tensor_scalar(
                out=r_st[:], in0=r_st[:], scalar1=1.0 / 1024.0, scalar2=None,
                op0=OP.mult), reads=["r_st"], writes=["r_st"])
            ctx.emit("V", lambda v: v.tensor_tensor(
                out=var16[:], in0=m_st[:], in1=m_st[:], op=OP.mult),
                reads=["m_st"], writes=["var16"])
            ctx.emit("V", lambda v: v.tensor_tensor(
                out=var16[:], in0=r_st[:], in1=var16[:], op=OP.subtract),
                reads=["r_st", "var16"], writes=["var16"])
            ctx.emit("V", lambda v: v.tensor_scalar(
                out=var16[:], in0=var16[:], scalar1=1.0, scalar2=LN_EPS,
                op0=OP.mult, op1=OP.add), reads=["var16"], writes=["var16"])
            ctx.emit("V", lambda v: v.reciprocal(out=var16[:], in_=var16[:]),
                     reads=["var16"], writes=["var16"])
            ctx.emit("S", lambda s: s.activation(
                out=r_st[:], in_=var16[:], func=AF.Sqrt),
                reads=["var16"], writes=["r_st"])

        def ln_xn_gen(tc, rb=None, sc_all=None, sc_key=None, pingpong=False):
            xs = x[:, tc * 1024:(tc + 1) * 1024]
            ctx.emit("V", lambda v: v.tensor_scalar(
                out=xn_f[:], in0=xs, scalar1=m_st[:, tc:tc + 1],
                scalar2=r_st[:, tc:tc + 1], op0=OP.subtract, op1=OP.mult),
                reads=["x%d" % tc, "m_st", "r_st"], writes=["xn_f"])
            yield
            for dc in range(DC):
                if pingpong:
                    bk = dc % 2
                    pcol = bk * 512 + (dc % 4) // 2 * 128
                else:
                    bk = 0
                    pcol = (dc % 4) * 128
                ctx.emit("T", lambda t: t.transpose(
                    out=P[2][:, pcol:pcol + 128],
                    in_=xn_f[:, dc * 128:(dc + 1) * 128], identity=c_id[:]),
                    reads=["xn_f", "c_id"], writes=[PKa(2, bk)])
                yield
                ctx.emit("S", lambda s: s.copy(
                    out=xnT[:, dc * T + tc * 128: dc * T + (tc + 1) * 128],
                    in_=P[2][:, pcol:pcol + 128]),
                    reads=[PKa(2, bk)], writes=["xnT"])
                yield
                if rb is not None:
                    ctx.emit("T", lambda t: t.matmul(
                        out=P[2][0:128, 512:512 + E],
                        lhsT=xnT[:, dc * T + tc * 128: dc * T + (tc + 1) * 128],
                        rhs=rb[:, dc * E:(dc + 1) * E],
                        start=(dc == 0), stop=(dc == DC - 1)),
                        reads=["xnT", "rb"], writes=[PKa(2, 1)], accum=(dc > 0))
                    yield
            if rb is not None:
                ctx.emit("V", lambda v: v.tensor_copy(
                    out=sc_all[:, tc * E:(tc + 1) * E],
                    in_=P[2][0:128, 512:512 + E]),
                    reads=[PKa(2, 1)], writes=[sc_key])
                yield

        def ln_xn(tc, rb=None, sc_all=None, sc_key=None):
            for _ in ln_xn_gen(tc, rb, sc_all, sc_key):
                pass

        if True:
            st0 = stk
            rb = SB(st0, "rb", [128, DC * E], BF16)
            sc_all = SB(st0, "sc_all", [128, TC * E], F32)
            ctx.dma("Q", rb[:].rearrange("p (dc e) -> p dc e", e=E),
                    rcache.rearrange("(dc p) e -> p dc e", p=128),
                    reads=["rcache"], writes=["rb"])
            m1n = SB(st0, "m1n", [128, 1], F32)
            m2n = SB(st0, "m2n", [128, 1], F32)
            eqt = SB(st0, "eqt", [128, E], F32)
            at = SB(st0, "at", [128, E], F32)
            i1t = SB(st0, "i1t", [128, 1], F32)
            mk1 = SB(st0, "mk1", [128, E], F32)
            mk2 = SB(st0, "mk2", [128, E], F32)
            smk = SB(st0, "smk", [128, E], F32)
            e2t = SB(st0, "e2t", [128, 1], F32)
            w1t = SB(st0, "w1t", [128, 1], F32)
            w2t = SB(st0, "w2t", [128, 1], F32)
            ewt = SB(st0, "ewt", [128, E], F32)
            c_i8 = SB(st0, "c_i8", [128, E], F32)
            c_iB = SB(st0, "c_iB", [128, E], F32)
            ctx.dma("Q", c_i8[:], iota8, writes=["c_i8"])
            ctx.dma("Q", c_iB[:], iotaB, writes=["c_iB"])

            def topk_tc(tc):
                scs = sc_all[:, tc * E:(tc + 1) * E]
                ctx.emit("V", lambda v: v.tensor_reduce(
                    out=m1n[:], in_=scs, axis=AX.X, op=OP.max, negate=True),
                    reads=["sc_all"], writes=["m1n"])
                ctx.emit("V", lambda v: v.tensor_scalar(
                    out=eqt[:], in0=scs, scalar1=m1n[:], scalar2=0.0,
                    op0=OP.add, op1=OP.is_equal),
                    reads=["sc_all", "m1n"], writes=["eqt"])
                ctx.emit("V", lambda v: v.scalar_tensor_tensor(
                    out=at[:], in0=eqt[:], scalar=-BIG, in1=c_iB[:],
                    op0=OP.mult, op1=OP.add), reads=["eqt", "c_iB"], writes=["at"])
                ctx.emit("V", lambda v: v.tensor_reduce(
                    out=i1t[:], in_=at[:], axis=AX.X, op=OP.min),
                    reads=["at"], writes=["i1t"])
                ctx.emit("V", lambda v: v.tensor_scalar(
                    out=mk1[:], in0=c_i8[:], scalar1=i1t[:], scalar2=0.0,
                    op0=OP.subtract, op1=OP.is_equal),
                    reads=["c_i8", "i1t"], writes=["mk1"])
                ctx.emit("V", lambda v: v.scalar_tensor_tensor(
                    out=smk[:], in0=mk1[:], scalar=-BIG, in1=scs,
                    op0=OP.mult, op1=OP.add), reads=["mk1", "sc_all"], writes=["smk"])
                ctx.emit("V", lambda v: v.tensor_reduce(
                    out=m2n[:], in_=smk[:], axis=AX.X, op=OP.max, negate=True),
                    reads=["smk"], writes=["m2n"])
                ctx.emit("V", lambda v: v.tensor_scalar(
                    out=eqt[:], in0=smk[:], scalar1=m2n[:], scalar2=0.0,
                    op0=OP.add, op1=OP.is_equal), reads=["smk", "m2n"], writes=["eqt"])
                ctx.emit("V", lambda v: v.scalar_tensor_tensor(
                    out=at[:], in0=eqt[:], scalar=-BIG, in1=c_iB[:],
                    op0=OP.mult, op1=OP.add), reads=["eqt", "c_iB"], writes=["at"])
                ctx.emit("V", lambda v: v.tensor_reduce(
                    out=i1t[:], in_=at[:], axis=AX.X, op=OP.min),
                    reads=["at"], writes=["i1t"])
                ctx.emit("V", lambda v: v.tensor_scalar(
                    out=mk2[:], in0=c_i8[:], scalar1=i1t[:], scalar2=0.0,
                    op0=OP.subtract, op1=OP.is_equal),
                    reads=["c_i8", "i1t"], writes=["mk2"])
                ctx.emit("V", lambda v: v.tensor_scalar(
                    out=e2t[:], in0=m1n[:], scalar1=m2n[:], scalar2=None,
                    op0=OP.subtract), reads=["m1n", "m2n"], writes=["e2t"])
                ctx.emit("S", lambda s: s.activation(
                    out=e2t[:], in_=e2t[:], func=AF.Exp),
                    reads=["e2t"], writes=["e2t"])
                ctx.emit("V", lambda v: v.tensor_scalar(
                    out=w1t[:], in0=e2t[:], scalar1=1.0, scalar2=None, op0=OP.add),
                    reads=["e2t"], writes=["w1t"])
                ctx.emit("V", lambda v: v.reciprocal(out=w1t[:], in_=w1t[:]),
                         reads=["w1t"], writes=["w1t"])
                ctx.emit("V", lambda v: v.tensor_scalar(
                    out=w2t[:], in0=e2t[:], scalar1=w1t[:], scalar2=None,
                    op0=OP.mult), reads=["e2t", "w1t"], writes=["w2t"])
                ctx.emit("V", lambda v: v.tensor_scalar(
                    out=ewt[:], in0=mk2[:], scalar1=w2t[:], scalar2=None,
                    op0=OP.mult), reads=["mk2", "w2t"], writes=["ewt"])
                ctx.emit("V", lambda v: v.scalar_tensor_tensor(
                    out=ew[:, tc * E:(tc + 1) * E], in0=mk1[:], scalar=w1t[:],
                    in1=ewt[:], op0=OP.mult, op1=OP.add),
                    reads=["mk1", "w1t", "ewt"], writes=["ew"])


        stAB = stk.enter_context(ExitStack())
        kvb = SB(stAB, "kvb", [128, 8 * 128], BF16)
        scb = SB(stAB, "scb", [128, 16], BF16)
        with ExitStack() as stA:
            wkv = SB(stA, "wkv", [128, DC * 2048], BF16)
            ctx.dma("Q", wkv[:].rearrange("p (dc f) -> p dc f", f=2048),
                    qkvcache.rearrange("(dc p) f -> p dc f", p=128)[:, :, 1024:3072],
                    reads=["qkvcache"], writes=["wkv"])
            k_bfs = [SB(stA, "k_bf%d" % i, [128, 1024], BF16) for i in range(2)]
            v_bfs = [SB(stA, "v_bf%d" % i, [128, 1024], BF16) for i in range(2)]
            eltmps = [SB(stA, "eltmp%d" % i, [128, 512], F32) for i in range(2)]
            eltmp2s = [SB(stA, "eltmp2%d" % i, [128, 512], F32) for i in range(2)]
            ksum = SB(stA, "ksum", [1, 1024], F32)
            ctx.emit("V", lambda v: v.memset(ksum[:], 0.0), writes=["ksum"])

            def projkv_gen(tc):
                k_bf, k_bfk = k_bfs[tc % 2], "k_bf%d" % (tc % 2)
                v_bf, v_bfk = v_bfs[tc % 2], "v_bf%d" % (tc % 2)
                for part, dst, dstk in ((0, k_bf, k_bfk), (1, v_bf, v_bfk)):
                    for h in range(2):
                        pt, ptk = P[part], PKa(part, h)
                        pcol = h * 512
                        eltmp, eltmp_k = eltmps[h], "eltmp%d" % h
                        eltmp2, eltmp2_k = eltmp2s[h], "eltmp2%d" % h
                        for dc in range(DC):
                            ctx.emit("T", lambda t: t.matmul(
                                out=pt[:, pcol:pcol + 512],
                                lhsT=xnT[:, dc * T + tc * 128: dc * T + (tc + 1) * 128],
                                rhs=wkv[:, dc * 2048 + part * 1024 + pcol:
                                        dc * 2048 + part * 1024 + pcol + 512],
                                start=(dc == 0), stop=(dc == DC - 1)),
                                reads=["xnT", "wkv"], writes=[ptk], accum=(dc > 0))
                            yield
                        if part == 0:
                            ctx.emit("V", lambda v: v.tensor_scalar(
                                out=eltmp[:], in0=pt[:, pcol:pcol + 512],
                                scalar1=0.0, scalar2=None, op0=OP.min),
                                reads=[ptk], writes=[eltmp_k])
                            yield
                            ctx.emit("S", lambda s: s.activation(
                                out=eltmp2[:], in_=eltmp[:], func=AF.Exp),
                                reads=[eltmp_k], writes=[eltmp2_k])
                            yield
                            ctx.emit("V", lambda v: v.scalar_tensor_tensor(
                                out=dst[:, pcol:pcol + 512],
                                in0=pt[:, pcol:pcol + 512], scalar=0.0,
                                in1=eltmp2[:], op0=OP.max, op1=OP.add),
                                reads=[ptk, eltmp2_k], writes=[dstk])
                            yield
                        else:
                            ctx.emit("S", lambda s: s.copy(
                                out=dst[:, pcol:pcol + 512],
                                in_=pt[:, pcol:pcol + 512]),
                                reads=[ptk], writes=[dstk])
                            yield
                for c in range(8):
                    st_ = (tc == 0 and c in (0, 4))
                    sp_ = (tc == TC - 1 and c in (3, 7))
                    ctx.emit("T", lambda t: t.matmul(
                        out=P[3][:, c * 128:(c + 1) * 128],
                        lhsT=k_bf[:, c * 128:(c + 1) * 128],
                        rhs=v_bf[:, c * 128:(c + 1) * 128],
                        start=st_, stop=sp_),
                        reads=[k_bfk, v_bfk], writes=PKf(3),
                        accum=not (tc == 0 and c == 0))
                    yield
                for h in range(2):
                    ctx.emit("T", lambda t: t.matmul(
                        out=P[2][0:1, 0:512], lhsT=c_ok[:],
                        rhs=k_bf[:, h * 512:(h + 1) * 512], start=True, stop=True),
                        reads=["c_ok", k_bfk], writes=[PKa(2, 0)])
                    yield
                    ctx.emit("V", lambda v: v.scalar_tensor_tensor(
                        out=ksum[0:1, h * 512:(h + 1) * 512],
                        in0=P[2][0:1, 0:512], scalar=1.0,
                        in1=ksum[0:1, h * 512:(h + 1) * 512],
                        op0=OP.mult, op1=OP.add),
                        reads=[PKa(2, 0), "ksum"], writes=["ksum"])
                    yield
                topk_tc(tc)

            ln_stats()
            ln_xn(0, rb=rb, sc_all=sc_all, sc_key="sc_all")
            for tc in range(TC):
                nxt = (ln_xn_gen(tc + 1, rb, sc_all, "sc_all")
                       if tc + 1 < TC else None)
                weave([nxt, projkv_gen(tc)])
            ctx.emit("V", lambda v: v.memset(kvb[:], 0.0), writes=["kvb"])
            for c in range(8):
                ctx.emit("V", lambda v: v.tensor_copy(
                    out=kvb[0:64, c * 128:c * 128 + 64],
                    in_=P[3][0:64, c * 128:c * 128 + 64]),
                    reads=[PKa(3, c // 4)], writes=["kvb"])
                ctx.emit("V", lambda v: v.tensor_copy(
                    out=kvb[64:128, c * 128 + 64:c * 128 + 128],
                    in_=P[3][64:128, c * 128 + 64:c * 128 + 128]),
                    reads=[PKa(3, c // 4)], writes=["kvb"])
            ksb = SB(stA, "ksb", [1, 1024], BF16)
            ctx.emit("V", lambda v: v.tensor_copy(out=ksb[:], in_=ksum[:]),
                     reads=["ksum"], writes=["ksb"])
            ctx.emit("V", lambda v: v.memset(scb[:], 0.0), writes=["scb"])
            for c in range(8):
                ctx.dma("Q", scb[0:64, 2 * c:2 * c + 1],
                        ksb[0:1, c * 128:c * 128 + 64],
                        reads=["ksb"], writes=["scb"])
                ctx.dma("Q", scb[64:128, 2 * c + 1:2 * c + 2],
                        ksb[0:1, c * 128 + 64:c * 128 + 128],
                        reads=["ksb"], writes=["scb"])
            ctx.barrier()

        with ExitStack() as stB:
            c_s2 = SB(stB, "c_s2", [2, 128], BF16)
            ctx.dma("G", c_s2[:], sel2, writes=["c_s2"])
            wqo = SB(stB, "wqo", [128, DC * 1024], BF16)
            bout_b = bcast_row(stB, "bout_b", bout.rearrange("(o d) -> o d", o=1),
                               D, 2)
            q_f = SB(stB, "q_f", [128, 1024], F32)
            qT = SB(stB, "qT", [128, DC * 512], BF16)
            att_fm = SB(stB, "att_fm", [128, 8 * 512], BF16)
            r2ab = [SB(stB, "r2ab%d" % i, [2, 512], BF16) for i in range(2)]
            r2af = [SB(stB, "r2af%d" % i, [2, 512], F32) for i in range(2)]
            r2ts = [SB(stB, "r2t%d" % i, [128, 512], F32) for i in range(2)]
            eltmp = SB(stB, "eltmpB", [128, 512], F32)
            eltmp2 = SB(stB, "eltmpB2", [128, 512], F32)
            for it in range(NIT):
                ctx.dma("Q", wqo[:].rearrange("p (dc f) -> p dc f", f=1024),
                        qkvcache.rearrange("(dc p) f -> p dc f", p=128)[:, :, 0:1024],
                        reads=["qkvcache"], writes=["wqo"])
                for lc in range(4):
                    tc = it * 4 + lc
                    for h in range(2):
                        for dc in range(DC):
                            ctx.emit("T", lambda t: t.matmul(
                                out=P[h][:, 0:512],
                                lhsT=xnT[:, dc * T + tc * 128: dc * T + (tc + 1) * 128],
                                rhs=wqo[:, dc * 1024 + h * 512: dc * 1024 + h * 512 + 512],
                                start=(dc == 0), stop=(dc == DC - 1)),
                                reads=["xnT", "wqo"], writes=[PKa(h, 0)], accum=(dc > 0))
                        ctx.emit("V", lambda v: v.tensor_scalar(
                            out=eltmp[:], in0=P[h][:, 0:512], scalar1=0.0,
                            scalar2=None, op0=OP.min), reads=[PKa(h, 0)], writes=["eltmpB"])
                        ctx.emit("S", lambda s: s.activation(
                            out=eltmp2[:], in_=eltmp[:], func=AF.Exp),
                            reads=["eltmpB"], writes=["eltmpB2"])
                        ctx.emit("V", lambda v: v.scalar_tensor_tensor(
                            out=q_f[:, h * 512:(h + 1) * 512], in0=P[h][:, 0:512],
                            scalar=0.0, in1=eltmp2[:], op0=OP.max, op1=OP.add),
                            reads=[PKa(h, 0), "eltmpB2"], writes=["q_f"])
                    for dc in range(DC):
                        bk = dc % 2
                        pcol = bk * 512 + ((dc % 4) // 2) * 128
                        ctx.emit("T", lambda t: t.transpose(
                            out=P[2][:, pcol:pcol + 128],
                            in_=q_f[:, dc * 128:(dc + 1) * 128], identity=c_id[:]),
                            reads=["q_f", "c_id"], writes=[PKa(2, bk)])
                        ctx.emit("S", lambda s: s.copy(
                            out=qT[:, dc * 512 + lc * 128: dc * 512 + (lc + 1) * 128],
                            in_=P[2][:, pcol:pcol + 128]),
                            reads=[PKa(2, bk)], writes=["qT"])
                ctx.dma("S", wqo[:].rearrange("p (fc f) -> p fc f", f=1024),
                        ocache.rearrange("(fc p) f -> p fc f", p=128),
                        reads=["ocache"], writes=["wqo"])
                for c in range(8):
                    cp = c % 2
                    pq, pqk = (P[2], PKa(2, 1)) if cp == 0 else (P[3], PKa(3, 1))
                    ctx.emit("T", lambda t: t.matmul(
                        out=pq[0:2, 512:1024], lhsT=scb[:, 2 * c:2 * c + 2],
                        rhs=qT[:, c * 512:(c + 1) * 512], start=True, stop=True),
                        reads=["scb", "qT"], writes=[pqk])
                    ctx.emit("V", lambda v: v.tensor_scalar(
                        out=r2af[cp][:], in0=pq[0:2, 512:1024], scalar1=ATT_EPS,
                        scalar2=None, op0=OP.add),
                        reads=[pqk], writes=["r2af%d" % cp])
                    ctx.emit("V", lambda v: v.reciprocal(
                        out=r2af[cp][:], in_=r2af[cp][:]),
                        reads=["r2af%d" % cp], writes=["r2af%d" % cp])
                    ctx.emit("V", lambda v: v.tensor_copy(
                        out=r2ab[cp][:], in_=r2af[cp][:]),
                        reads=["r2af%d" % cp], writes=["r2ab%d" % cp])
                    ctx.emit("T", lambda t: t.matmul(
                        out=P[1][:, cp * 512:cp * 512 + 512],
                        lhsT=c_s2[:], rhs=r2ab[cp][:],
                        start=True, stop=True),
                        reads=["c_s2", "r2ab%d" % cp], writes=[PKa(1, cp)])
                    ctx.emit("V", lambda v: v.tensor_copy(
                        out=r2ts[cp][:], in_=P[1][:, cp * 512:cp * 512 + 512]),
                        reads=[PKa(1, cp)], writes=["r2t%d" % cp])
                    ctx.emit("T", lambda t: t.matmul(
                        out=P[0][:, cp * 512:cp * 512 + 512],
                        lhsT=kvb[:, c * 128:(c + 1) * 128],
                        rhs=qT[:, c * 512:(c + 1) * 512], start=True, stop=True),
                        reads=["kvb", "qT"], writes=[PKa(0, cp)])
                    ctx.emit("V", lambda v: v.tensor_tensor(
                        out=att_fm[:, c * 512:(c + 1) * 512],
                        in0=P[0][:, cp * 512:cp * 512 + 512],
                        in1=r2ts[cp][:], op=OP.mult),
                        reads=[PKa(0, cp), "r2t%d" % cp], writes=["att_fm"])
                for lc in range(4):
                    tc = it * 4 + lc
                    for h in range(2):
                        for fc in range(DC):
                            ctx.emit("T", lambda t: t.matmul(
                                out=P[3][:, h * 512:(h + 1) * 512],
                                lhsT=att_fm[:, fc * 512 + lc * 128:
                                            fc * 512 + (lc + 1) * 128],
                                rhs=wqo[:, fc * 1024 + h * 512:
                                        fc * 1024 + h * 512 + 512],
                                start=(fc == 0), stop=(fc == DC - 1)),
                                reads=["att_fm", "wqo"], writes=[PKa(3, h)],
                                accum=(fc > 0))
                        ctx.emit("V", lambda v: v.scalar_tensor_tensor(
                            out=s_tok[:, tc * 1024 + h * 512:
                                      tc * 1024 + (h + 1) * 512],
                            in0=P[3][:, h * 512:(h + 1) * 512], scalar=1.0,
                            in1=bout_b[:, h * 512:(h + 1) * 512],
                            op0=OP.mult, op1=OP.add),
                            reads=[PKa(3, h), "bout_b"], writes=["s%d" % tc])
            ctx.barrier()
        stAB.close()

        with ExitStack() as stM:
            h_w = SB(stM, "h_w", [128, FC * 512], BF16)
            gacc = SB(stM, "gacc", [128, 4 * 1024], F32)
            gbufs = [SB(stM, "gbuf%d" % i, [128, DC * 256], BF16) for i in range(2)]
            ubufs = [SB(stM, "ubuf%d" % i, [128, DC * 256], BF16) for i in range(2)]
            dbufs = [SB(stM, "dbuf%d" % i, [128, 2 * 1024], BF16) for i in range(2)]
            gsils = [xn_f[:, 0:512], xn_f[:, 512:1024]]
            gtmp, gtmp_k = lndump, "lndump"
            GOFF = [2 * i for i in range(11)]
            DGROUPS = [2] * 11
            GOFFD = [2 * i for i in range(11)]

            units = []
            for tb in range(NTB):
                for e in range(E):
                    for g in range(len(FGROUPS)):
                        units.append(("gu", tb, e, g))
                    for g in range(len(DGROUPS)):
                        units.append(("dn", tb, e, g))
                units.append(("fin", tb, 0, 0))

            gu_ct = [0]
            dn_ct = [0]

            def issue_dma(u):
                kind, tb, e, g = u
                if kind == "fin":
                    return
                if kind == "gu":
                    ng = FGROUPS[g]
                    fc0 = GOFF[g]
                    w_ = ng * 128
                    sl = gu_ct[0] % 2
                    gu_ct[0] += 1
                    gb, ub = gbufs[sl], ubufs[sl]
                    gk, uk = "gbuf%d" % sl, "ubuf%d" % sl
                    ctx.dma("Q", gb[:, 0:DC * w_].rearrange(
                        "p (dc f) -> p dc f", f=w_),
                        gcache.rearrange("e (dc p) f -> e p dc f", p=128)
                        [e, :, :, fc0 * 128:(fc0 + ng) * 128],
                        reads=["gcache"], writes=[gk], slot=gk)
                    ctx.dma("S", ub[:, 0:DC * w_].rearrange(
                        "p (dc f) -> p dc f", f=w_),
                        ucache.rearrange("e (dc p) f -> e p dc f", p=128)
                        [e, :, :, fc0 * 128:(fc0 + ng) * 128],
                        reads=["ucache"], writes=[uk], slot=uk)
                else:
                    ng = DGROUPS[g]
                    fc0 = GOFFD[g]
                    sl = dn_ct[0] % 2
                    dn_ct[0] += 1
                    db = dbufs[sl]
                    dk = "dbuf%d" % sl
                    ctx.dma("S" if g % 2 else "Q",
                            db[:, 0:ng * 1024].rearrange(
                                "p (fc f) -> p fc f", f=1024),
                            dcache.rearrange("e (fc p) f -> e p fc f", p=128)
                            [e, :, fc0:fc0 + ng, :],
                            reads=["dcache"], writes=[dk], slot=dk)

            cg_ct = [0]
            cd_ct = [0]

            def emit_compute(u):
                kind, tb, e, g = u
                if kind == "fin":
                    for lc in range(4):
                        tc = tb * 4 + lc
                        ctx.emit("S", lambda s: s.activation(
                            out=gtmp[:], in_=gacc[:, lc * 1024:(lc + 1) * 1024],
                            func=AF.Silu), reads=["gacc"], writes=[gtmp_k])
                        ctx.emit("V", lambda v: v.tensor_tensor(
                            out=gtmp[:], in0=gtmp[:],
                            in1=s_tok[:, tc * 1024:(tc + 1) * 1024], op=OP.mult),
                            reads=[gtmp_k, "s%d" % tc], writes=[gtmp_k])
                        ctx.emit("V", lambda v: v.tensor_tensor(
                            out=x[:, tc * 1024:(tc + 1) * 1024],
                            in0=x[:, tc * 1024:(tc + 1) * 1024], in1=gtmp[:],
                            op=OP.add), reads=[gtmp_k, "x%d" % tc],
                            writes=["x%d" % tc])
                    return
                if kind == "gu":
                    ng = FGROUPS[g]
                    fc0 = GOFF[g]
                    w_ = ng * 128
                    sl = cg_ct[0] % 2
                    cg_ct[0] += 1
                    gb, ub = gbufs[sl], ubufs[sl]
                    gk, uk = "gbuf%d" % sl, "ubuf%d" % sl
                    for fl in range(ng):
                        fc = fc0 + fl
                        pcg = (fc % 2) * 512
                        gsb = gsils[fc % 2]
                        gsk = "xn_f:%d" % (fc % 2)
                        for dc in range(DC):
                            ctx.emit("T", lambda t: t.matmul(
                                out=P[0][:, pcg:pcg + 512],
                                lhsT=gb[:, dc * w_ + fl * 128: dc * w_ + (fl + 1) * 128],
                                rhs=xnT[:, dc * T + tb * 512: dc * T + (tb + 1) * 512],
                                start=(dc == 0), stop=(dc == DC - 1)),
                                reads=[gk, "xnT"], writes=[PKa(0, fc % 2)],
                                accum=(dc > 0))
                        ctx.emit("S", lambda s: s.activation(
                            out=gsb, in_=P[0][:, pcg:pcg + 512], func=AF.Silu),
                            reads=[PKa(0, fc % 2)], writes=[gsk])
                        for dc in range(DC):
                            ctx.emit("T", lambda t: t.matmul(
                                out=P[1][:, pcg:pcg + 512],
                                lhsT=ub[:, dc * w_ + fl * 128: dc * w_ + (fl + 1) * 128],
                                rhs=xnT[:, dc * T + tb * 512: dc * T + (tb + 1) * 512],
                                start=(dc == 0), stop=(dc == DC - 1)),
                                reads=[uk, "xnT"], writes=[PKa(1, fc % 2)],
                                accum=(dc > 0))
                        ctx.emit("V", lambda v: v.tensor_tensor(
                            out=h_w[:, fc * 512:(fc + 1) * 512],
                            in0=P[1][:, pcg:pcg + 512], in1=gsb, op=OP.mult),
                            reads=[PKa(1, fc % 2), gsk], writes=["h_w"])
                else:
                    ng = DGROUPS[g]
                    fc0 = GOFFD[g]
                    sl = cd_ct[0] % 2
                    cd_ct[0] += 1
                    db = dbufs[sl]
                    dk = "dbuf%d" % sl
                    for fl in range(ng):
                        fc = fc0 + fl
                        for lc in range(4):
                            for h in range(2):
                                ctx.emit("T", lambda t: t.matmul(
                                    out=P[lc][:, h * 512:(h + 1) * 512],
                                    lhsT=h_w[:, fc * 512 + lc * 128:
                                             fc * 512 + (lc + 1) * 128],
                                    rhs=db[:, fl * 1024 + h * 512:
                                           fl * 1024 + h * 512 + 512],
                                    start=(fc == 0), stop=(fc == FC - 1)),
                                    reads=["h_w", dk], writes=[PKa(lc, h)],
                                    accum=(fc > 0))
                    if g == len(DGROUPS) - 1:
                        for lc in range(4):
                            tc = tb * 4 + lc
                            wecol = ew[:, tc * E + e: tc * E + e + 1]
                            if e == 0:
                                ctx.emit("V", lambda v: v.tensor_scalar(
                                    out=gacc[:, lc * 1024:(lc + 1) * 1024],
                                    in0=P[lc][:], scalar1=wecol, scalar2=None,
                                    op0=OP.mult),
                                    reads=PKf(lc) + ["ew"], writes=["gacc"])
                            else:
                                ctx.emit("V", lambda v: v.scalar_tensor_tensor(
                                    out=gacc[:, lc * 1024:(lc + 1) * 1024],
                                    in0=P[lc][:], scalar=wecol,
                                    in1=gacc[:, lc * 1024:(lc + 1) * 1024],
                                    op0=OP.mult, op1=OP.add),
                                    reads=PKf(lc) + ["ew", "gacc"], writes=["gacc"])

            pend = 0
            for i in range(len(units)):
                while pend <= i + 1 and pend < len(units):
                    issue_dma(units[pend])
                    pend += 1
                emit_compute(units[i])
            ctx.barrier()

        c2f = math.sqrt(128.0) / STD
        c3f = MU / STD
        ln_stats()
        ctx.dma("Q", wsb2[:], RAP(tensor=wscale.tensor, offset=0,
                                  ap=[[0, 128], [1, 1]]), writes=["wsb2"])
        with ExitStack() as stb:
            c_ohT = SB(stb, "c_ohT", [NB, S + 768], F32)
            ctx.dma("Q", c_ohT[:], ohmat, writes=["c_ohT"])
            relb = SB(stb, "relb", [NB, 1], F32)
            ctx.dma("Q", relb[:], rel_bias.rearrange("(b o) -> b o", o=1),
                    writes=["relb"])
            btmp = SB(stb, "btmp", [128, 128], F32)
            bt2 = SB(stb, "bt2", [128, 128], F32)

            def biasext_gen():
                ctx.emit("V", lambda v: v.memset(btmp[:], 0.0), writes=["btmp"])
                yield
                for ch in range(NBE):
                    ctx.emit("T", lambda t: t.matmul(
                        out=P[1][:, 0:1], lhsT=c_ohT[:, ch * 128:(ch + 1) * 128],
                        rhs=relb[:], start=True, stop=True),
                        reads=["c_ohT", "relb"], writes=[PKa(1, 0)])
                    yield
                    ctx.emit("V", lambda v: v.tensor_scalar(
                        out=btmp[:, ch:ch + 1], in0=P[1][:, 0:1], scalar1=c2f,
                        scalar2=-c3f, op0=OP.mult, op1=OP.add),
                        reads=[PKa(1, 0)], writes=["btmp"])
                    yield
                ctx.emit("T", lambda t: t.transpose(
                    out=P[1][:, 0:128], in_=btmp[:], identity=c_id[:]),
                    reads=["btmp", "c_id"], writes=[PKa(1, 0)])
                yield
                ctx.emit("V", lambda v: v.tensor_copy(
                    out=bt2[:], in_=P[1][:, 0:128]), reads=[PKa(1, 0)], writes=["bt2"])
                yield
                ctx.dma("Q", bias_ext.rearrange("(c p) -> c p", p=128),
                        bt2[0:NBE, :], reads=["bt2"], writes=["bias_ext"])
                yield

            def qstats_gen():
                for tc in range(TC):
                    xs = x[:, tc * 1024:(tc + 1) * 1024]
                    ctx.emit("V", lambda v: v.tensor_scalar(
                        out=lndump[:], in0=xs, scalar1=m_st[:, tc:tc + 1],
                        scalar2=r_st[:, tc:tc + 1], op0=OP.subtract, op1=OP.mult),
                        reads=["x%d" % tc, "m_st", "r_st"], writes=["lndump"])
                    yield
                    ctx.emit("V", lambda v: v.tensor_reduce(
                        out=amx_a[:, tc:tc + 1], in_=lndump[:], axis=AX.X,
                        op=OP.max, apply_absolute_value=True),
                        reads=["lndump"], writes=["amx_a"])
                    yield
                    ctx.emit("S", lambda s: s.activation(
                        out=lndump[:], in_=lndump[:], func=AF.Square,
                        accum_out=stot_a[:, tc:tc + 1]),
                        reads=["lndump"], writes=["lndump", "stot_a"])
                    yield
                ctx.emit("V", lambda v: v.tensor_scalar(
                    out=stot_a[:], in0=stot_a[:], scalar1=1.0 / 1024.0,
                    scalar2=RMS_EPS, op0=OP.mult, op1=OP.add),
                    reads=["stot_a"], writes=["stot_a"])
                yield
                ctx.emit("V", lambda v: v.reciprocal(out=stot_a[:], in_=stot_a[:]),
                         reads=["stot_a"], writes=["stot_a"])
                yield
                ctx.emit("S", lambda s: s.activation(
                    out=stot_a[:], in_=stot_a[:], func=AF.Sqrt),
                    reads=["stot_a"], writes=["stot_a"])
                yield
                ctx.emit("V", lambda v: v.tensor_tensor(
                    out=ct_a[:], in0=amx_a[:], in1=stot_a[:], op=OP.mult),
                    reads=["amx_a", "stot_a"], writes=["ct_a"])
                yield
                ctx.emit("V", lambda v: v.tensor_scalar(
                    out=ct_a[:], in0=ct_a[:], scalar1=ACT_EPS, scalar2=None,
                    op0=OP.max), reads=["ct_a"], writes=["ct_a"])
                yield
                ctx.emit("V", lambda v: v.reciprocal(out=amx_a[:], in_=ct_a[:]),
                         reads=["ct_a"], writes=["amx_a"])
                yield
                ctx.emit("V", lambda v: v.tensor_tensor(
                    out=stot_a[:], in0=stot_a[:], in1=amx_a[:], op=OP.mult),
                    reads=["stot_a", "amx_a"], writes=["stot_a"])
                yield
                ctx.emit("V", lambda v: v.tensor_scalar(
                    out=stot_a[:], in0=stot_a[:], scalar1=127.0, scalar2=None,
                    op0=OP.mult), reads=["stot_a"], writes=["stot_a"])
                yield
                ctx.emit("V", lambda v: v.tensor_scalar(
                    out=cq2_a[:], in0=ct_a[:], scalar1=wsb2[:], scalar2=None,
                    op0=OP.mult), reads=["ct_a", "wsb2"], writes=["cq2_a"])
                yield
                ctx.emit("V", lambda v: v.tensor_scalar(
                    out=cq2_a[:], in0=cq2_a[:], scalar1=1.0 / 127.0, scalar2=None,
                    op0=OP.mult), reads=["cq2_a"], writes=["cq2_a"])
                yield
                ctx.emit("V", lambda v: v.tensor_tensor(
                    out=amx_a[:], in0=r_st[:], in1=stot_a[:], op=OP.mult),
                    reads=["r_st", "stot_a"], writes=["amx_a"])
                yield
                ctx.emit("V", lambda v: v.scalar_tensor_tensor(
                    out=ct_a[:], in0=m_st[:], scalar=-1.0, in1=amx_a[:],
                    op0=OP.mult, op1=OP.mult),
                    reads=["m_st", "amx_a"], writes=["ct_a"])
                yield

            def ln2_gen():
                for tc in range(TC):
                    for _ in ln_xn_gen(tc, pingpong=True):
                        yield

            weave([ln2_gen(), biasext_gen(), qstats_gen()])
            ctx.barrier()

        with ExitStack() as st1:

            bias2_b = SB(st1, "bias2_b", [128, D], F32)
            stAtt = st1.enter_context(ExitStack())
            qf = SB(stAtt, "qf", [128, T], BF16)
            kf = SB(stAtt, "kf", [128, T], BF16)
            with ExitStack() as stq:
                qk_f = SB(stq, "qk_f", [128, T], F32)
                wqkb = SB(stq, "wqkb", [128, DC * DQK], BF16)
                ctx.dma("Q", wqkb[:].rearrange("p (dc f) -> p dc f", f=DQK),
                        qkcache.rearrange("(dc p) f -> p dc f", p=128),
                        reads=["qkcache"], writes=["wqkb"])
                bqk_t = SB(stq, "bqk_t", [128, 1], F32)
                ctx.dma("Q", bqk_t[:], bqk.rearrange("(p o) -> p o", o=1),
                        writes=["bqk_t"])
                g0 = SB(stq, "g0", [128, 1], F32)
                g1 = SB(stq, "g1", [128, 1], F32)
                b0 = SB(stq, "b0", [128, 1], F32)
                b1 = SB(stq, "b1", [128, 1], F32)
                ctx.dma("Q", g0[:], RAP(tensor=gamma.tensor, offset=0,
                                        ap=[[1, 128], [1, 1]]), writes=["g0"])
                ctx.dma("Q", g1[:], RAP(tensor=gamma.tensor, offset=128,
                                        ap=[[1, 128], [1, 1]]), writes=["g1"])
                ctx.dma("Q", b0[:], RAP(tensor=beta.tensor, offset=0,
                                        ap=[[1, 128], [1, 1]]), writes=["b0"])
                ctx.dma("Q", b1[:], RAP(tensor=beta.tensor, offset=128,
                                        ap=[[1, 128], [1, 1]]), writes=["b1"])
                bcast_row_into(bias2_b, stq, "bias2_b",
                               bias2.rearrange("(o d) -> o d", o=1), D, 3)
                cqs = 1.0 / (float(S) * STD)
                ctx.emit("V", lambda v: v.tensor_scalar(
                    out=g0[:], in0=g0[:], scalar1=cqs, scalar2=None, op0=OP.mult),
                    reads=["g0"], writes=["g0"])
                ctx.emit("V", lambda v: v.tensor_scalar(
                    out=b0[:], in0=b0[:], scalar1=cqs, scalar2=None, op0=OP.mult),
                    reads=["b0"], writes=["b0"])
                for it in range(NIT):
                    for dc in range(DC):
                        ctx.emit("T", lambda t: t.matmul(
                            out=P[0][:, 0:512],
                            lhsT=wqkb[:, dc * DQK:(dc + 1) * DQK],
                            rhs=xnT[:, dc * T + it * 512: dc * T + (it + 1) * 512],
                            start=(dc == 0), stop=(dc == DC - 1)),
                            reads=["wqkb", "xnT"], writes=[PKa(0, 0)], accum=(dc > 0))
                    ctx.emit("S", lambda s: s.activation(
                        out=qk_f[:, it * 512:(it + 1) * 512], in_=P[0][:, 0:512],
                        func=AF.Silu, bias=bqk_t[:], scale=1.0),
                        reads=[PKa(0, 0), "bqk_t"], writes=["qk_f"])
                ctx.emit("V", lambda v: v.tensor_scalar(
                    out=qf[:], in0=qk_f[:], scalar1=g0[:], scalar2=b0[:],
                    op0=OP.mult, op1=OP.add),
                    reads=["qk_f", "g0", "b0"], writes=["qf"])
                ctx.emit("V", lambda v: v.tensor_scalar(
                    out=kf[:], in0=qk_f[:], scalar1=g1[:], scalar2=b1[:],
                    op0=OP.mult, op1=OP.add),
                    reads=["qk_f", "g1", "b1"], writes=["kf"])
                ctx.barrier()

            v_tok = SB(stAtt, "v_tok", [128, TC * 1024], BF16)
            with ExitStack() as stv:
                bv_b = bcast_row(stv, "bv_b", bv.rearrange("(o d) -> o d", o=1),
                                 DV, 2)
                wv_h = SB(stv, "wv_h", [128, DC * 512], BF16)
                vtmp = SB(stv, "vtmp", [128, 512], F32)
                for h in range(2):
                    ctx.dma("Q", wv_h[:].rearrange("p (dc f) -> p dc f", f=512),
                            vcache.rearrange("(dc p) f -> p dc f", p=128)
                            [:, :, h * 512:(h + 1) * 512],
                            reads=["vcache"], writes=["wv_h"])
                    for tc in range(TC):
                        for dc in range(DC):
                            ctx.emit("T", lambda t: t.matmul(
                                out=P[h][:, 0:512],
                                lhsT=xnT[:, dc * T + tc * 128: dc * T + (tc + 1) * 128],
                                rhs=wv_h[:, dc * 512:(dc + 1) * 512],
                                start=(dc == 0), stop=(dc == DC - 1)),
                                reads=["xnT", "wv_h"], writes=[PKa(h, 0)], accum=(dc > 0))
                        ctx.emit("V", lambda v: v.scalar_tensor_tensor(
                            out=vtmp[:], in0=P[h][:, 0:512], scalar=1.0,
                            in1=bv_b[:, h * 512:(h + 1) * 512],
                            op0=OP.mult, op1=OP.add),
                            reads=[PKa(h, 0), "bv_b"], writes=["vtmp"])
                        ctx.emit("S", lambda s: s.activation(
                            out=v_tok[:, tc * 1024 + h * 512:
                                      tc * 1024 + (h + 1) * 512],
                            in_=vtmp[:], func=AF.Silu),
                            reads=["vtmp"], writes=["v_tok"])
                ctx.barrier()

            xnT_stk.close()
            wTb = SB(st1, "wTb", [128, DC * 1024], BF16)

            def w8_gen():
                w8f, w8f_k = yq, "yq"
                for oc in range(DC):
                    ctx.dma("G", w8f[:],
                            w_int8.rearrange("(oc p) d -> oc p d", p=128)[oc],
                            writes=[w8f_k])
                    yield
                    for dc in range(DC):
                        bk = dc % 2
                        pcol = bk * 512 + ((dc % 4) // 2) * 128
                        ctx.emit("T", lambda t: t.transpose(
                            out=P[3][:, pcol:pcol + 128],
                            in_=w8f[:, dc * 128:(dc + 1) * 128], identity=c_id[:]),
                            reads=[w8f_k, "c_id"], writes=[PKa(3, bk)])
                        yield
                        ctx.emit("S", lambda s: s.copy(
                            out=wTb[:, dc * 1024 + oc * 128:
                                    dc * 1024 + (oc + 1) * 128],
                            in_=P[3][:, pcol:pcol + 128]),
                            reads=[PKa(3, bk)], writes=["wTb"])
                        yield

            yq = SB(st1, "yq", [128, 1024], F32)
            ti32 = SB(st1, "ti32", [128, 1024], I32)
            xqT = SB(st1, "xqT", [128, DC * 128], BF16)
            yq1 = SB(st1, "yq1", [128, 1024], F32)
            yq_s = [(yq, "yq"), (yq1, "yq1")]
            ti_s = [(ti32, "ti32"), (ti32, "ti32")]
            dif_s = [(xn_f, "xn_f"), (lndump, "lndump")]
            xqT_s = [(xqT, "xqT"), (xqT, "xqT")]
            ctx.barrier()

            def ffn_quant(tc):
                p = tc % 2
                yqb, yq_k = yq_s[p]
                tib, ti_k = ti_s[p]
                difb, dif_k = dif_s[p]
                ctx.emit("S", lambda s: s.activation(
                    out=yqb[:], in_=x[:, tc * 1024:(tc + 1) * 1024],
                    func=AF.Identity,
                    scale=amx_a[:, tc:tc + 1], bias=ct_a[:, tc:tc + 1]),
                    reads=["x%d" % tc, "amx_a", "ct_a"], writes=[yq_k])
                yield
                ctx.emit("S", lambda s: s.copy(out=tib[:], in_=yqb[:]),
                         reads=[yq_k], writes=[ti_k])
                yield
                ctx.emit("V", lambda v: v.scalar_tensor_tensor(
                    out=difb[:], in0=tib[:], scalar=-1.0, in1=yqb[:],
                    op0=OP.mult, op1=OP.add), reads=[ti_k, yq_k], writes=[dif_k])
                yield
                ctx.emit("V", lambda v: v.tensor_scalar(
                    out=yqb[:], in0=difb[:], scalar1=-0.5, scalar2=None,
                    op0=OP.is_le), reads=[dif_k], writes=[yq_k])
                yield
                ctx.emit("V", lambda v: v.tensor_scalar(
                    out=difb[:], in0=difb[:], scalar1=0.5, scalar2=None,
                    op0=OP.is_ge), reads=[dif_k], writes=[dif_k])
                yield
                ctx.emit("V", lambda v: v.scalar_tensor_tensor(
                    out=difb[:], in0=yqb[:], scalar=-1.0, in1=difb[:],
                    op0=OP.mult, op1=OP.add), reads=[yq_k, dif_k], writes=[dif_k])
                yield
                ctx.emit("V", lambda v: v.tensor_tensor(
                    out=yqb[:], in0=tib[:], in1=difb[:], op=OP.add),
                    reads=[ti_k, dif_k], writes=[yq_k])
                yield
                ctx.emit("V", lambda v: v.tensor_scalar(
                    out=yqb[:], in0=yqb[:], scalar1=-128.0, scalar2=127.0,
                    op0=OP.max, op1=OP.min), reads=[yq_k], writes=[yq_k])
                yield
                ctx.emit("V", lambda v: v.tensor_scalar(
                    out=yqb[:], in0=yqb[:], scalar1=cq2_a[:, tc:tc + 1],
                    scalar2=None, op0=OP.mult),
                    reads=[yq_k, "cq2_a"], writes=[yq_k])
                yield

            def ffn_mm(tc):
                p = tc % 2
                yqb, yq_k = yq_s[p]
                difb, dif_k = dif_s[p]
                xqTb, xqT_k = xqT_s[p]
                g2t = difb[:, 0:512]
                g2t_k = dif_k
                for dc in range(DC):
                    bk = dc % 2
                    pcol = bk * 512 + ((dc % 4) // 2) * 128
                    ctx.emit("T", lambda t: t.transpose(
                        out=P[3][:, pcol:pcol + 128],
                        in_=yqb[:, dc * 128:(dc + 1) * 128], identity=c_id[:]),
                        reads=[yq_k, "c_id"], writes=[PKa(3, bk)])
                    yield
                    ctx.emit("S", lambda s: s.copy(
                        out=xqTb[:, dc * 128:(dc + 1) * 128],
                        in_=P[3][:, pcol:pcol + 128]),
                        reads=[PKa(3, bk)], writes=[xqT_k])
                    yield
                for h in range(2):
                    for dc in range(DC):
                        ctx.emit("T", lambda t: t.matmul(
                            out=P[3][:, h * 512:(h + 1) * 512],
                            lhsT=xqTb[:, dc * 128:(dc + 1) * 128],
                            rhs=wTb[:, dc * 1024 + h * 512:
                                    dc * 1024 + h * 512 + 512],
                            start=(dc == 0), stop=(dc == DC - 1)),
                            reads=[xqT_k, "wTb"],
                            writes=[PKa(3, h)], accum=(dc > 0))
                        yield
                    ctx.emit("V", lambda v: v.scalar_tensor_tensor(
                        out=g2t, in0=P[3][:, h * 512:(h + 1) * 512], scalar=1.0,
                        in1=bias2_b[:, h * 512:(h + 1) * 512],
                        op0=OP.mult, op1=OP.add),
                        reads=[PKa(3, h), "bias2_b"], writes=[g2t_k])
                    yield
                    ctx.emit("S", lambda s: s.activation(
                        out=g2t, in_=g2t, func=AF.Silu),
                        reads=[g2t_k], writes=[g2t_k])
                    yield
                    ctx.emit("V", lambda v: v.tensor_tensor(
                        out=g2t, in0=g2t,
                        in1=s_tok[:, tc * 1024 + h * 512:
                                  tc * 1024 + (h + 1) * 512], op=OP.mult),
                        reads=[g2t_k, "s%d" % tc], writes=[g2t_k])
                    yield
                    ctx.emit("V", lambda v: v.tensor_tensor(
                        out=x[:, tc * 1024 + h * 512: tc * 1024 + (h + 1) * 512],
                        in0=x[:, tc * 1024 + h * 512: tc * 1024 + (h + 1) * 512],
                        in1=g2t, op=OP.add),
                        reads=[g2t_k, "x%d" % tc], writes=["x%d" % tc])
                    yield

            def interleave(*gs):
                gs = [g for g in gs if g is not None]
                while gs:
                    alive = []
                    for g in gs:
                        try:
                            next(g)
                            alive.append(g)
                        except StopIteration:
                            pass
                    gs = alive
                    yield

            with ExitStack() as sts:
                c_j = SB(sts, "c_j", [128, 128], F32)
                ctx.dma("Q", c_j[:], jrev, writes=["c_j"])
                c_mk = SB(sts, "c_mk", [128, 4 * 512], BF16)
                ctx.dma("Q", c_mk[:], cmasks, writes=["c_mk"])
                strip = SB(sts, "strip", [128, TC * 512], BF16)
                t1b = SB(sts, "t1b", [128, 512], F32)
                biasT = SB(sts, "biasT", [128, 512], F32)
                zt = SB(sts, "zt", [128, 512], F32)
                glt, glt_k = t1b, "t1b"
                dent, dent_k = biasT, "biasT"
                att2, att2_k = zt, "zt"
                def simav_gen(it):
                    i0 = it * 512
                    jcmax = min(TC - 1, it * 4 + 3)
                    for jc in range(jcmax + 1):
                        base = (i0 - jc * 128) + 513
                        ctx.dma("S", t1b[:],
                                RAP(tensor=bias_ext.tensor, offset=base,
                                    ap=[[1, 128], [1, 512]]),
                                reads=["bias_ext"], writes=["t1b"])
                        jp = (jc % 2) * 512
                        ctx.emit("T", lambda t: t.matmul(
                            out=P[1][:, jp:jp + 512], lhsT=c_j[:], rhs=t1b[:],
                            start=True, stop=True),
                            reads=["c_j", "t1b"], writes=[PKa(1, jc % 2)])
                        ctx.emit("S", lambda s: s.copy(
                            out=biasT[:], in_=P[1][:, jp:jp + 512]),
                            reads=[PKa(1, jc % 2)], writes=["biasT"])
                        ctx.emit("T", lambda t: t.matmul(
                            out=P[0][:, jp:jp + 512],
                            lhsT=kf[:, jc * 128:(jc + 1) * 128],
                            rhs=qf[:, it * 512:(it + 1) * 512],
                            start=True, stop=True),
                            reads=["kf", "qf"], writes=[PKa(0, jc % 2)])
                        ctx.emit("V", lambda v: v.scalar_tensor_tensor(
                            out=zt[:], in0=P[0][:, jp:jp + 512], scalar=1.0,
                            in1=biasT[:], op0=OP.mult, op1=OP.add),
                            reads=[PKa(0, jc % 2), "biasT"], writes=["zt"])
                        ctx.emit("S", lambda s: s.activation(
                            out=glt[:], in_=zt[:], func=AF.Gelu),
                            reads=["zt"], writes=[glt_k])
                        ctx.emit("V", lambda v: v.scalar_tensor_tensor(
                            out=dent[:], in0=zt[:], scalar=0.0, in1=zt[:],
                            op0=OP.is_equal, op1=OP.add),
                            reads=["zt"], writes=[dent_k])
                        ctx.emit("V", lambda v: v.reciprocal(
                            out=dent[:], in_=dent[:]), reads=[dent_k], writes=[dent_k])
                        mq = it * 4 - jc
                        if mq >= 1:
                            ctx.emit("V", lambda v: v.tensor_tensor(
                                out=strip[:, jc * 512:(jc + 1) * 512], in0=glt[:],
                                in1=dent[:], op=OP.mult),
                                reads=[glt_k, dent_k], writes=["strip"])
                        else:
                            mi = -mq
                            ctx.emit("V", lambda v: v.tensor_tensor(
                                out=att2[:], in0=glt[:], in1=dent[:], op=OP.mult),
                                reads=[glt_k, dent_k], writes=[att2_k])
                            ctx.emit("V", lambda v: v.tensor_tensor(
                                out=strip[:, jc * 512:(jc + 1) * 512], in0=att2[:],
                                in1=c_mk[:, mi * 512:(mi + 1) * 512], op=OP.mult),
                                reads=[att2_k, "c_mk"], writes=["strip"])
                    for lc in range(4):
                        tci = it * 4 + lc
                        for jc in range(tci + 1):
                            for h in range(2):
                                ctx.emit("T", lambda t: t.matmul(
                                    out=P[2][:, h * 512:(h + 1) * 512],
                                    lhsT=strip[:, jc * 512 + lc * 128:
                                               jc * 512 + (lc + 1) * 128],
                                    rhs=v_tok[:, jc * 1024 + h * 512:
                                              jc * 1024 + h * 512 + 512],
                                    start=(jc == 0), stop=(jc == tci)),
                                    reads=["strip", "v_tok"], writes=PKf(2),
                                    accum=(jc > 0))
                                yield
                        ctx.emit("S", lambda s: s.copy(
                            out=s_tok[:, tci * 1024:(tci + 1) * 1024], in_=P[2][:]),
                            reads=PKf(2), writes=["s%d" % tci])
                        yield

                def ffn_chain(it):
                    for lc in range(4):
                        tc = it * 4 + lc
                        if tc == 0:
                            for _ in ffn_quant(0):
                                yield
                        nq = ffn_quant(tc + 1) if tc + 1 < TC else None
                        for _ in interleave(ffn_mm(tc), nq):
                            yield

                pend = []
                for it in range(NIT):
                    gens = [simav_gen(it)] + pend
                    if it == 0:
                        gens.append(w8_gen())
                    weave(gens)
                    pend = [ffn_chain(it)]
                weave(pend)
        for tc in range(TC):
            ctx.dma("Q", x_out.rearrange("(tc p) d -> p tc d", p=128)[:, tc],
                    x[:, tc * 1024:(tc + 1) * 1024],
                    reads=["x%d" % tc], writes=[], slot="xout")
        ctx.eng["Q"].wait_ge(ctx.sems["D:xout"], ctx.dmaval["D:xout"])


def _host_consts(S):
    consts = {}
    consts["c_ident"] = np.eye(128, dtype=np.float32)
    consts["c_jrev"] = np.eye(128, dtype=np.float32)[::-1].copy()
    consts["c_ones1"] = np.ones((1, 128), dtype=np.float32)
    consts["c_onesk"] = np.ones((128, 1), dtype=ml_dtypes.bfloat16)
    consts["c_iota8"] = np.broadcast_to(
        np.arange(E, dtype=np.float32), (128, E)).copy()
    consts["c_iotaB"] = consts["c_iota8"] + np.float32(BIG)
    sel2 = np.zeros((2, 128), dtype=np.float32)
    sel2[0, 0:64] = 1.0
    sel2[1, 64:128] = 1.0
    consts["c_sel2"] = sel2
    s16 = np.zeros((16, 8 * 128), dtype=np.float32)
    for c in range(8):
        for p in range(128):
            s16[2 * c + p // 64, c * 128 + p] = 1.0
    consts["c_sel16"] = s16.astype(ml_dtypes.bfloat16)
    masks = np.zeros((128, 4 * 512), dtype=np.float32)
    jj = np.arange(128)[:, None]
    ii = np.arange(512)[None, :]
    for mi in range(4):
        masks[:, mi * 512:(mi + 1) * 512] = (jj <= ii - mi * 128).astype(np.float32)
    consts["c_masks"] = masks.astype(ml_dtypes.bfloat16)
    n_ext = S + 768
    oh = np.zeros((NB, n_ext), dtype=np.float32)
    max_exact = NB // 2
    for dp in range(n_ext):
        delta = dp - 640
        if delta < 0 or delta >= S:
            continue
        if delta < max_exact:
            b = delta
        else:
            n_safe = np.float32(max(delta, 1))
            val = max_exact + int(
                np.float32(np.log(n_safe / np.float32(max_exact))
                           / np.float32(math.log(MAXD / max_exact))
                           * np.float32(NB - max_exact)))
            b = min(val, NB - 1)
        oh[b, dp] = 1.0
    consts["c_oh"] = oh
    return consts


def kernel(L0_attn_x, L0_attn_Wqkv, L0_attn_Wout, L0_attn_bout,
           L0_ffn_router_w, L0_ffn_gate_proj, L0_ffn_up_proj, L0_ffn_down_proj,
           L1_attn_wqk, L1_attn_bqk, L1_attn_gamma, L1_attn_beta,
           L1_attn_wv, L1_attn_bv, L1_attn_rel_bias,
           L1_ffn_w_int8, L1_ffn_weight_scale, L1_ffn_bias):
    x = np.ascontiguousarray(np.asarray(L0_attn_x, dtype=np.float32))
    B, T, D_ = x.shape
    assert B == NCORES and D_ == D
    nc = bass.Bass("TRN2", target_bir_lowering=False)
    build_program(nc, T)
    consts = _host_consts(T)
    common = dict(
        L0_Wqkv=np.ascontiguousarray(np.asarray(L0_attn_Wqkv, np.float32)),
        L0_Wout=np.ascontiguousarray(np.asarray(L0_attn_Wout, np.float32)),
        L0_bout=np.ascontiguousarray(np.asarray(L0_attn_bout, np.float32)),
        L0_router=np.ascontiguousarray(np.asarray(L0_ffn_router_w, np.float32)),
        L0_gate=np.ascontiguousarray(np.asarray(L0_ffn_gate_proj, np.float32)),
        L0_up=np.ascontiguousarray(np.asarray(L0_ffn_up_proj, np.float32)),
        L0_down=np.ascontiguousarray(np.asarray(L0_ffn_down_proj, np.float32)),
        L1_wqk=np.ascontiguousarray(np.asarray(L1_attn_wqk, np.float32)),
        L1_bqk=np.ascontiguousarray(np.asarray(L1_attn_bqk, np.float32)),
        L1_gamma=np.ascontiguousarray(np.asarray(L1_attn_gamma, np.float32)),
        L1_beta=np.ascontiguousarray(np.asarray(L1_attn_beta, np.float32)),
        L1_wv=np.ascontiguousarray(np.asarray(L1_attn_wv, np.float32)),
        L1_bv=np.ascontiguousarray(np.asarray(L1_attn_bv, np.float32)),
        L1_rel=np.ascontiguousarray(np.asarray(L1_attn_rel_bias, np.float32)),
        L1_w8=np.ascontiguousarray(np.asarray(L1_ffn_w_int8, np.int32)),
        L1_ws=np.asarray(L1_ffn_weight_scale, np.float32).reshape(1),
        L1_bias=np.ascontiguousarray(np.asarray(L1_ffn_bias, np.float32)),
        **consts,
    )
    in_maps = [dict(common, x_in=np.ascontiguousarray(x[c])) for c in range(NCORES)]
    res = run_bass_kernel_spmd(nc, in_maps, core_ids=list(range(NCORES)))
    out = np.stack([np.asarray(res.results[c]["x_out"]) for c in range(NCORES)],
                   axis=0)
    return out.astype(np.float32)


# version 19
# speedup vs baseline: 1.0304x; 1.0304x over previous
import math
import numpy as np
import ml_dtypes
import concourse.bass as bass
import concourse.mybir as mybir
from concourse.bass_utils import run_bass_kernel_spmd
from concourse.alu_op_type import AluOpType as OP
from concourse import dve_ops
from bass_rust import AP as RAP
from contextlib import ExitStack

F32 = mybir.dt.float32
BF16 = mybir.dt.bfloat16
I32 = mybir.dt.int32
I16 = mybir.dt.int16
AF = mybir.ActivationFunctionType
AX = mybir.AxisListType

NCORES = 8
D = 1024
DC = D // 128
E, FF = 8, 2816
FC = FF // 128
DQK, DV = 128, 1024
NB, MAXD = 32, 128
LN_EPS = 1e-5
ATT_EPS = 1e-6
RMS_EPS = 1e-6
ACT_EPS = 1e-5
MU = math.sqrt(0.5)
STD = math.sqrt(0.25 * math.pi)
FGROUPS = [2] * 11
BIG = 65536.0


class Ctx:

    def __init__(self, nc, sems, mksem):
        self.nc = nc
        self.eng = {"T": nc.tensor, "V": nc.vector, "S": nc.scalar,
                    "G": nc.gpsimd, "Q": nc.sync}
        self.sems = sems
        self.mksem = mksem
        self.count = {k: 0 for k in "TVSGQ"}
        self.ilog = {k: [] for k in "TVSGQ"}
        self.incs = {k: [] for k in "TVSGQ"}
        self.semval = {k: 0 for k in "TVSGQ"}
        self.dmaval = {}
        self.wm = {k: {} for k in "TVSGQ"}
        self.lastc = {k: None for k in "TVSGQ"}
        self.bufs = {}
        self.noburst = set()
        self.sticky = {}

    def _prod_sem(self, prod):
        kind, idx = prod
        if kind.startswith("D:"):
            return (kind, idx)
        for (i2, v2) in self.incs[kind]:
            if i2 >= idx:
                return (kind, v2)
        v = self.semval[kind] + 1
        self.semval[kind] = v
        self.ilog[kind][idx].then_inc(self.sems[kind], 1)
        self.incs[kind].append((idx, v))
        return (kind, v)

    def _wait(self, ename, semkey, val):
        if self.wm[ename].get(semkey, 0) >= val:
            return
        self.eng[ename].wait_ge(self.sems[semkey], val)
        self.wm[ename][semkey] = val

    def _collect(self, ename, reads, writes, needed):
        for b in reads:
            s = self.bufs.get(b)
            if s and s["w"] is not None:
                p = s["w"]
                if ename == "T" and p[0] == "T":
                    continue
                k, v = self._prod_sem(p)
                needed[k] = max(needed.get(k, 0), v)
        for b in writes:
            s = self.bufs.get(b)
            if s is None:
                continue
            deps = list(s["r"])
            if s["w"] is not None:
                deps.append(s["w"])
            for p in deps:
                if ename == "T" and p[0] == "T":
                    continue
                k, v = self._prod_sem(p)
                needed[k] = max(needed.get(k, 0), v)

    def emit(self, ename, fn, reads=(), writes=(), accum=False):
        needed = {}
        self._collect(ename, reads, () if accum else writes, needed)
        for k in sorted(needed):
            self._wait(ename, k, needed[k])
        instr = fn(self.eng[ename])
        idx = self.count[ename]
        self.count[ename] += 1
        self.ilog[ename].append(instr)
        self.lastc[ename] = idx
        prod = (ename, idx)
        for b in writes:
            if accum and b in self.bufs:
                self.bufs[b]["w"] = prod
            else:
                self.bufs[b] = {"w": prod, "r": []}
        for b in reads:
            self.bufs.setdefault(b, {"w": None, "r": []})["r"].append(prod)
        return instr

    def dma(self, qname, out, in_, reads=(), writes=(), slot=None):
        if slot is None:
            slot = writes[0] if writes else "store"
        semkey = "D:" + slot
        if semkey not in self.sems:
            self.sems[semkey] = self.mksem(slot)
            self.dmaval[semkey] = 0
        needed = {}
        self._collect(qname, reads, writes, needed)
        for k in sorted(needed):
            self._wait(qname, k, needed[k])
        instr = self.eng[qname].dma_start(out=out, in_=in_)
        idx = self.count[qname]
        self.count[qname] += 1
        self.ilog[qname].append(instr)
        self.dmaval[semkey] += 16
        v = self.dmaval[semkey]
        instr.then_inc(self.sems[semkey], 16)
        prod = (semkey, v)
        for b in writes:
            self.bufs[b] = {"w": prod, "r": []}
        for b in reads:
            self.bufs.setdefault(b, {"w": None, "r": []})["r"].append(prod)
        return instr

    def barrier(self):
        for k in "TVSGQ":
            if self.lastc[k] is not None:
                self._prod_sem((k, self.lastc[k]))
        targets = [(k, self.semval[k]) for k in "TVSGQ" if self.semval[k] > 0]
        targets += [(k, v) for k, v in self.dmaval.items()
                    if v > 0 and k not in self.noburst]
        for ename in "TVSGQ":
            for k, v in targets:
                self._wait(ename, k, v)
        self.bufs = {}
        for k, p in self.sticky.items():
            self.bufs[k] = {"w": p, "r": []}


def weave(gens):
    gens = [g for g in gens if g is not None]
    while gens:
        alive = []
        for g in gens:
            try:
                next(g)
                alive.append(g)
            except StopIteration:
                continue
        gens = alive


def c_hi_lo(x):
    hi = np.float32(x)
    lo = np.float32(x - float(hi))
    return float(hi), float(lo)


def build_program(nc, T):
    TC = T // 128
    NIT = T // 512
    NTB = T // 512
    S = T
    NBE = (S + 768) // 128

    dt = lambda name, shape, dty, kind: nc.dram_tensor(name, shape, dty, kind=kind).ap()
    x_in = dt("x_in", [T, D], F32, "ExternalInput")
    Wqkv = dt("L0_Wqkv", [D, 3 * D], F32, "ExternalInput")
    Wout = dt("L0_Wout", [D, D], F32, "ExternalInput")
    bout = dt("L0_bout", [D], F32, "ExternalInput")
    router_w = dt("L0_router", [D, E], F32, "ExternalInput")
    gate_p = dt("L0_gate", [E, D, FF], F32, "ExternalInput")
    up_p = dt("L0_up", [E, D, FF], F32, "ExternalInput")
    down_p = dt("L0_down", [E, FF, D], F32, "ExternalInput")
    wqk = dt("L1_wqk", [D, DQK], F32, "ExternalInput")
    bqk = dt("L1_bqk", [DQK], F32, "ExternalInput")
    gamma = dt("L1_gamma", [2, DQK], F32, "ExternalInput")
    beta = dt("L1_beta", [2, DQK], F32, "ExternalInput")
    wv = dt("L1_wv", [D, DV], F32, "ExternalInput")
    bv = dt("L1_bv", [DV], F32, "ExternalInput")
    rel_bias = dt("L1_rel", [NB], F32, "ExternalInput")
    w_int8 = dt("L1_w8", [D, D], I32, "ExternalInput")
    wscale = dt("L1_ws", [1], F32, "ExternalInput")
    bias2 = dt("L1_bias", [D], F32, "ExternalInput")
    ident = dt("c_ident", [128, 128], F32, "ExternalInput")
    jrev = dt("c_jrev", [128, 128], F32, "ExternalInput")
    ones1 = dt("c_ones1", [1, 128], F32, "ExternalInput")
    onesk = dt("c_onesk", [128, 1], BF16, "ExternalInput")
    iota8 = dt("c_iota8", [128, E], F32, "ExternalInput")
    iotaB = dt("c_iotaB", [128, E], F32, "ExternalInput")
    sel2 = dt("c_sel2", [2, 128], F32, "ExternalInput")
    sel16 = dt("c_sel16", [16, 8 * 128], BF16, "ExternalInput")
    cmasks = dt("c_masks", [128, 4 * 512], BF16, "ExternalInput")
    ohmat = dt("c_oh", [NB, S + 768], F32, "ExternalInput")
    x_out = dt("x_out", [T, D], F32, "ExternalOutput")
    bias_ext = dt("s_bias", [S + 768], F32, "Internal")
    gcache = dt("s_gcache", [E, D, FF], BF16, "Internal")
    ucache = dt("s_ucache", [E, D, FF], BF16, "Internal")
    dcache = dt("s_dcache", [E, FF, D], BF16, "Internal")
    qkvcache = dt("s_qkvc", [D, 3 * D], BF16, "Internal")
    ocache = dt("s_oc", [D, D], BF16, "Internal")
    vcache = dt("s_vc", [D, DV], BF16, "Internal")
    qkcache = dt("s_qkc", [D, DQK], BF16, "Internal")
    rcache = dt("s_rc", [D, E], BF16, "Internal")

    with ExitStack() as stk:
        sems = {}
        for k in ["T", "V", "S", "G"]:
            sems[k] = stk.enter_context(nc.semaphore("sem_" + k))
        mksem = lambda slot: stk.enter_context(nc.semaphore("dsem_" + slot.replace(":", "_")))
        ctx = Ctx(nc, sems, mksem)

        def SB(st, name, shape, dty):
            return st.enter_context(nc.sbuf_tensor(name, shape, dty))

        x = SB(stk, "x", [128, TC * 1024], F32)
        xnT_stk = stk.enter_context(ExitStack())
        xnT = xnT_stk.enter_context(nc.sbuf_tensor("xnT", [128, DC * T], BF16,
                                                   side="right"))
        s_tok = SB(stk, "s_tok", [128, TC * 1024], BF16)
        m_st = SB(stk, "m_st", [128, TC], F32)
        r_st = SB(stk, "r_st", [128, TC], F32)
        ew = SB(stk, "ew", [128, TC * E], F32)
        c_id = SB(stk, "c_id", [128, 128], F32)
        c_o1 = SB(stk, "c_o1", [1, 128], F32)
        c_ok = SB(stk, "c_ok", [128, 1], BF16)
        xn_f = SB(stk, "xn_f", [128, 1024], F32)
        lndump = SB(stk, "lndump", [128, 1024], F32)
        ssum = SB(stk, "ssum", [128, 1], F32)
        ssq = SB(stk, "ssq", [128, 1], F32)
        varv = SB(stk, "varv", [128, 1], F32)
        P = [stk.enter_context(nc.psum_tensor("P%d" % i, [128, 1024], F32))
             for i in range(4)]
        PKa = lambda i, h: "P%d:%d" % (i, h)
        PKf = lambda i: ["P%d:0" % i, "P%d:1" % i]
        PSL = lambda i, s: "P%d:0s%d" % (i, s)
        PSLall = lambda i: [PSL(i, s) for s in range(4)]

        blk = stk.enter_context(nc.Block())

        ctx.dma("Q", c_id[:], ident, writes=["c_id"])
        ctx.dma("Q", c_o1[:], ones1, writes=["c_o1"])
        ctx.dma("G", c_ok[:], onesk, writes=["c_ok"])
        ctx.dma("Q", x[:].rearrange("p (tc d) -> p tc d", d=1024),
                x_in.rearrange("(tc p) d -> p tc d", p=128),
                writes=["x%d" % t for t in range(TC)], slot="xld")
        ctx._wait("G", "D:xld", 16)
        ctx.dma("G", rcache, router_w, writes=["rcache"], slot="rcw")
        ctx.dma("G", qkvcache, Wqkv, writes=["qkvcache"], slot="qkvcw")
        ctx.dma("G", ocache, Wout, writes=["ocache"], slot="ocw")
        ctx.dma("G", qkcache, wqk, writes=["qkcache"], slot="qkcw")
        ctx.dma("G", vcache, wv, writes=["vcache"], slot="vcw")
        ctx.sems["MP"] = ctx.mksem("MP")

        def bcast_row(st, name, row_ap, n, pi):
            t = SB(st, name, [128, n], F32)
            bcast_row_into(t, st, name, row_ap, n, pi)
            return t

        def bcast_row_into(t, st, name, row_ap, n, pi):
            rrow = SB(st, name + "_r", [1, n], F32)
            ctx.dma("Q", rrow[:], row_ap, writes=[name + "_r"])
            for h in range(0, n, 512):
                w = min(512, n - h)
                pb = PKa(pi, (h % 1024) // 512)
                ctx.emit("T", lambda t_: t_.matmul(
                    out=P[pi][0:128, h % 1024:h % 1024 + w], lhsT=c_o1[:],
                    rhs=rrow[0:1, h:h + w], start=True, stop=True),
                    reads=["c_o1", name + "_r"], writes=[pb])
                ctx.emit("V", lambda v: v.tensor_copy(
                    out=t[:, h:h + w], in_=P[pi][0:128, h % 1024:h % 1024 + w]),
                    reads=[pb], writes=[name])
            return t

        var16 = SB(stk, "var16", [128, TC], F32)
        stot_a = SB(stk, "stot_a", [128, TC], F32)
        cq2_a = SB(stk, "cq2_a", [128, TC], F32)
        amx_a = SB(stk, "amx_a", [128, TC], F32)
        ct_a = SB(stk, "ct_a", [128, TC], F32)
        wsb2 = SB(stk, "wsb2", [128, 1], F32)

        def ln_stats():
            ctx.emit("V", lambda v: v.tensor_reduce(
                out=m_st[:], in_=x[:].rearrange("p (tc d) -> p tc d", d=1024),
                axis=AX.X, op=OP.add),
                reads=["x%d" % t for t in range(TC)], writes=["m_st"])
            for tc in range(TC):
                ctx.emit("S", lambda s: s.activation(
                    out=lndump[:], in_=x[:, tc * 1024:(tc + 1) * 1024],
                    func=AF.Square, accum_out=r_st[:, tc:tc + 1]),
                    reads=["x%d" % tc], writes=["lndump", "r_st"])
            ctx.emit("V", lambda v: v.tensor_scalar(
                out=m_st[:], in0=m_st[:], scalar1=1.0 / 1024.0, scalar2=None,
                op0=OP.mult), reads=["m_st"], writes=["m_st"])
            ctx.emit("V", lambda v: v.tensor_scalar(
                out=r_st[:], in0=r_st[:], scalar1=1.0 / 1024.0, scalar2=None,
                op0=OP.mult), reads=["r_st"], writes=["r_st"])
            ctx.emit("V", lambda v: v.tensor_tensor(
                out=var16[:], in0=m_st[:], in1=m_st[:], op=OP.mult),
                reads=["m_st"], writes=["var16"])
            ctx.emit("V", lambda v: v.tensor_tensor(
                out=var16[:], in0=r_st[:], in1=var16[:], op=OP.subtract),
                reads=["r_st", "var16"], writes=["var16"])
            ctx.emit("V", lambda v: v.tensor_scalar(
                out=var16[:], in0=var16[:], scalar1=1.0, scalar2=LN_EPS,
                op0=OP.mult, op1=OP.add), reads=["var16"], writes=["var16"])
            ctx.emit("V", lambda v: v.reciprocal(out=var16[:], in_=var16[:]),
                     reads=["var16"], writes=["var16"])
            ctx.emit("S", lambda s: s.activation(
                out=r_st[:], in_=var16[:], func=AF.Sqrt),
                reads=["var16"], writes=["r_st"])

        def ln_xn_gen(tc, rb=None, sc_all=None, sc_key=None, pingpong=False):
            xs = x[:, tc * 1024:(tc + 1) * 1024]
            ctx.emit("V", lambda v: v.tensor_scalar(
                out=xn_f[:], in0=xs, scalar1=m_st[:, tc:tc + 1],
                scalar2=r_st[:, tc:tc + 1], op0=OP.subtract, op1=OP.mult),
                reads=["x%d" % tc, "m_st", "r_st"], writes=["xn_f"])
            yield
            for dc in range(DC):
                if pingpong:
                    bk = dc % 2
                    pcol = bk * 512 + (dc % 4) // 2 * 128
                else:
                    bk = 0
                    pcol = (dc % 4) * 128
                ctx.emit("T", lambda t: t.transpose(
                    out=P[2][:, pcol:pcol + 128],
                    in_=xn_f[:, dc * 128:(dc + 1) * 128], identity=c_id[:]),
                    reads=["xn_f", "c_id"], writes=[PKa(2, bk)])
                yield
                ctx.emit("S", lambda s: s.copy(
                    out=xnT[:, dc * T + tc * 128: dc * T + (tc + 1) * 128],
                    in_=P[2][:, pcol:pcol + 128]),
                    reads=[PKa(2, bk)], writes=["xnT"])
                yield
                if rb is not None:
                    ctx.emit("T", lambda t: t.matmul(
                        out=P[2][0:128, 512:512 + E],
                        lhsT=xnT[:, dc * T + tc * 128: dc * T + (tc + 1) * 128],
                        rhs=rb[:, dc * E:(dc + 1) * E],
                        start=(dc == 0), stop=(dc == DC - 1)),
                        reads=["xnT", "rb"], writes=[PKa(2, 1)], accum=(dc > 0))
                    yield
            if rb is not None:
                ctx.emit("V", lambda v: v.tensor_copy(
                    out=sc_all[:, tc * E:(tc + 1) * E],
                    in_=P[2][0:128, 512:512 + E]),
                    reads=[PKa(2, 1)], writes=[sc_key])
                yield

        def ln_xn(tc, rb=None, sc_all=None, sc_key=None):
            for _ in ln_xn_gen(tc, rb, sc_all, sc_key):
                pass

        if True:
            st0 = stk
            rb = SB(st0, "rb", [128, DC * E], BF16)
            sc_all = SB(st0, "sc_all", [128, TC * E], F32)
            ctx.dma("Q", rb[:].rearrange("p (dc e) -> p dc e", e=E),
                    rcache.rearrange("(dc p) e -> p dc e", p=128),
                    reads=["rcache"], writes=["rb"])
            m1n = SB(st0, "m1n", [128, 1], F32)
            m2n = SB(st0, "m2n", [128, 1], F32)
            eqt = SB(st0, "eqt", [128, E], F32)
            at = SB(st0, "at", [128, E], F32)
            i1t = SB(st0, "i1t", [128, 1], F32)
            mk1 = SB(st0, "mk1", [128, E], F32)
            mk2 = SB(st0, "mk2", [128, E], F32)
            smk = SB(st0, "smk", [128, E], F32)
            e2t = SB(st0, "e2t", [128, 1], F32)
            w1t = SB(st0, "w1t", [128, 1], F32)
            w2t = SB(st0, "w2t", [128, 1], F32)
            ewt = SB(st0, "ewt", [128, E], F32)
            c_i8 = SB(st0, "c_i8", [128, E], F32)
            c_iB = SB(st0, "c_iB", [128, E], F32)
            ctx.dma("Q", c_i8[:], iota8, writes=["c_i8"])
            ctx.dma("Q", c_iB[:], iotaB, writes=["c_iB"])

            def topk_tc(tc):
                scs = sc_all[:, tc * E:(tc + 1) * E]
                ctx.emit("V", lambda v: v.tensor_reduce(
                    out=m1n[:], in_=scs, axis=AX.X, op=OP.max, negate=True),
                    reads=["sc_all"], writes=["m1n"])
                ctx.emit("V", lambda v: v.tensor_scalar(
                    out=eqt[:], in0=scs, scalar1=m1n[:], scalar2=0.0,
                    op0=OP.add, op1=OP.is_equal),
                    reads=["sc_all", "m1n"], writes=["eqt"])
                ctx.emit("V", lambda v: v.scalar_tensor_tensor(
                    out=at[:], in0=eqt[:], scalar=-BIG, in1=c_iB[:],
                    op0=OP.mult, op1=OP.add), reads=["eqt", "c_iB"], writes=["at"])
                ctx.emit("V", lambda v: v.tensor_reduce(
                    out=i1t[:], in_=at[:], axis=AX.X, op=OP.min),
                    reads=["at"], writes=["i1t"])
                ctx.emit("V", lambda v: v.tensor_scalar(
                    out=mk1[:], in0=c_i8[:], scalar1=i1t[:], scalar2=0.0,
                    op0=OP.subtract, op1=OP.is_equal),
                    reads=["c_i8", "i1t"], writes=["mk1"])
                ctx.emit("V", lambda v: v.scalar_tensor_tensor(
                    out=smk[:], in0=mk1[:], scalar=-BIG, in1=scs,
                    op0=OP.mult, op1=OP.add), reads=["mk1", "sc_all"], writes=["smk"])
                ctx.emit("V", lambda v: v.tensor_reduce(
                    out=m2n[:], in_=smk[:], axis=AX.X, op=OP.max, negate=True),
                    reads=["smk"], writes=["m2n"])
                ctx.emit("V", lambda v: v.tensor_scalar(
                    out=eqt[:], in0=smk[:], scalar1=m2n[:], scalar2=0.0,
                    op0=OP.add, op1=OP.is_equal), reads=["smk", "m2n"], writes=["eqt"])
                ctx.emit("V", lambda v: v.scalar_tensor_tensor(
                    out=at[:], in0=eqt[:], scalar=-BIG, in1=c_iB[:],
                    op0=OP.mult, op1=OP.add), reads=["eqt", "c_iB"], writes=["at"])
                ctx.emit("V", lambda v: v.tensor_reduce(
                    out=i1t[:], in_=at[:], axis=AX.X, op=OP.min),
                    reads=["at"], writes=["i1t"])
                ctx.emit("V", lambda v: v.tensor_scalar(
                    out=mk2[:], in0=c_i8[:], scalar1=i1t[:], scalar2=0.0,
                    op0=OP.subtract, op1=OP.is_equal),
                    reads=["c_i8", "i1t"], writes=["mk2"])
                ctx.emit("V", lambda v: v.tensor_scalar(
                    out=e2t[:], in0=m1n[:], scalar1=m2n[:], scalar2=None,
                    op0=OP.subtract), reads=["m1n", "m2n"], writes=["e2t"])
                ctx.emit("S", lambda s: s.activation(
                    out=e2t[:], in_=e2t[:], func=AF.Exp),
                    reads=["e2t"], writes=["e2t"])
                ctx.emit("V", lambda v: v.tensor_scalar(
                    out=w1t[:], in0=e2t[:], scalar1=1.0, scalar2=None, op0=OP.add),
                    reads=["e2t"], writes=["w1t"])
                ctx.emit("V", lambda v: v.reciprocal(out=w1t[:], in_=w1t[:]),
                         reads=["w1t"], writes=["w1t"])
                ctx.emit("V", lambda v: v.tensor_scalar(
                    out=w2t[:], in0=e2t[:], scalar1=w1t[:], scalar2=None,
                    op0=OP.mult), reads=["e2t", "w1t"], writes=["w2t"])
                ctx.emit("V", lambda v: v.tensor_scalar(
                    out=ewt[:], in0=mk2[:], scalar1=w2t[:], scalar2=None,
                    op0=OP.mult), reads=["mk2", "w2t"], writes=["ewt"])
                ctx.emit("V", lambda v: v.scalar_tensor_tensor(
                    out=ew[:, tc * E:(tc + 1) * E], in0=mk1[:], scalar=w1t[:],
                    in1=ewt[:], op0=OP.mult, op1=OP.add),
                    reads=["mk1", "w1t", "ewt"], writes=["ew"])


        stAB = stk.enter_context(ExitStack())
        kvb = SB(stAB, "kvb", [128, 8 * 128], BF16)
        scb = SB(stAB, "scb", [128, 16], BF16)
        with ExitStack() as stA:
            wkv = SB(stA, "wkv", [128, DC * 2048], BF16)
            ctx.dma("Q", wkv[:].rearrange("p (dc f) -> p dc f", f=2048),
                    qkvcache.rearrange("(dc p) f -> p dc f", p=128)[:, :, 1024:3072],
                    reads=["qkvcache"], writes=["wkv"])
            k_bfs = [SB(stA, "k_bf%d" % i, [128, 1024], BF16) for i in range(2)]
            v_bfs = [SB(stA, "v_bf%d" % i, [128, 1024], BF16) for i in range(2)]
            eltmps = [SB(stA, "eltmp%d" % i, [128, 512], F32) for i in range(2)]
            eltmp2s = [SB(stA, "eltmp2%d" % i, [128, 512], F32) for i in range(2)]
            ksum = SB(stA, "ksum", [1, 1024], F32)
            ctx.emit("V", lambda v: v.memset(ksum[:], 0.0), writes=["ksum"])

            def projkv_gen(tc):
                k_bf, k_bfk = k_bfs[tc % 2], "k_bf%d" % (tc % 2)
                v_bf, v_bfk = v_bfs[tc % 2], "v_bf%d" % (tc % 2)
                for part, dst, dstk in ((0, k_bf, k_bfk), (1, v_bf, v_bfk)):
                    for h in range(2):
                        pt, ptk = P[part], PKa(part, h)
                        pcol = h * 512
                        eltmp, eltmp_k = eltmps[h], "eltmp%d" % h
                        eltmp2, eltmp2_k = eltmp2s[h], "eltmp2%d" % h
                        for dc in range(DC):
                            ctx.emit("T", lambda t: t.matmul(
                                out=pt[:, pcol:pcol + 512],
                                lhsT=xnT[:, dc * T + tc * 128: dc * T + (tc + 1) * 128],
                                rhs=wkv[:, dc * 2048 + part * 1024 + pcol:
                                        dc * 2048 + part * 1024 + pcol + 512],
                                start=(dc == 0), stop=(dc == DC - 1)),
                                reads=["xnT", "wkv"], writes=[ptk], accum=(dc > 0))
                            yield
                        if part == 0:
                            ctx.emit("V", lambda v: v.tensor_scalar(
                                out=eltmp[:], in0=pt[:, pcol:pcol + 512],
                                scalar1=0.0, scalar2=None, op0=OP.min),
                                reads=[ptk], writes=[eltmp_k])
                            yield
                            ctx.emit("S", lambda s: s.activation(
                                out=eltmp2[:], in_=eltmp[:], func=AF.Exp),
                                reads=[eltmp_k], writes=[eltmp2_k])
                            yield
                            ctx.emit("V", lambda v: v.scalar_tensor_tensor(
                                out=dst[:, pcol:pcol + 512],
                                in0=pt[:, pcol:pcol + 512], scalar=0.0,
                                in1=eltmp2[:], op0=OP.max, op1=OP.add),
                                reads=[ptk, eltmp2_k], writes=[dstk])
                            yield
                        else:
                            ctx.emit("S", lambda s: s.copy(
                                out=dst[:, pcol:pcol + 512],
                                in_=pt[:, pcol:pcol + 512]),
                                reads=[ptk], writes=[dstk])
                            yield
                for c in range(8):
                    st_ = (tc == 0 and c in (0, 4))
                    sp_ = (tc == TC - 1 and c in (3, 7))
                    ctx.emit("T", lambda t: t.matmul(
                        out=P[3][:, c * 128:(c + 1) * 128],
                        lhsT=k_bf[:, c * 128:(c + 1) * 128],
                        rhs=v_bf[:, c * 128:(c + 1) * 128],
                        start=st_, stop=sp_),
                        reads=[k_bfk, v_bfk], writes=PKf(3),
                        accum=not (tc == 0 and c == 0))
                    yield
                for h in range(2):
                    ctx.emit("T", lambda t: t.matmul(
                        out=P[2][0:1, 0:512], lhsT=c_ok[:],
                        rhs=k_bf[:, h * 512:(h + 1) * 512], start=True, stop=True),
                        reads=["c_ok", k_bfk], writes=[PKa(2, 0)])
                    yield
                    ctx.emit("V", lambda v: v.scalar_tensor_tensor(
                        out=ksum[0:1, h * 512:(h + 1) * 512],
                        in0=P[2][0:1, 0:512], scalar=1.0,
                        in1=ksum[0:1, h * 512:(h + 1) * 512],
                        op0=OP.mult, op1=OP.add),
                        reads=[PKa(2, 0), "ksum"], writes=["ksum"])
                    yield
                topk_tc(tc)

            ln_stats()
            ln_xn(0, rb=rb, sc_all=sc_all, sc_key="sc_all")
            for tc in range(TC):
                nxt = (ln_xn_gen(tc + 1, rb, sc_all, "sc_all")
                       if tc + 1 < TC else None)
                weave([nxt, projkv_gen(tc)])
            ctx.emit("V", lambda v: v.memset(kvb[:], 0.0), writes=["kvb"])
            for c in range(8):
                ctx.emit("V", lambda v: v.tensor_copy(
                    out=kvb[0:64, c * 128:c * 128 + 64],
                    in_=P[3][0:64, c * 128:c * 128 + 64]),
                    reads=[PKa(3, c // 4)], writes=["kvb"])
                ctx.emit("V", lambda v: v.tensor_copy(
                    out=kvb[64:128, c * 128 + 64:c * 128 + 128],
                    in_=P[3][64:128, c * 128 + 64:c * 128 + 128]),
                    reads=[PKa(3, c // 4)], writes=["kvb"])
            ksb = SB(stA, "ksb", [1, 1024], BF16)
            ctx.emit("V", lambda v: v.tensor_copy(out=ksb[:], in_=ksum[:]),
                     reads=["ksum"], writes=["ksb"])
            ctx.emit("V", lambda v: v.memset(scb[:], 0.0), writes=["scb"])
            for c in range(8):
                ctx.dma("Q", scb[0:64, 2 * c:2 * c + 1],
                        ksb[0:1, c * 128:c * 128 + 64],
                        reads=["ksb"], writes=["scb"])
                ctx.dma("Q", scb[64:128, 2 * c + 1:2 * c + 2],
                        ksb[0:1, c * 128 + 64:c * 128 + 128],
                        reads=["ksb"], writes=["scb"])
            ctx.barrier()
            _mp = ctx.emit("V", lambda v: v.nop())
            _mp.then_inc(ctx.sems["MP"], 1)

        with ExitStack() as stB:
            c_s2 = SB(stB, "c_s2", [2, 128], BF16)
            ctx.dma("G", c_s2[:], sel2, writes=["c_s2"])
            wqo = SB(stB, "wqo", [128, DC * 1024], BF16)
            bout_b = bcast_row(stB, "bout_b", bout.rearrange("(o d) -> o d", o=1),
                               D, 2)
            q_f = SB(stB, "q_f", [128, 1024], F32)
            qT = SB(stB, "qT", [128, DC * 512], BF16)
            att_fm = SB(stB, "att_fm", [128, 8 * 512], BF16)
            r2ab = [SB(stB, "r2ab%d" % i, [2, 512], BF16) for i in range(2)]
            r2af = [SB(stB, "r2af%d" % i, [2, 512], F32) for i in range(2)]
            r2ts = [SB(stB, "r2t%d" % i, [128, 512], F32) for i in range(2)]
            eltmp = SB(stB, "eltmpB", [128, 512], F32)
            eltmp2 = SB(stB, "eltmpB2", [128, 512], F32)
            for it in range(NIT):
                ctx.dma("Q", wqo[:].rearrange("p (dc f) -> p dc f", f=1024),
                        qkvcache.rearrange("(dc p) f -> p dc f", p=128)[:, :, 0:1024],
                        reads=["qkvcache"], writes=["wqo"])
                for lc in range(4):
                    tc = it * 4 + lc
                    for h in range(2):
                        for dc in range(DC):
                            ctx.emit("T", lambda t: t.matmul(
                                out=P[h][:, 0:512],
                                lhsT=xnT[:, dc * T + tc * 128: dc * T + (tc + 1) * 128],
                                rhs=wqo[:, dc * 1024 + h * 512: dc * 1024 + h * 512 + 512],
                                start=(dc == 0), stop=(dc == DC - 1)),
                                reads=["xnT", "wqo"], writes=[PKa(h, 0)], accum=(dc > 0))
                        ctx.emit("V", lambda v: v.tensor_scalar(
                            out=eltmp[:], in0=P[h][:, 0:512], scalar1=0.0,
                            scalar2=None, op0=OP.min), reads=[PKa(h, 0)], writes=["eltmpB"])
                        ctx.emit("S", lambda s: s.activation(
                            out=eltmp2[:], in_=eltmp[:], func=AF.Exp),
                            reads=["eltmpB"], writes=["eltmpB2"])
                        ctx.emit("V", lambda v: v.scalar_tensor_tensor(
                            out=q_f[:, h * 512:(h + 1) * 512], in0=P[h][:, 0:512],
                            scalar=0.0, in1=eltmp2[:], op0=OP.max, op1=OP.add),
                            reads=[PKa(h, 0), "eltmpB2"], writes=["q_f"])
                    for dc in range(DC):
                        bk = dc % 2
                        pcol = bk * 512 + ((dc % 4) // 2) * 128
                        ctx.emit("T", lambda t: t.transpose(
                            out=P[2][:, pcol:pcol + 128],
                            in_=q_f[:, dc * 128:(dc + 1) * 128], identity=c_id[:]),
                            reads=["q_f", "c_id"], writes=[PKa(2, bk)])
                        ctx.emit("S", lambda s: s.copy(
                            out=qT[:, dc * 512 + lc * 128: dc * 512 + (lc + 1) * 128],
                            in_=P[2][:, pcol:pcol + 128]),
                            reads=[PKa(2, bk)], writes=["qT"])
                ctx.dma("S", wqo[:].rearrange("p (fc f) -> p fc f", f=1024),
                        ocache.rearrange("(fc p) f -> p fc f", p=128),
                        reads=["ocache"], writes=["wqo"])
                for c in range(8):
                    cp = c % 2
                    pq, pqk = (P[2], PKa(2, 1)) if cp == 0 else (P[3], PKa(3, 1))
                    ctx.emit("T", lambda t: t.matmul(
                        out=pq[0:2, 512:1024], lhsT=scb[:, 2 * c:2 * c + 2],
                        rhs=qT[:, c * 512:(c + 1) * 512], start=True, stop=True),
                        reads=["scb", "qT"], writes=[pqk])
                    ctx.emit("V", lambda v: v.tensor_scalar(
                        out=r2af[cp][:], in0=pq[0:2, 512:1024], scalar1=ATT_EPS,
                        scalar2=None, op0=OP.add),
                        reads=[pqk], writes=["r2af%d" % cp])
                    ctx.emit("V", lambda v: v.reciprocal(
                        out=r2af[cp][:], in_=r2af[cp][:]),
                        reads=["r2af%d" % cp], writes=["r2af%d" % cp])
                    ctx.emit("V", lambda v: v.tensor_copy(
                        out=r2ab[cp][:], in_=r2af[cp][:]),
                        reads=["r2af%d" % cp], writes=["r2ab%d" % cp])
                    ctx.emit("T", lambda t: t.matmul(
                        out=P[1][:, cp * 512:cp * 512 + 512],
                        lhsT=c_s2[:], rhs=r2ab[cp][:],
                        start=True, stop=True),
                        reads=["c_s2", "r2ab%d" % cp], writes=[PKa(1, cp)])
                    ctx.emit("V", lambda v: v.tensor_copy(
                        out=r2ts[cp][:], in_=P[1][:, cp * 512:cp * 512 + 512]),
                        reads=[PKa(1, cp)], writes=["r2t%d" % cp])
                    ctx.emit("T", lambda t: t.matmul(
                        out=P[0][:, cp * 512:cp * 512 + 512],
                        lhsT=kvb[:, c * 128:(c + 1) * 128],
                        rhs=qT[:, c * 512:(c + 1) * 512], start=True, stop=True),
                        reads=["kvb", "qT"], writes=[PKa(0, cp)])
                    ctx.emit("V", lambda v: v.tensor_tensor(
                        out=att_fm[:, c * 512:(c + 1) * 512],
                        in0=P[0][:, cp * 512:cp * 512 + 512],
                        in1=r2ts[cp][:], op=OP.mult),
                        reads=[PKa(0, cp), "r2t%d" % cp], writes=["att_fm"])
                for lc in range(4):
                    tc = it * 4 + lc
                    for h in range(2):
                        for fc in range(DC):
                            ctx.emit("T", lambda t: t.matmul(
                                out=P[3][:, h * 512:(h + 1) * 512],
                                lhsT=att_fm[:, fc * 512 + lc * 128:
                                            fc * 512 + (lc + 1) * 128],
                                rhs=wqo[:, fc * 1024 + h * 512:
                                        fc * 1024 + h * 512 + 512],
                                start=(fc == 0), stop=(fc == DC - 1)),
                                reads=["att_fm", "wqo"], writes=[PKa(3, h)],
                                accum=(fc > 0))
                        ctx.emit("V", lambda v: v.scalar_tensor_tensor(
                            out=s_tok[:, tc * 1024 + h * 512:
                                      tc * 1024 + (h + 1) * 512],
                            in0=P[3][:, h * 512:(h + 1) * 512], scalar=1.0,
                            in1=bout_b[:, h * 512:(h + 1) * 512],
                            op0=OP.mult, op1=OP.add),
                            reads=[PKa(3, h), "bout_b"], writes=["s%d" % tc])
            ctx.barrier()
        stAB.close()

        for e_ in range(E):
            ctx._wait("G", "MP", max(1, e_))
            ctx.dma("G", gcache[e_], gate_p[e_], writes=["gcache%d" % e_],
                    slot="gcw%d" % e_)
            ctx.dma("G", ucache[e_], up_p[e_], writes=["ucache%d" % e_],
                    slot="ucw%d" % e_)
            ctx.dma("G", dcache[e_], down_p[e_], writes=["dcache%d" % e_],
                    slot="dcw%d" % e_)
            for kk in ("gcache%d" % e_, "ucache%d" % e_, "dcache%d" % e_):
                ctx.sticky[kk] = ctx.bufs[kk]["w"]
            ctx.noburst.update({"D:gcw%d" % e_, "D:ucw%d" % e_,
                                "D:dcw%d" % e_})

        with ExitStack() as stM:
            h_w = SB(stM, "h_w", [128, FC * 512], BF16)
            gacc = SB(stM, "gacc", [128, 4 * 1024], F32)
            gbufs = [SB(stM, "gbuf%d" % i, [128, DC * 256], BF16) for i in range(2)]
            ubufs = [SB(stM, "ubuf%d" % i, [128, DC * 256], BF16) for i in range(2)]
            dbufs = [SB(stM, "dbuf%d" % i, [128, 2 * 1024], BF16) for i in range(2)]
            gsils = [xn_f[:, 0:512], xn_f[:, 512:1024]]
            gtmp, gtmp_k = lndump, "lndump"
            GOFF = [2 * i for i in range(11)]
            DGROUPS = [2] * 11
            GOFFD = [2 * i for i in range(11)]

            units = []
            for tb in range(NTB):
                for e in range(E):
                    for g in range(len(FGROUPS)):
                        units.append(("gu", tb, e, g))
                    for g in range(len(DGROUPS)):
                        units.append(("dn", tb, e, g))
                units.append(("fin", tb, 0, 0))

            gu_ct = [0]
            dn_ct = [0]

            def issue_dma(u):
                kind, tb, e, g = u
                if kind == "fin":
                    return
                if kind == "gu":
                    ng = FGROUPS[g]
                    fc0 = GOFF[g]
                    w_ = ng * 128
                    sl = gu_ct[0] % 2
                    gu_ct[0] += 1
                    gb, ub = gbufs[sl], ubufs[sl]
                    gk, uk = "gbuf%d" % sl, "ubuf%d" % sl
                    ctx.dma("Q", gb[:, 0:DC * w_].rearrange(
                        "p (dc f) -> p dc f", f=w_),
                        gcache.rearrange("e (dc p) f -> e p dc f", p=128)
                        [e, :, :, fc0 * 128:(fc0 + ng) * 128],
                        reads=["gcache%d" % e], writes=[gk], slot=gk)
                    ctx.dma("S", ub[:, 0:DC * w_].rearrange(
                        "p (dc f) -> p dc f", f=w_),
                        ucache.rearrange("e (dc p) f -> e p dc f", p=128)
                        [e, :, :, fc0 * 128:(fc0 + ng) * 128],
                        reads=["ucache%d" % e], writes=[uk], slot=uk)
                else:
                    ng = DGROUPS[g]
                    fc0 = GOFFD[g]
                    sl = dn_ct[0] % 2
                    dn_ct[0] += 1
                    db = dbufs[sl]
                    dk = "dbuf%d" % sl
                    ctx.dma("S" if g % 2 else "Q",
                            db[:, 0:ng * 1024].rearrange(
                                "p (fc f) -> p fc f", f=1024),
                            dcache.rearrange("e (fc p) f -> e p fc f", p=128)
                            [e, :, fc0:fc0 + ng, :],
                            reads=["dcache%d" % e], writes=[dk], slot=dk)

            cg_ct = [0]
            cd_ct = [0]

            def emit_compute(u):
                kind, tb, e, g = u
                if kind == "fin":
                    for lc in range(4):
                        tc = tb * 4 + lc
                        ctx.emit("S", lambda s: s.activation(
                            out=gtmp[:], in_=gacc[:, lc * 1024:(lc + 1) * 1024],
                            func=AF.Silu), reads=["gacc"], writes=[gtmp_k])
                        ctx.emit("V", lambda v: v.tensor_tensor(
                            out=gtmp[:], in0=gtmp[:],
                            in1=s_tok[:, tc * 1024:(tc + 1) * 1024], op=OP.mult),
                            reads=[gtmp_k, "s%d" % tc], writes=[gtmp_k])
                        ctx.emit("V", lambda v: v.tensor_tensor(
                            out=x[:, tc * 1024:(tc + 1) * 1024],
                            in0=x[:, tc * 1024:(tc + 1) * 1024], in1=gtmp[:],
                            op=OP.add), reads=[gtmp_k, "x%d" % tc],
                            writes=["x%d" % tc])
                    return
                if kind == "gu":
                    if tb == 0 and g == 0:
                        _m = ctx.emit("V", lambda v: v.nop())
                        _m.then_inc(ctx.sems["MP"], 1)
                    ng = FGROUPS[g]
                    fc0 = GOFF[g]
                    w_ = ng * 128
                    sl = cg_ct[0] % 2
                    cg_ct[0] += 1
                    gb, ub = gbufs[sl], ubufs[sl]
                    gk, uk = "gbuf%d" % sl, "ubuf%d" % sl
                    for fl in range(ng):
                        fc = fc0 + fl
                        pcg = (fc % 2) * 512
                        gsb = gsils[fc % 2]
                        gsk = "xn_f:%d" % (fc % 2)
                        for dc in range(DC):
                            ctx.emit("T", lambda t: t.matmul(
                                out=P[0][:, pcg:pcg + 512],
                                lhsT=gb[:, dc * w_ + fl * 128: dc * w_ + (fl + 1) * 128],
                                rhs=xnT[:, dc * T + tb * 512: dc * T + (tb + 1) * 512],
                                start=(dc == 0), stop=(dc == DC - 1)),
                                reads=[gk, "xnT"], writes=[PKa(0, fc % 2)],
                                accum=(dc > 0))
                        ctx.emit("S", lambda s: s.activation(
                            out=gsb, in_=P[0][:, pcg:pcg + 512], func=AF.Silu),
                            reads=[PKa(0, fc % 2)], writes=[gsk])
                        for dc in range(DC):
                            ctx.emit("T", lambda t: t.matmul(
                                out=P[1][:, pcg:pcg + 512],
                                lhsT=ub[:, dc * w_ + fl * 128: dc * w_ + (fl + 1) * 128],
                                rhs=xnT[:, dc * T + tb * 512: dc * T + (tb + 1) * 512],
                                start=(dc == 0), stop=(dc == DC - 1)),
                                reads=[uk, "xnT"], writes=[PKa(1, fc % 2)],
                                accum=(dc > 0))
                        ctx.emit("V", lambda v: v.tensor_tensor(
                            out=h_w[:, fc * 512:(fc + 1) * 512],
                            in0=P[1][:, pcg:pcg + 512], in1=gsb, op=OP.mult),
                            reads=[PKa(1, fc % 2), gsk], writes=["h_w"])
                else:
                    ng = DGROUPS[g]
                    fc0 = GOFFD[g]
                    sl = cd_ct[0] % 2
                    cd_ct[0] += 1
                    db = dbufs[sl]
                    dk = "dbuf%d" % sl
                    for fl in range(ng):
                        fc = fc0 + fl
                        for lc in range(4):
                            for h in range(2):
                                ctx.emit("T", lambda t: t.matmul(
                                    out=P[lc][:, h * 512:(h + 1) * 512],
                                    lhsT=h_w[:, fc * 512 + lc * 128:
                                             fc * 512 + (lc + 1) * 128],
                                    rhs=db[:, fl * 1024 + h * 512:
                                           fl * 1024 + h * 512 + 512],
                                    start=(fc == 0), stop=(fc == FC - 1)),
                                    reads=["h_w", dk], writes=[PKa(lc, h)],
                                    accum=(fc > 0))
                    if g == len(DGROUPS) - 1:
                        for lc in range(4):
                            tc = tb * 4 + lc
                            wecol = ew[:, tc * E + e: tc * E + e + 1]
                            if e == 0:
                                ctx.emit("V", lambda v: v.tensor_scalar(
                                    out=gacc[:, lc * 1024:(lc + 1) * 1024],
                                    in0=P[lc][:], scalar1=wecol, scalar2=None,
                                    op0=OP.mult),
                                    reads=PKf(lc) + ["ew"], writes=["gacc"])
                            else:
                                ctx.emit("V", lambda v: v.scalar_tensor_tensor(
                                    out=gacc[:, lc * 1024:(lc + 1) * 1024],
                                    in0=P[lc][:], scalar=wecol,
                                    in1=gacc[:, lc * 1024:(lc + 1) * 1024],
                                    op0=OP.mult, op1=OP.add),
                                    reads=PKf(lc) + ["ew", "gacc"], writes=["gacc"])

            pend = 0
            for i in range(len(units)):
                while pend <= i + 1 and pend < len(units):
                    issue_dma(units[pend])
                    pend += 1
                emit_compute(units[i])
            ctx.barrier()

        c2f = math.sqrt(128.0) / STD
        c3f = MU / STD
        ln_stats()
        ctx.dma("Q", wsb2[:], RAP(tensor=wscale.tensor, offset=0,
                                  ap=[[0, 128], [1, 1]]), writes=["wsb2"])
        with ExitStack() as stb:
            c_ohT = SB(stb, "c_ohT", [NB, S + 768], F32)
            ctx.dma("Q", c_ohT[:], ohmat, writes=["c_ohT"])
            relb = SB(stb, "relb", [NB, 1], F32)
            ctx.dma("Q", relb[:], rel_bias.rearrange("(b o) -> b o", o=1),
                    writes=["relb"])
            btmp = SB(stb, "btmp", [128, 128], F32)
            bt2 = SB(stb, "bt2", [128, 128], F32)

            def biasext_gen():
                ctx.emit("V", lambda v: v.memset(btmp[:], 0.0), writes=["btmp"])
                yield
                for ch in range(NBE):
                    ctx.emit("T", lambda t: t.matmul(
                        out=P[1][:, 0:1], lhsT=c_ohT[:, ch * 128:(ch + 1) * 128],
                        rhs=relb[:], start=True, stop=True),
                        reads=["c_ohT", "relb"], writes=[PKa(1, 0)])
                    yield
                    ctx.emit("V", lambda v: v.tensor_scalar(
                        out=btmp[:, ch:ch + 1], in0=P[1][:, 0:1], scalar1=c2f,
                        scalar2=-c3f, op0=OP.mult, op1=OP.add),
                        reads=[PKa(1, 0)], writes=["btmp"])
                    yield
                ctx.emit("T", lambda t: t.transpose(
                    out=P[1][:, 0:128], in_=btmp[:], identity=c_id[:]),
                    reads=["btmp", "c_id"], writes=[PKa(1, 0)])
                yield
                ctx.emit("V", lambda v: v.tensor_copy(
                    out=bt2[:], in_=P[1][:, 0:128]), reads=[PKa(1, 0)], writes=["bt2"])
                yield
                ctx.dma("Q", bias_ext.rearrange("(c p) -> c p", p=128),
                        bt2[0:NBE, :], reads=["bt2"], writes=["bias_ext"])
                yield

            def qstats_gen():
                for tc in range(TC):
                    xs = x[:, tc * 1024:(tc + 1) * 1024]
                    ctx.emit("V", lambda v: v.tensor_scalar(
                        out=lndump[:], in0=xs, scalar1=m_st[:, tc:tc + 1],
                        scalar2=r_st[:, tc:tc + 1], op0=OP.subtract, op1=OP.mult),
                        reads=["x%d" % tc, "m_st", "r_st"], writes=["lndump"])
                    yield
                    ctx.emit("V", lambda v: v.tensor_reduce(
                        out=amx_a[:, tc:tc + 1], in_=lndump[:], axis=AX.X,
                        op=OP.max, apply_absolute_value=True),
                        reads=["lndump"], writes=["amx_a"])
                    yield
                    ctx.emit("S", lambda s: s.activation(
                        out=lndump[:], in_=lndump[:], func=AF.Square,
                        accum_out=stot_a[:, tc:tc + 1]),
                        reads=["lndump"], writes=["lndump", "stot_a"])
                    yield
                ctx.emit("V", lambda v: v.tensor_scalar(
                    out=stot_a[:], in0=stot_a[:], scalar1=1.0 / 1024.0,
                    scalar2=RMS_EPS, op0=OP.mult, op1=OP.add),
                    reads=["stot_a"], writes=["stot_a"])
                yield
                ctx.emit("V", lambda v: v.reciprocal(out=stot_a[:], in_=stot_a[:]),
                         reads=["stot_a"], writes=["stot_a"])
                yield
                ctx.emit("S", lambda s: s.activation(
                    out=stot_a[:], in_=stot_a[:], func=AF.Sqrt),
                    reads=["stot_a"], writes=["stot_a"])
                yield
                ctx.emit("V", lambda v: v.tensor_tensor(
                    out=ct_a[:], in0=amx_a[:], in1=stot_a[:], op=OP.mult),
                    reads=["amx_a", "stot_a"], writes=["ct_a"])
                yield
                ctx.emit("V", lambda v: v.tensor_scalar(
                    out=ct_a[:], in0=ct_a[:], scalar1=ACT_EPS, scalar2=None,
                    op0=OP.max), reads=["ct_a"], writes=["ct_a"])
                yield
                ctx.emit("V", lambda v: v.reciprocal(out=amx_a[:], in_=ct_a[:]),
                         reads=["ct_a"], writes=["amx_a"])
                yield
                ctx.emit("V", lambda v: v.tensor_tensor(
                    out=stot_a[:], in0=stot_a[:], in1=amx_a[:], op=OP.mult),
                    reads=["stot_a", "amx_a"], writes=["stot_a"])
                yield
                ctx.emit("V", lambda v: v.tensor_scalar(
                    out=stot_a[:], in0=stot_a[:], scalar1=127.0, scalar2=None,
                    op0=OP.mult), reads=["stot_a"], writes=["stot_a"])
                yield
                ctx.emit("V", lambda v: v.tensor_scalar(
                    out=cq2_a[:], in0=ct_a[:], scalar1=wsb2[:], scalar2=None,
                    op0=OP.mult), reads=["ct_a", "wsb2"], writes=["cq2_a"])
                yield
                ctx.emit("V", lambda v: v.tensor_scalar(
                    out=cq2_a[:], in0=cq2_a[:], scalar1=1.0 / 127.0, scalar2=None,
                    op0=OP.mult), reads=["cq2_a"], writes=["cq2_a"])
                yield
                ctx.emit("V", lambda v: v.tensor_tensor(
                    out=amx_a[:], in0=r_st[:], in1=stot_a[:], op=OP.mult),
                    reads=["r_st", "stot_a"], writes=["amx_a"])
                yield
                ctx.emit("V", lambda v: v.scalar_tensor_tensor(
                    out=ct_a[:], in0=m_st[:], scalar=-1.0, in1=amx_a[:],
                    op0=OP.mult, op1=OP.mult),
                    reads=["m_st", "amx_a"], writes=["ct_a"])
                yield

            def ln2_gen():
                for tc in range(TC):
                    for _ in ln_xn_gen(tc, pingpong=True):
                        yield

            weave([ln2_gen(), biasext_gen(), qstats_gen()])
            ctx.barrier()

        with ExitStack() as st1:

            bias2_b = SB(st1, "bias2_b", [128, D], F32)
            stAtt = st1.enter_context(ExitStack())
            qf = SB(stAtt, "qf", [128, T], BF16)
            kf = SB(stAtt, "kf", [128, T], BF16)
            with ExitStack() as stq:
                qk_f = SB(stq, "qk_f", [128, T], F32)
                wqkb = SB(stq, "wqkb", [128, DC * DQK], BF16)
                ctx.dma("Q", wqkb[:].rearrange("p (dc f) -> p dc f", f=DQK),
                        qkcache.rearrange("(dc p) f -> p dc f", p=128),
                        reads=["qkcache"], writes=["wqkb"])
                bqk_t = SB(stq, "bqk_t", [128, 1], F32)
                ctx.dma("Q", bqk_t[:], bqk.rearrange("(p o) -> p o", o=1),
                        writes=["bqk_t"])
                g0 = SB(stq, "g0", [128, 1], F32)
                g1 = SB(stq, "g1", [128, 1], F32)
                b0 = SB(stq, "b0", [128, 1], F32)
                b1 = SB(stq, "b1", [128, 1], F32)
                ctx.dma("Q", g0[:], RAP(tensor=gamma.tensor, offset=0,
                                        ap=[[1, 128], [1, 1]]), writes=["g0"])
                ctx.dma("Q", g1[:], RAP(tensor=gamma.tensor, offset=128,
                                        ap=[[1, 128], [1, 1]]), writes=["g1"])
                ctx.dma("Q", b0[:], RAP(tensor=beta.tensor, offset=0,
                                        ap=[[1, 128], [1, 1]]), writes=["b0"])
                ctx.dma("Q", b1[:], RAP(tensor=beta.tensor, offset=128,
                                        ap=[[1, 128], [1, 1]]), writes=["b1"])
                bcast_row_into(bias2_b, stq, "bias2_b",
                               bias2.rearrange("(o d) -> o d", o=1), D, 3)
                cqs = 1.0 / (float(S) * STD)
                ctx.emit("V", lambda v: v.tensor_scalar(
                    out=g0[:], in0=g0[:], scalar1=cqs, scalar2=None, op0=OP.mult),
                    reads=["g0"], writes=["g0"])
                ctx.emit("V", lambda v: v.tensor_scalar(
                    out=b0[:], in0=b0[:], scalar1=cqs, scalar2=None, op0=OP.mult),
                    reads=["b0"], writes=["b0"])
                for it in range(NIT):
                    for dc in range(DC):
                        ctx.emit("T", lambda t: t.matmul(
                            out=P[0][:, 0:512],
                            lhsT=wqkb[:, dc * DQK:(dc + 1) * DQK],
                            rhs=xnT[:, dc * T + it * 512: dc * T + (it + 1) * 512],
                            start=(dc == 0), stop=(dc == DC - 1)),
                            reads=["wqkb", "xnT"], writes=[PKa(0, 0)], accum=(dc > 0))
                    ctx.emit("S", lambda s: s.activation(
                        out=qk_f[:, it * 512:(it + 1) * 512], in_=P[0][:, 0:512],
                        func=AF.Silu, bias=bqk_t[:], scale=1.0),
                        reads=[PKa(0, 0), "bqk_t"], writes=["qk_f"])
                ctx.emit("V", lambda v: v.tensor_scalar(
                    out=qf[:], in0=qk_f[:], scalar1=g0[:], scalar2=b0[:],
                    op0=OP.mult, op1=OP.add),
                    reads=["qk_f", "g0", "b0"], writes=["qf"])
                ctx.emit("V", lambda v: v.tensor_scalar(
                    out=kf[:], in0=qk_f[:], scalar1=g1[:], scalar2=b1[:],
                    op0=OP.mult, op1=OP.add),
                    reads=["qk_f", "g1", "b1"], writes=["kf"])
                ctx.barrier()

            v_tok = SB(stAtt, "v_tok", [128, TC * 1024], BF16)
            with ExitStack() as stv:
                bv_b = bcast_row(stv, "bv_b", bv.rearrange("(o d) -> o d", o=1),
                                 DV, 2)
                wv_h = SB(stv, "wv_h", [128, DC * 512], BF16)
                vtmp = SB(stv, "vtmp", [128, 512], F32)
                for h in range(2):
                    ctx.dma("Q", wv_h[:].rearrange("p (dc f) -> p dc f", f=512),
                            vcache.rearrange("(dc p) f -> p dc f", p=128)
                            [:, :, h * 512:(h + 1) * 512],
                            reads=["vcache"], writes=["wv_h"])
                    for tc in range(TC):
                        for dc in range(DC):
                            ctx.emit("T", lambda t: t.matmul(
                                out=P[h][:, 0:512],
                                lhsT=xnT[:, dc * T + tc * 128: dc * T + (tc + 1) * 128],
                                rhs=wv_h[:, dc * 512:(dc + 1) * 512],
                                start=(dc == 0), stop=(dc == DC - 1)),
                                reads=["xnT", "wv_h"], writes=[PKa(h, 0)], accum=(dc > 0))
                        ctx.emit("V", lambda v: v.scalar_tensor_tensor(
                            out=vtmp[:], in0=P[h][:, 0:512], scalar=1.0,
                            in1=bv_b[:, h * 512:(h + 1) * 512],
                            op0=OP.mult, op1=OP.add),
                            reads=[PKa(h, 0), "bv_b"], writes=["vtmp"])
                        ctx.emit("S", lambda s: s.activation(
                            out=v_tok[:, tc * 1024 + h * 512:
                                      tc * 1024 + (h + 1) * 512],
                            in_=vtmp[:], func=AF.Silu),
                            reads=["vtmp"], writes=["v_tok"])
                ctx.barrier()

            xnT_stk.close()
            wTb = SB(st1, "wTb", [128, DC * 1024], BF16)

            def w8_gen():
                w8f, w8f_k = yq, "yq"
                for oc in range(DC):
                    ctx.dma("G", w8f[:],
                            w_int8.rearrange("(oc p) d -> oc p d", p=128)[oc],
                            writes=[w8f_k])
                    yield
                    for dc in range(DC):
                        bk = dc % 2
                        pcol = bk * 512 + ((dc % 4) // 2) * 128
                        ctx.emit("T", lambda t: t.transpose(
                            out=P[3][:, pcol:pcol + 128],
                            in_=w8f[:, dc * 128:(dc + 1) * 128], identity=c_id[:]),
                            reads=[w8f_k, "c_id"], writes=[PKa(3, bk)])
                        yield
                        ctx.emit("S", lambda s: s.copy(
                            out=wTb[:, dc * 1024 + oc * 128:
                                    dc * 1024 + (oc + 1) * 128],
                            in_=P[3][:, pcol:pcol + 128]),
                            reads=[PKa(3, bk)], writes=["wTb"])
                        yield

            yq = SB(st1, "yq", [128, 1024], F32)
            ti32 = SB(st1, "ti32", [128, 1024], I32)
            xqT = SB(st1, "xqT", [128, DC * 128], BF16)
            yq1 = SB(st1, "yq1", [128, 1024], F32)
            yq_s = [(yq, "yq"), (yq1, "yq1")]
            ti_s = [(ti32, "ti32"), (ti32, "ti32")]
            dif_s = [(xn_f, "xn_f"), (lndump, "lndump")]
            xqT_s = [(xqT, "xqT"), (xqT, "xqT")]
            ctx.barrier()

            def ffn_quant(tc):
                p = tc % 2
                yqb, yq_k = yq_s[p]
                tib, ti_k = ti_s[p]
                difb, dif_k = dif_s[p]
                ctx.emit("S", lambda s: s.activation(
                    out=yqb[:], in_=x[:, tc * 1024:(tc + 1) * 1024],
                    func=AF.Identity,
                    scale=amx_a[:, tc:tc + 1], bias=ct_a[:, tc:tc + 1]),
                    reads=["x%d" % tc, "amx_a", "ct_a"], writes=[yq_k])
                yield
                ctx.emit("S", lambda s: s.copy(out=tib[:], in_=yqb[:]),
                         reads=[yq_k], writes=[ti_k])
                yield
                ctx.emit("V", lambda v: v.scalar_tensor_tensor(
                    out=difb[:], in0=tib[:], scalar=-1.0, in1=yqb[:],
                    op0=OP.mult, op1=OP.add), reads=[ti_k, yq_k], writes=[dif_k])
                yield
                ctx.emit("V", lambda v: v.tensor_scalar(
                    out=yqb[:], in0=difb[:], scalar1=-0.5, scalar2=None,
                    op0=OP.is_le), reads=[dif_k], writes=[yq_k])
                yield
                ctx.emit("V", lambda v: v.tensor_scalar(
                    out=difb[:], in0=difb[:], scalar1=0.5, scalar2=None,
                    op0=OP.is_ge), reads=[dif_k], writes=[dif_k])
                yield
                ctx.emit("V", lambda v: v.scalar_tensor_tensor(
                    out=difb[:], in0=yqb[:], scalar=-1.0, in1=difb[:],
                    op0=OP.mult, op1=OP.add), reads=[yq_k, dif_k], writes=[dif_k])
                yield
                ctx.emit("V", lambda v: v.tensor_tensor(
                    out=yqb[:], in0=tib[:], in1=difb[:], op=OP.add),
                    reads=[ti_k, dif_k], writes=[yq_k])
                yield
                ctx.emit("V", lambda v: v.tensor_scalar(
                    out=yqb[:], in0=yqb[:], scalar1=-128.0, scalar2=127.0,
                    op0=OP.max, op1=OP.min), reads=[yq_k], writes=[yq_k])
                yield
                ctx.emit("V", lambda v: v.tensor_scalar(
                    out=yqb[:], in0=yqb[:], scalar1=cq2_a[:, tc:tc + 1],
                    scalar2=None, op0=OP.mult),
                    reads=[yq_k, "cq2_a"], writes=[yq_k])
                yield

            def ffn_mm(tc):
                p = tc % 2
                yqb, yq_k = yq_s[p]
                difb, dif_k = dif_s[p]
                xqTb, xqT_k = xqT_s[p]
                g2t = difb[:, 0:512]
                g2t_k = dif_k
                for dc in range(DC):
                    bk = dc % 2
                    pcol = bk * 512 + ((dc % 4) // 2) * 128
                    ctx.emit("T", lambda t: t.transpose(
                        out=P[3][:, pcol:pcol + 128],
                        in_=yqb[:, dc * 128:(dc + 1) * 128], identity=c_id[:]),
                        reads=[yq_k, "c_id"], writes=[PKa(3, bk)])
                    yield
                    ctx.emit("S", lambda s: s.copy(
                        out=xqTb[:, dc * 128:(dc + 1) * 128],
                        in_=P[3][:, pcol:pcol + 128]),
                        reads=[PKa(3, bk)], writes=[xqT_k])
                    yield
                for h in range(2):
                    for dc in range(DC):
                        ctx.emit("T", lambda t: t.matmul(
                            out=P[3][:, h * 512:(h + 1) * 512],
                            lhsT=xqTb[:, dc * 128:(dc + 1) * 128],
                            rhs=wTb[:, dc * 1024 + h * 512:
                                    dc * 1024 + h * 512 + 512],
                            start=(dc == 0), stop=(dc == DC - 1)),
                            reads=[xqT_k, "wTb"],
                            writes=[PKa(3, h)], accum=(dc > 0))
                        yield
                    ctx.emit("V", lambda v: v.scalar_tensor_tensor(
                        out=g2t, in0=P[3][:, h * 512:(h + 1) * 512], scalar=1.0,
                        in1=bias2_b[:, h * 512:(h + 1) * 512],
                        op0=OP.mult, op1=OP.add),
                        reads=[PKa(3, h), "bias2_b"], writes=[g2t_k])
                    yield
                    ctx.emit("S", lambda s: s.activation(
                        out=g2t, in_=g2t, func=AF.Silu),
                        reads=[g2t_k], writes=[g2t_k])
                    yield
                    ctx.emit("V", lambda v: v.tensor_tensor(
                        out=g2t, in0=g2t,
                        in1=s_tok[:, tc * 1024 + h * 512:
                                  tc * 1024 + (h + 1) * 512], op=OP.mult),
                        reads=[g2t_k, "s%d" % tc], writes=[g2t_k])
                    yield
                    ctx.emit("V", lambda v: v.tensor_tensor(
                        out=x[:, tc * 1024 + h * 512: tc * 1024 + (h + 1) * 512],
                        in0=x[:, tc * 1024 + h * 512: tc * 1024 + (h + 1) * 512],
                        in1=g2t, op=OP.add),
                        reads=[g2t_k, "x%d" % tc], writes=["x%d" % tc])
                    yield

            def interleave(*gs):
                gs = [g for g in gs if g is not None]
                while gs:
                    alive = []
                    for g in gs:
                        try:
                            next(g)
                            alive.append(g)
                        except StopIteration:
                            pass
                    gs = alive
                    yield

            with ExitStack() as sts:
                c_j = SB(sts, "c_j", [128, 128], F32)
                ctx.dma("Q", c_j[:], jrev, writes=["c_j"])
                c_mk = SB(sts, "c_mk", [128, 4 * 512], BF16)
                ctx.dma("Q", c_mk[:], cmasks, writes=["c_mk"])
                strip = SB(sts, "strip", [128, TC * 512], BF16)
                t1b = SB(sts, "t1b", [128, 512], F32)
                biasT = SB(sts, "biasT", [128, 512], F32)
                zt = SB(sts, "zt", [128, 512], F32)
                glt, glt_k = t1b, "t1b"
                dent, dent_k = biasT, "biasT"
                att2, att2_k = zt, "zt"
                def simav_gen(it):
                    i0 = it * 512
                    jcmax = min(TC - 1, it * 4 + 3)
                    for jc in range(jcmax + 1):
                        base = (i0 - jc * 128) + 513
                        ctx.dma("S", t1b[:],
                                RAP(tensor=bias_ext.tensor, offset=base,
                                    ap=[[1, 128], [1, 512]]),
                                reads=["bias_ext"], writes=["t1b"])
                        jp = (jc % 2) * 512
                        ctx.emit("T", lambda t: t.matmul(
                            out=P[1][:, jp:jp + 512], lhsT=c_j[:], rhs=t1b[:],
                            start=True, stop=True),
                            reads=["c_j", "t1b"], writes=[PKa(1, jc % 2)])
                        ctx.emit("S", lambda s: s.copy(
                            out=biasT[:], in_=P[1][:, jp:jp + 512]),
                            reads=[PKa(1, jc % 2)], writes=["biasT"])
                        ctx.emit("T", lambda t: t.matmul(
                            out=P[0][:, jp:jp + 512],
                            lhsT=kf[:, jc * 128:(jc + 1) * 128],
                            rhs=qf[:, it * 512:(it + 1) * 512],
                            start=True, stop=True),
                            reads=["kf", "qf"], writes=[PKa(0, jc % 2)])
                        ctx.emit("V", lambda v: v.scalar_tensor_tensor(
                            out=zt[:], in0=P[0][:, jp:jp + 512], scalar=1.0,
                            in1=biasT[:], op0=OP.mult, op1=OP.add),
                            reads=[PKa(0, jc % 2), "biasT"], writes=["zt"])
                        ctx.emit("S", lambda s: s.activation(
                            out=glt[:], in_=zt[:], func=AF.Gelu),
                            reads=["zt"], writes=[glt_k])
                        ctx.emit("V", lambda v: v.scalar_tensor_tensor(
                            out=dent[:], in0=zt[:], scalar=0.0, in1=zt[:],
                            op0=OP.is_equal, op1=OP.add),
                            reads=["zt"], writes=[dent_k])
                        ctx.emit("V", lambda v: v.reciprocal(
                            out=dent[:], in_=dent[:]), reads=[dent_k], writes=[dent_k])
                        mq = it * 4 - jc
                        if mq >= 1:
                            ctx.emit("V", lambda v: v.tensor_tensor(
                                out=strip[:, jc * 512:(jc + 1) * 512], in0=glt[:],
                                in1=dent[:], op=OP.mult),
                                reads=[glt_k, dent_k], writes=["strip"])
                        else:
                            mi = -mq
                            ctx.emit("V", lambda v: v.tensor_tensor(
                                out=att2[:], in0=glt[:], in1=dent[:], op=OP.mult),
                                reads=[glt_k, dent_k], writes=[att2_k])
                            ctx.emit("V", lambda v: v.tensor_tensor(
                                out=strip[:, jc * 512:(jc + 1) * 512], in0=att2[:],
                                in1=c_mk[:, mi * 512:(mi + 1) * 512], op=OP.mult),
                                reads=[att2_k, "c_mk"], writes=["strip"])
                    for lc in range(4):
                        tci = it * 4 + lc
                        for jc in range(tci + 1):
                            for h in range(2):
                                ctx.emit("T", lambda t: t.matmul(
                                    out=P[2][:, h * 512:(h + 1) * 512],
                                    lhsT=strip[:, jc * 512 + lc * 128:
                                               jc * 512 + (lc + 1) * 128],
                                    rhs=v_tok[:, jc * 1024 + h * 512:
                                              jc * 1024 + h * 512 + 512],
                                    start=(jc == 0), stop=(jc == tci)),
                                    reads=["strip", "v_tok"], writes=PKf(2),
                                    accum=(jc > 0))
                                yield
                        ctx.emit("S", lambda s: s.copy(
                            out=s_tok[:, tci * 1024:(tci + 1) * 1024], in_=P[2][:]),
                            reads=PKf(2), writes=["s%d" % tci])
                        yield

                def ffn_chain(it):
                    for lc in range(4):
                        tc = it * 4 + lc
                        if tc == 0:
                            for _ in ffn_quant(0):
                                yield
                        nq = ffn_quant(tc + 1) if tc + 1 < TC else None
                        for _ in interleave(ffn_mm(tc), nq):
                            yield

                pend = []
                for it in range(NIT):
                    gens = [simav_gen(it)] + pend
                    if it == 0:
                        gens.append(w8_gen())
                    weave(gens)
                    pend = [ffn_chain(it)]
                weave(pend)
        for tc in range(TC):
            ctx.dma("Q", x_out.rearrange("(tc p) d -> p tc d", p=128)[:, tc],
                    x[:, tc * 1024:(tc + 1) * 1024],
                    reads=["x%d" % tc], writes=[], slot="xout")
        ctx.eng["Q"].wait_ge(ctx.sems["D:xout"], ctx.dmaval["D:xout"])


def _host_consts(S):
    consts = {}
    consts["c_ident"] = np.eye(128, dtype=np.float32)
    consts["c_jrev"] = np.eye(128, dtype=np.float32)[::-1].copy()
    consts["c_ones1"] = np.ones((1, 128), dtype=np.float32)
    consts["c_onesk"] = np.ones((128, 1), dtype=ml_dtypes.bfloat16)
    consts["c_iota8"] = np.broadcast_to(
        np.arange(E, dtype=np.float32), (128, E)).copy()
    consts["c_iotaB"] = consts["c_iota8"] + np.float32(BIG)
    sel2 = np.zeros((2, 128), dtype=np.float32)
    sel2[0, 0:64] = 1.0
    sel2[1, 64:128] = 1.0
    consts["c_sel2"] = sel2
    s16 = np.zeros((16, 8 * 128), dtype=np.float32)
    for c in range(8):
        for p in range(128):
            s16[2 * c + p // 64, c * 128 + p] = 1.0
    consts["c_sel16"] = s16.astype(ml_dtypes.bfloat16)
    masks = np.zeros((128, 4 * 512), dtype=np.float32)
    jj = np.arange(128)[:, None]
    ii = np.arange(512)[None, :]
    for mi in range(4):
        masks[:, mi * 512:(mi + 1) * 512] = (jj <= ii - mi * 128).astype(np.float32)
    consts["c_masks"] = masks.astype(ml_dtypes.bfloat16)
    n_ext = S + 768
    oh = np.zeros((NB, n_ext), dtype=np.float32)
    max_exact = NB // 2
    for dp in range(n_ext):
        delta = dp - 640
        if delta < 0 or delta >= S:
            continue
        if delta < max_exact:
            b = delta
        else:
            n_safe = np.float32(max(delta, 1))
            val = max_exact + int(
                np.float32(np.log(n_safe / np.float32(max_exact))
                           / np.float32(math.log(MAXD / max_exact))
                           * np.float32(NB - max_exact)))
            b = min(val, NB - 1)
        oh[b, dp] = 1.0
    consts["c_oh"] = oh
    return consts


def kernel(L0_attn_x, L0_attn_Wqkv, L0_attn_Wout, L0_attn_bout,
           L0_ffn_router_w, L0_ffn_gate_proj, L0_ffn_up_proj, L0_ffn_down_proj,
           L1_attn_wqk, L1_attn_bqk, L1_attn_gamma, L1_attn_beta,
           L1_attn_wv, L1_attn_bv, L1_attn_rel_bias,
           L1_ffn_w_int8, L1_ffn_weight_scale, L1_ffn_bias):
    x = np.ascontiguousarray(np.asarray(L0_attn_x, dtype=np.float32))
    B, T, D_ = x.shape
    assert B == NCORES and D_ == D
    nc = bass.Bass("TRN2", target_bir_lowering=False)
    build_program(nc, T)
    consts = _host_consts(T)
    common = dict(
        L0_Wqkv=np.ascontiguousarray(np.asarray(L0_attn_Wqkv, np.float32)),
        L0_Wout=np.ascontiguousarray(np.asarray(L0_attn_Wout, np.float32)),
        L0_bout=np.ascontiguousarray(np.asarray(L0_attn_bout, np.float32)),
        L0_router=np.ascontiguousarray(np.asarray(L0_ffn_router_w, np.float32)),
        L0_gate=np.ascontiguousarray(np.asarray(L0_ffn_gate_proj, np.float32)),
        L0_up=np.ascontiguousarray(np.asarray(L0_ffn_up_proj, np.float32)),
        L0_down=np.ascontiguousarray(np.asarray(L0_ffn_down_proj, np.float32)),
        L1_wqk=np.ascontiguousarray(np.asarray(L1_attn_wqk, np.float32)),
        L1_bqk=np.ascontiguousarray(np.asarray(L1_attn_bqk, np.float32)),
        L1_gamma=np.ascontiguousarray(np.asarray(L1_attn_gamma, np.float32)),
        L1_beta=np.ascontiguousarray(np.asarray(L1_attn_beta, np.float32)),
        L1_wv=np.ascontiguousarray(np.asarray(L1_attn_wv, np.float32)),
        L1_bv=np.ascontiguousarray(np.asarray(L1_attn_bv, np.float32)),
        L1_rel=np.ascontiguousarray(np.asarray(L1_attn_rel_bias, np.float32)),
        L1_w8=np.ascontiguousarray(np.asarray(L1_ffn_w_int8, np.int32)),
        L1_ws=np.asarray(L1_ffn_weight_scale, np.float32).reshape(1),
        L1_bias=np.ascontiguousarray(np.asarray(L1_ffn_bias, np.float32)),
        **consts,
    )
    in_maps = [dict(common, x_in=np.ascontiguousarray(x[c])) for c in range(NCORES)]
    res = run_bass_kernel_spmd(nc, in_maps, core_ids=list(range(NCORES)))
    out = np.stack([np.asarray(res.results[c]["x_out"]) for c in range(NCORES)],
                   axis=0)
    return out.astype(np.float32)


# version 22 (final)
# speedup vs baseline: 1.1140x; 1.0812x over previous
import math
import numpy as np
import ml_dtypes
import concourse.bass as bass
import concourse.mybir as mybir
from concourse.bass_utils import run_bass_kernel_spmd
from concourse.alu_op_type import AluOpType as OP
from concourse import dve_ops
from bass_rust import AP as RAP
from contextlib import ExitStack

F32 = mybir.dt.float32
BF16 = mybir.dt.bfloat16
I32 = mybir.dt.int32
I16 = mybir.dt.int16
AF = mybir.ActivationFunctionType
AX = mybir.AxisListType

NCORES = 8
D = 1024
DC = D // 128
E, FF = 8, 2816
FC = FF // 128
DQK, DV = 128, 1024
NB, MAXD = 32, 128
LN_EPS = 1e-5
ATT_EPS = 1e-6
RMS_EPS = 1e-6
ACT_EPS = 1e-5
MU = math.sqrt(0.5)
STD = math.sqrt(0.25 * math.pi)
FGROUPS = [2] * 11
BIG = 65536.0


class Ctx:

    def __init__(self, nc, sems, mksem):
        self.nc = nc
        self.eng = {"T": nc.tensor, "V": nc.vector, "S": nc.scalar,
                    "G": nc.gpsimd, "Q": nc.sync}
        self.sems = sems
        self.mksem = mksem
        self.count = {k: 0 for k in "TVSGQ"}
        self.ilog = {k: [] for k in "TVSGQ"}
        self.incs = {k: [] for k in "TVSGQ"}
        self.semval = {k: 0 for k in "TVSGQ"}
        self.dmaval = {}
        self.wm = {k: {} for k in "TVSGQ"}
        self.lastc = {k: None for k in "TVSGQ"}
        self.bufs = {}
        self.noburst = set()
        self.sticky = {}

    def _prod_sem(self, prod):
        kind, idx = prod
        if kind.startswith("D:"):
            return (kind, idx)
        for (i2, v2) in self.incs[kind]:
            if i2 >= idx:
                return (kind, v2)
        v = self.semval[kind] + 1
        self.semval[kind] = v
        self.ilog[kind][idx].then_inc(self.sems[kind], 1)
        self.incs[kind].append((idx, v))
        return (kind, v)

    def _wait(self, ename, semkey, val):
        if self.wm[ename].get(semkey, 0) >= val:
            return
        self.eng[ename].wait_ge(self.sems[semkey], val)
        self.wm[ename][semkey] = val

    def _collect(self, ename, reads, writes, needed):
        for b in reads:
            s = self.bufs.get(b)
            if s and s["w"] is not None:
                p = s["w"]
                if ename == "T" and p[0] == "T":
                    continue
                k, v = self._prod_sem(p)
                needed[k] = max(needed.get(k, 0), v)
        for b in writes:
            s = self.bufs.get(b)
            if s is None:
                continue
            deps = list(s["r"])
            if s["w"] is not None:
                deps.append(s["w"])
            for p in deps:
                if ename == "T" and p[0] == "T":
                    continue
                k, v = self._prod_sem(p)
                needed[k] = max(needed.get(k, 0), v)

    def emit(self, ename, fn, reads=(), writes=(), accum=False):
        needed = {}
        self._collect(ename, reads, () if accum else writes, needed)
        for k in sorted(needed):
            self._wait(ename, k, needed[k])
        instr = fn(self.eng[ename])
        idx = self.count[ename]
        self.count[ename] += 1
        self.ilog[ename].append(instr)
        self.lastc[ename] = idx
        prod = (ename, idx)
        for b in writes:
            if accum and b in self.bufs:
                self.bufs[b]["w"] = prod
            else:
                self.bufs[b] = {"w": prod, "r": []}
        for b in reads:
            self.bufs.setdefault(b, {"w": None, "r": []})["r"].append(prod)
        return instr

    def dma(self, qname, out, in_, reads=(), writes=(), slot=None):
        if slot is None:
            slot = writes[0] if writes else "store"
        semkey = "D:" + slot
        if semkey not in self.sems:
            self.sems[semkey] = self.mksem(slot)
            self.dmaval[semkey] = 0
        needed = {}
        self._collect(qname, reads, writes, needed)
        for k in sorted(needed):
            self._wait(qname, k, needed[k])
        instr = self.eng[qname].dma_start(out=out, in_=in_)
        idx = self.count[qname]
        self.count[qname] += 1
        self.ilog[qname].append(instr)
        self.dmaval[semkey] += 16
        v = self.dmaval[semkey]
        instr.then_inc(self.sems[semkey], 16)
        prod = (semkey, v)
        for b in writes:
            self.bufs[b] = {"w": prod, "r": []}
        for b in reads:
            self.bufs.setdefault(b, {"w": None, "r": []})["r"].append(prod)
        return instr

    def barrier(self):
        for k in "TVSGQ":
            if self.lastc[k] is not None:
                self._prod_sem((k, self.lastc[k]))
        targets = [(k, self.semval[k]) for k in "TVSGQ" if self.semval[k] > 0]
        targets += [(k, v) for k, v in self.dmaval.items()
                    if v > 0 and k not in self.noburst]
        for ename in "TVSGQ":
            for k, v in targets:
                self._wait(ename, k, v)
        self.bufs = {}
        for k, p in self.sticky.items():
            self.bufs[k] = {"w": p, "r": []}


def weave(gens):
    gens = [g for g in gens if g is not None]
    while gens:
        alive = []
        for g in gens:
            try:
                next(g)
                alive.append(g)
            except StopIteration:
                continue
        gens = alive


def c_hi_lo(x):
    hi = np.float32(x)
    lo = np.float32(x - float(hi))
    return float(hi), float(lo)


def build_program(nc, T):
    TC = T // 128
    NIT = T // 512
    NTB = T // 512
    S = T
    NBE = (S + 768) // 128

    dt = lambda name, shape, dty, kind: nc.dram_tensor(name, shape, dty, kind=kind).ap()
    x_in = dt("x_in", [T, D], F32, "ExternalInput")
    Wqkv = dt("L0_Wqkv", [D, 3 * D], F32, "ExternalInput")
    Wout = dt("L0_Wout", [D, D], F32, "ExternalInput")
    bout = dt("L0_bout", [D], F32, "ExternalInput")
    router_w = dt("L0_router", [D, E], F32, "ExternalInput")
    gate_p = dt("L0_gate", [E, D, FF], F32, "ExternalInput")
    up_p = dt("L0_up", [E, D, FF], F32, "ExternalInput")
    down_p = dt("L0_down", [E, FF, D], F32, "ExternalInput")
    wqk = dt("L1_wqk", [D, DQK], F32, "ExternalInput")
    bqk = dt("L1_bqk", [DQK], F32, "ExternalInput")
    gamma = dt("L1_gamma", [2, DQK], F32, "ExternalInput")
    beta = dt("L1_beta", [2, DQK], F32, "ExternalInput")
    wv = dt("L1_wv", [D, DV], F32, "ExternalInput")
    bv = dt("L1_bv", [DV], F32, "ExternalInput")
    rel_bias = dt("L1_rel", [NB], F32, "ExternalInput")
    w_int8 = dt("L1_w8", [D, D], I32, "ExternalInput")
    wscale = dt("L1_ws", [1], F32, "ExternalInput")
    bias2 = dt("L1_bias", [D], F32, "ExternalInput")
    ident = dt("c_ident", [128, 128], F32, "ExternalInput")
    jrev = dt("c_jrev", [128, 128], F32, "ExternalInput")
    ones1 = dt("c_ones1", [1, 128], F32, "ExternalInput")
    onesk = dt("c_onesk", [128, 1], BF16, "ExternalInput")
    iota8 = dt("c_iota8", [128, E], F32, "ExternalInput")
    iotaB = dt("c_iotaB", [128, E], F32, "ExternalInput")
    sel2 = dt("c_sel2", [2, 128], BF16, "ExternalInput")
    sel16 = dt("c_sel16", [16, 8 * 128], BF16, "ExternalInput")
    cmasks = dt("c_masks", [128, 4 * 512], BF16, "ExternalInput")
    ohmat = dt("c_oh", [NB, S + 768], F32, "ExternalInput")
    x_out = dt("x_out", [T, D], F32, "ExternalOutput")
    bias_ext = dt("s_bias", [S + 768], F32, "Internal")
    gcache = dt("s_gcache", [E, D, FF], BF16, "Internal")
    ucache = dt("s_ucache", [E, D, FF], BF16, "Internal")
    dcache = dt("s_dcache", [E, FF, D], BF16, "Internal")
    qkvcache = dt("s_qkvc", [D, 3 * D], BF16, "Internal")
    ocache = dt("s_oc", [D, D], BF16, "Internal")
    vcache = dt("s_vc", [D, DV], BF16, "Internal")
    qkcache = dt("s_qkc", [D, DQK], BF16, "Internal")
    rcache = dt("s_rc", [D, E], BF16, "Internal")

    with ExitStack() as stk:
        sems = {}
        for k in ["T", "V", "S", "G"]:
            sems[k] = stk.enter_context(nc.semaphore("sem_" + k))
        mksem = lambda slot: stk.enter_context(nc.semaphore("dsem_" + slot.replace(":", "_")))
        ctx = Ctx(nc, sems, mksem)

        def SB(st, name, shape, dty):
            return st.enter_context(nc.sbuf_tensor(name, shape, dty))

        x = SB(stk, "x", [128, TC * 1024], F32)
        xnT_stk = stk.enter_context(ExitStack())
        xnT = xnT_stk.enter_context(nc.sbuf_tensor("xnT", [128, DC * T], BF16,
                                                   side="right"))
        s_tok = SB(stk, "s_tok", [128, TC * 1024], BF16)
        m_st = SB(stk, "m_st", [128, TC], F32)
        r_st = SB(stk, "r_st", [128, TC], F32)
        ew = SB(stk, "ew", [128, TC * E], F32)
        c_id = SB(stk, "c_id", [128, 128], F32)
        c_o1 = SB(stk, "c_o1", [1, 128], F32)
        c_ok = SB(stk, "c_ok", [128, 1], BF16)
        xn_f = SB(stk, "xn_f", [128, 1024], F32)
        lndump = SB(stk, "lndump", [128, 1024], F32)
        ssum = SB(stk, "ssum", [128, 1], F32)
        ssq = SB(stk, "ssq", [128, 1], F32)
        varv = SB(stk, "varv", [128, 1], F32)
        P = [stk.enter_context(nc.psum_tensor("P%d" % i, [128, 1024], F32))
             for i in range(4)]
        PKa = lambda i, h: "P%d:%d" % (i, h)
        PKf = lambda i: ["P%d:0" % i, "P%d:1" % i]
        PSL = lambda i, s: "P%d:0s%d" % (i, s)
        PSLall = lambda i: [PSL(i, s) for s in range(4)]

        blk = stk.enter_context(nc.Block())

        ctx.dma("Q", c_id[:], ident, writes=["c_id"])
        ctx.dma("Q", c_o1[:], ones1, writes=["c_o1"])
        ctx.dma("G", c_ok[:], onesk, writes=["c_ok"])

        ctx.dma("Q", x[:].rearrange("p (tc d) -> p tc d", d=1024),
                x_in.rearrange("(tc p) d -> p tc d", p=128),
                writes=["x%d" % t for t in range(TC)], slot="xld")
        ctx._wait("G", "D:xld", 16)
        ctx.dma("G", rcache, router_w, writes=["rcache"], slot="rcw")
        ctx.dma("G", qkvcache, Wqkv, writes=["qkvcache"], slot="qkvcw")
        ctx.dma("G", ocache, Wout, writes=["ocache"], slot="ocw")
        ctx.dma("G", qkcache, wqk, writes=["qkcache"], slot="qkcw")
        ctx.dma("G", vcache, wv, writes=["vcache"], slot="vcw")
        ctx.sems["MP"] = ctx.mksem("MP")

        def bcast_row(st, name, row_ap, n, pi):
            t = SB(st, name, [128, n], F32)
            bcast_row_into(t, st, name, row_ap, n, pi)
            return t

        def bcast_row_into(t, st, name, row_ap, n, pi):
            rrow = SB(st, name + "_r", [1, n], F32)
            ctx.dma("Q", rrow[:], row_ap, writes=[name + "_r"])
            for h in range(0, n, 512):
                w = min(512, n - h)
                pb = PKa(pi, (h % 1024) // 512)
                ctx.emit("T", lambda t_: t_.matmul(
                    out=P[pi][0:128, h % 1024:h % 1024 + w], lhsT=c_o1[:],
                    rhs=rrow[0:1, h:h + w], start=True, stop=True),
                    reads=["c_o1", name + "_r"], writes=[pb])
                ctx.emit("V", lambda v: v.tensor_copy(
                    out=t[:, h:h + w], in_=P[pi][0:128, h % 1024:h % 1024 + w]),
                    reads=[pb], writes=[name])
            return t

        var16 = SB(stk, "var16", [128, TC], F32)
        stot_a = SB(stk, "stot_a", [128, TC], F32)
        cq2_a = SB(stk, "cq2_a", [128, TC], F32)
        amx_a = SB(stk, "amx_a", [128, TC], F32)
        ct_a = SB(stk, "ct_a", [128, TC], F32)
        wsb2 = SB(stk, "wsb2", [128, 1], F32)

        def ln_stats():
            ctx.emit("V", lambda v: v.tensor_reduce(
                out=m_st[:], in_=x[:].rearrange("p (tc d) -> p tc d", d=1024),
                axis=AX.X, op=OP.add),
                reads=["x%d" % t for t in range(TC)], writes=["m_st"])
            for tc in range(TC):
                ctx.emit("S", lambda s: s.activation(
                    out=lndump[:], in_=x[:, tc * 1024:(tc + 1) * 1024],
                    func=AF.Square, accum_out=r_st[:, tc:tc + 1]),
                    reads=["x%d" % tc], writes=["lndump", "r_st"])
            ctx.emit("V", lambda v: v.tensor_scalar(
                out=m_st[:], in0=m_st[:], scalar1=1.0 / 1024.0, scalar2=None,
                op0=OP.mult), reads=["m_st"], writes=["m_st"])
            ctx.emit("V", lambda v: v.tensor_scalar(
                out=r_st[:], in0=r_st[:], scalar1=1.0 / 1024.0, scalar2=None,
                op0=OP.mult), reads=["r_st"], writes=["r_st"])
            ctx.emit("V", lambda v: v.tensor_tensor(
                out=var16[:], in0=m_st[:], in1=m_st[:], op=OP.mult),
                reads=["m_st"], writes=["var16"])
            ctx.emit("V", lambda v: v.tensor_tensor(
                out=var16[:], in0=r_st[:], in1=var16[:], op=OP.subtract),
                reads=["r_st", "var16"], writes=["var16"])
            ctx.emit("V", lambda v: v.tensor_scalar(
                out=var16[:], in0=var16[:], scalar1=1.0, scalar2=LN_EPS,
                op0=OP.mult, op1=OP.add), reads=["var16"], writes=["var16"])
            ctx.emit("V", lambda v: v.reciprocal(out=var16[:], in_=var16[:]),
                     reads=["var16"], writes=["var16"])
            ctx.emit("S", lambda s: s.activation(
                out=r_st[:], in_=var16[:], func=AF.Sqrt),
                reads=["var16"], writes=["r_st"])

        def ln_xn_gen(tc, rb=None, sc_all=None, sc_key=None, pingpong=False):
            xs = x[:, tc * 1024:(tc + 1) * 1024]
            ctx.emit("V", lambda v: v.tensor_scalar(
                out=xn_f[:], in0=xs, scalar1=m_st[:, tc:tc + 1],
                scalar2=r_st[:, tc:tc + 1], op0=OP.subtract, op1=OP.mult),
                reads=["x%d" % tc, "m_st", "r_st"], writes=["xn_f"])
            yield
            for dc in range(DC):
                if pingpong:
                    bk = dc % 2
                    pcol = bk * 512 + (dc % 4) // 2 * 128
                else:
                    bk = 0
                    pcol = (dc % 4) * 128
                ctx.emit("T", lambda t: t.transpose(
                    out=P[2][:, pcol:pcol + 128],
                    in_=xn_f[:, dc * 128:(dc + 1) * 128], identity=c_id[:]),
                    reads=["xn_f", "c_id"], writes=[PKa(2, bk)])
                yield
                ctx.emit("S", lambda s: s.copy(
                    out=xnT[:, dc * T + tc * 128: dc * T + (tc + 1) * 128],
                    in_=P[2][:, pcol:pcol + 128]),
                    reads=[PKa(2, bk)], writes=["xnT"])
                yield
                if rb is not None:
                    ctx.emit("T", lambda t: t.matmul(
                        out=P[2][0:128, 512:512 + E],
                        lhsT=xnT[:, dc * T + tc * 128: dc * T + (tc + 1) * 128],
                        rhs=rb[:, dc * E:(dc + 1) * E],
                        start=(dc == 0), stop=(dc == DC - 1)),
                        reads=["xnT", "rb"], writes=[PKa(2, 1)], accum=(dc > 0))
                    yield
            if rb is not None:
                ctx.emit("V", lambda v: v.tensor_copy(
                    out=sc_all[:, tc * E:(tc + 1) * E],
                    in_=P[2][0:128, 512:512 + E]),
                    reads=[PKa(2, 1)], writes=[sc_key])
                yield

        def ln_xn(tc, rb=None, sc_all=None, sc_key=None):
            for _ in ln_xn_gen(tc, rb, sc_all, sc_key):
                pass

        if True:
            st0 = stk
            rb = SB(st0, "rb", [128, DC * E], BF16)
            sc_all = SB(st0, "sc_all", [128, TC * E], F32)
            ctx.dma("Q", rb[:].rearrange("p (dc e) -> p dc e", e=E),
                    rcache.rearrange("(dc p) e -> p dc e", p=128),
                    reads=["rcache"], writes=["rb"])
            m1n = SB(st0, "m1n", [128, 1], F32)
            m2n = SB(st0, "m2n", [128, 1], F32)
            eqt = SB(st0, "eqt", [128, E], F32)
            at = SB(st0, "at", [128, E], F32)
            i1t = SB(st0, "i1t", [128, 1], F32)
            mk1 = SB(st0, "mk1", [128, E], F32)
            mk2 = SB(st0, "mk2", [128, E], F32)
            smk = SB(st0, "smk", [128, E], F32)
            e2t = SB(st0, "e2t", [128, 1], F32)
            w1t = SB(st0, "w1t", [128, 1], F32)
            w2t = SB(st0, "w2t", [128, 1], F32)
            ewt = SB(st0, "ewt", [128, E], F32)
            c_i8 = SB(st0, "c_i8", [128, E], F32)
            c_iB = SB(st0, "c_iB", [128, E], F32)
            ctx.dma("Q", c_i8[:], iota8, writes=["c_i8"])
            ctx.dma("Q", c_iB[:], iotaB, writes=["c_iB"])

            def topk_tc(tc):
                scs = sc_all[:, tc * E:(tc + 1) * E]
                ctx.emit("V", lambda v: v.tensor_reduce(
                    out=m1n[:], in_=scs, axis=AX.X, op=OP.max, negate=True),
                    reads=["sc_all"], writes=["m1n"])
                ctx.emit("V", lambda v: v.tensor_scalar(
                    out=eqt[:], in0=scs, scalar1=m1n[:], scalar2=0.0,
                    op0=OP.add, op1=OP.is_equal),
                    reads=["sc_all", "m1n"], writes=["eqt"])
                ctx.emit("V", lambda v: v.scalar_tensor_tensor(
                    out=at[:], in0=eqt[:], scalar=-BIG, in1=c_iB[:],
                    op0=OP.mult, op1=OP.add), reads=["eqt", "c_iB"], writes=["at"])
                ctx.emit("V", lambda v: v.tensor_reduce(
                    out=i1t[:], in_=at[:], axis=AX.X, op=OP.min),
                    reads=["at"], writes=["i1t"])
                ctx.emit("V", lambda v: v.tensor_scalar(
                    out=mk1[:], in0=c_i8[:], scalar1=i1t[:], scalar2=0.0,
                    op0=OP.subtract, op1=OP.is_equal),
                    reads=["c_i8", "i1t"], writes=["mk1"])
                ctx.emit("V", lambda v: v.scalar_tensor_tensor(
                    out=smk[:], in0=mk1[:], scalar=-BIG, in1=scs,
                    op0=OP.mult, op1=OP.add), reads=["mk1", "sc_all"], writes=["smk"])
                ctx.emit("V", lambda v: v.tensor_reduce(
                    out=m2n[:], in_=smk[:], axis=AX.X, op=OP.max, negate=True),
                    reads=["smk"], writes=["m2n"])
                ctx.emit("V", lambda v: v.tensor_scalar(
                    out=eqt[:], in0=smk[:], scalar1=m2n[:], scalar2=0.0,
                    op0=OP.add, op1=OP.is_equal), reads=["smk", "m2n"], writes=["eqt"])
                ctx.emit("V", lambda v: v.scalar_tensor_tensor(
                    out=at[:], in0=eqt[:], scalar=-BIG, in1=c_iB[:],
                    op0=OP.mult, op1=OP.add), reads=["eqt", "c_iB"], writes=["at"])
                ctx.emit("V", lambda v: v.tensor_reduce(
                    out=i1t[:], in_=at[:], axis=AX.X, op=OP.min),
                    reads=["at"], writes=["i1t"])
                ctx.emit("V", lambda v: v.tensor_scalar(
                    out=mk2[:], in0=c_i8[:], scalar1=i1t[:], scalar2=0.0,
                    op0=OP.subtract, op1=OP.is_equal),
                    reads=["c_i8", "i1t"], writes=["mk2"])
                ctx.emit("V", lambda v: v.tensor_scalar(
                    out=e2t[:], in0=m1n[:], scalar1=m2n[:], scalar2=None,
                    op0=OP.subtract), reads=["m1n", "m2n"], writes=["e2t"])
                ctx.emit("S", lambda s: s.activation(
                    out=e2t[:], in_=e2t[:], func=AF.Exp),
                    reads=["e2t"], writes=["e2t"])
                ctx.emit("V", lambda v: v.tensor_scalar(
                    out=w1t[:], in0=e2t[:], scalar1=1.0, scalar2=None, op0=OP.add),
                    reads=["e2t"], writes=["w1t"])
                ctx.emit("V", lambda v: v.reciprocal(out=w1t[:], in_=w1t[:]),
                         reads=["w1t"], writes=["w1t"])
                ctx.emit("V", lambda v: v.tensor_scalar(
                    out=w2t[:], in0=e2t[:], scalar1=w1t[:], scalar2=None,
                    op0=OP.mult), reads=["e2t", "w1t"], writes=["w2t"])
                ctx.emit("V", lambda v: v.tensor_scalar(
                    out=ewt[:], in0=mk2[:], scalar1=w2t[:], scalar2=None,
                    op0=OP.mult), reads=["mk2", "w2t"], writes=["ewt"])
                ctx.emit("V", lambda v: v.scalar_tensor_tensor(
                    out=ew[:, tc * E:(tc + 1) * E], in0=mk1[:], scalar=w1t[:],
                    in1=ewt[:], op0=OP.mult, op1=OP.add),
                    reads=["mk1", "w1t", "ewt"], writes=["ew"])


        stAB = stk.enter_context(ExitStack())
        kvb = SB(stAB, "kvb", [128, 8 * 128], BF16)
        scb = SB(stAB, "scb", [128, 16], BF16)
        with ExitStack() as stA:
            wkv = SB(stA, "wkv", [128, DC * 2048], BF16)
            ctx.dma("Q", wkv[:].rearrange("p (dc f) -> p dc f", f=2048),
                    qkvcache.rearrange("(dc p) f -> p dc f", p=128)[:, :, 1024:3072],
                    reads=["qkvcache"], writes=["wkv"])
            ctx._wait("G", "D:wkv", 16)
            for e_ in range(E):
                ctx.dma("G", gcache[e_], gate_p[e_], writes=["gcache%d" % e_],
                        slot="gcw%d" % e_)
                ctx.dma("G", ucache[e_], up_p[e_], writes=["ucache%d" % e_],
                        slot="ucw%d" % e_)
                for kk in ("gcache%d" % e_, "ucache%d" % e_):
                    ctx.sticky[kk] = ctx.bufs[kk]["w"]
                ctx.noburst.update({"D:gcw%d" % e_, "D:ucw%d" % e_})
            k_bfs = [SB(stA, "k_bf%d" % i, [128, 1024], BF16) for i in range(2)]
            v_bfs = [SB(stA, "v_bf%d" % i, [128, 1024], BF16) for i in range(2)]
            eltmps = [SB(stA, "eltmp%d" % i, [128, 512], F32) for i in range(2)]
            eltmp2s = [SB(stA, "eltmp2%d" % i, [128, 512], F32) for i in range(2)]
            ksum = SB(stA, "ksum", [1, 1024], F32)
            ctx.emit("V", lambda v: v.memset(ksum[:], 0.0), writes=["ksum"])

            def projkv_gen(tc):
                k_bf, k_bfk = k_bfs[tc % 2], "k_bf%d" % (tc % 2)
                v_bf, v_bfk = v_bfs[tc % 2], "v_bf%d" % (tc % 2)
                for part, dst, dstk in ((0, k_bf, k_bfk), (1, v_bf, v_bfk)):
                    for h in range(2):
                        pt, ptk = P[part], PKa(part, h)
                        pcol = h * 512
                        eltmp, eltmp_k = eltmps[h], "eltmp%d" % h
                        eltmp2, eltmp2_k = eltmp2s[h], "eltmp2%d" % h
                        for dc in range(DC):
                            ctx.emit("T", lambda t: t.matmul(
                                out=pt[:, pcol:pcol + 512],
                                lhsT=xnT[:, dc * T + tc * 128: dc * T + (tc + 1) * 128],
                                rhs=wkv[:, dc * 2048 + part * 1024 + pcol:
                                        dc * 2048 + part * 1024 + pcol + 512],
                                start=(dc == 0), stop=(dc == DC - 1)),
                                reads=["xnT", "wkv"], writes=[ptk], accum=(dc > 0))
                            yield
                        if part == 0:
                            ctx.emit("V", lambda v: v.tensor_scalar(
                                out=eltmp[:], in0=pt[:, pcol:pcol + 512],
                                scalar1=0.0, scalar2=None, op0=OP.min),
                                reads=[ptk], writes=[eltmp_k])
                            yield
                            ctx.emit("S", lambda s: s.activation(
                                out=eltmp2[:], in_=eltmp[:], func=AF.Exp),
                                reads=[eltmp_k], writes=[eltmp2_k])
                            yield
                            ctx.emit("V", lambda v: v.scalar_tensor_tensor(
                                out=dst[:, pcol:pcol + 512],
                                in0=pt[:, pcol:pcol + 512], scalar=0.0,
                                in1=eltmp2[:], op0=OP.max, op1=OP.add),
                                reads=[ptk, eltmp2_k], writes=[dstk])
                            yield
                        else:
                            ctx.emit("S", lambda s: s.copy(
                                out=dst[:, pcol:pcol + 512],
                                in_=pt[:, pcol:pcol + 512]),
                                reads=[ptk], writes=[dstk])
                            yield
                for c in range(8):
                    st_ = (tc == 0 and c in (0, 4))
                    sp_ = (tc == TC - 1 and c in (3, 7))
                    ctx.emit("T", lambda t: t.matmul(
                        out=P[3][:, c * 128:(c + 1) * 128],
                        lhsT=k_bf[:, c * 128:(c + 1) * 128],
                        rhs=v_bf[:, c * 128:(c + 1) * 128],
                        start=st_, stop=sp_),
                        reads=[k_bfk, v_bfk], writes=PKf(3),
                        accum=not (tc == 0 and c == 0))
                    yield
                for h in range(2):
                    ctx.emit("T", lambda t: t.matmul(
                        out=P[2][0:1, 0:512], lhsT=c_ok[:],
                        rhs=k_bf[:, h * 512:(h + 1) * 512], start=True, stop=True),
                        reads=["c_ok", k_bfk], writes=[PKa(2, 0)])
                    yield
                    ctx.emit("V", lambda v: v.scalar_tensor_tensor(
                        out=ksum[0:1, h * 512:(h + 1) * 512],
                        in0=P[2][0:1, 0:512], scalar=1.0,
                        in1=ksum[0:1, h * 512:(h + 1) * 512],
                        op0=OP.mult, op1=OP.add),
                        reads=[PKa(2, 0), "ksum"], writes=["ksum"])
                    yield
                topk_tc(tc)

            ln_stats()
            ln_xn(0, rb=rb, sc_all=sc_all, sc_key="sc_all")
            for tc in range(TC):
                nxt = (ln_xn_gen(tc + 1, rb, sc_all, "sc_all")
                       if tc + 1 < TC else None)
                weave([nxt, projkv_gen(tc)])
            ctx.emit("V", lambda v: v.memset(kvb[:], 0.0), writes=["kvb"])
            for c in range(8):
                ctx.emit("V", lambda v: v.tensor_copy(
                    out=kvb[0:64, c * 128:c * 128 + 64],
                    in_=P[3][0:64, c * 128:c * 128 + 64]),
                    reads=[PKa(3, c // 4)], writes=["kvb"])
                ctx.emit("V", lambda v: v.tensor_copy(
                    out=kvb[64:128, c * 128 + 64:c * 128 + 128],
                    in_=P[3][64:128, c * 128 + 64:c * 128 + 128]),
                    reads=[PKa(3, c // 4)], writes=["kvb"])
            ksb = SB(stA, "ksb", [1, 1024], BF16)
            ctx.emit("V", lambda v: v.tensor_copy(out=ksb[:], in_=ksum[:]),
                     reads=["ksum"], writes=["ksb"])
            ctx.emit("V", lambda v: v.memset(scb[:], 0.0), writes=["scb"])
            for c in range(8):
                ctx.dma("Q", scb[0:64, 2 * c:2 * c + 1],
                        ksb[0:1, c * 128:c * 128 + 64],
                        reads=["ksb"], writes=["scb"])
                ctx.dma("Q", scb[64:128, 2 * c + 1:2 * c + 2],
                        ksb[0:1, c * 128 + 64:c * 128 + 128],
                        reads=["ksb"], writes=["scb"])
            ctx.barrier()
            _mp = ctx.emit("V", lambda v: v.nop())
            _mp.then_inc(ctx.sems["MP"], 1)

        with ExitStack() as stB:
            c_s2 = SB(stB, "c_s2", [2, 128], BF16)
            ctx.dma("Q", c_s2[:], sel2, writes=["c_s2"])
            wqo = SB(stB, "wqo", [128, DC * 1024], BF16)
            bout_b = bcast_row(stB, "bout_b", bout.rearrange("(o d) -> o d", o=1),
                               D, 2)
            q_f = SB(stB, "q_f", [128, 1024], F32)
            qT = SB(stB, "qT", [128, DC * 512], BF16)
            att_fm = SB(stB, "att_fm", [128, 8 * 512], BF16)
            r2ab = [SB(stB, "r2ab%d" % i, [2, 512], BF16) for i in range(2)]
            r2af = [SB(stB, "r2af%d" % i, [2, 512], F32) for i in range(2)]
            r2ts = [SB(stB, "r2t%d" % i, [128, 512], F32) for i in range(2)]
            eltmp = SB(stB, "eltmpB", [128, 512], F32)
            eltmp2 = SB(stB, "eltmpB2", [128, 512], F32)
            for it in range(NIT):
                ctx.dma("Q", wqo[:].rearrange("p (dc f) -> p dc f", f=1024),
                        qkvcache.rearrange("(dc p) f -> p dc f", p=128)[:, :, 0:1024],
                        reads=["qkvcache"], writes=["wqo"])
                for lc in range(4):
                    tc = it * 4 + lc
                    for h in range(2):
                        for dc in range(DC):
                            ctx.emit("T", lambda t: t.matmul(
                                out=P[h][:, 0:512],
                                lhsT=xnT[:, dc * T + tc * 128: dc * T + (tc + 1) * 128],
                                rhs=wqo[:, dc * 1024 + h * 512: dc * 1024 + h * 512 + 512],
                                start=(dc == 0), stop=(dc == DC - 1)),
                                reads=["xnT", "wqo"], writes=[PKa(h, 0)], accum=(dc > 0))
                        ctx.emit("V", lambda v: v.tensor_scalar(
                            out=eltmp[:], in0=P[h][:, 0:512], scalar1=0.0,
                            scalar2=None, op0=OP.min), reads=[PKa(h, 0)], writes=["eltmpB"])
                        ctx.emit("S", lambda s: s.activation(
                            out=eltmp2[:], in_=eltmp[:], func=AF.Exp),
                            reads=["eltmpB"], writes=["eltmpB2"])
                        ctx.emit("V", lambda v: v.scalar_tensor_tensor(
                            out=q_f[:, h * 512:(h + 1) * 512], in0=P[h][:, 0:512],
                            scalar=0.0, in1=eltmp2[:], op0=OP.max, op1=OP.add),
                            reads=[PKa(h, 0), "eltmpB2"], writes=["q_f"])
                    for dc in range(DC):
                        bk = dc % 2
                        pcol = bk * 512 + ((dc % 4) // 2) * 128
                        ctx.emit("T", lambda t: t.transpose(
                            out=P[2][:, pcol:pcol + 128],
                            in_=q_f[:, dc * 128:(dc + 1) * 128], identity=c_id[:]),
                            reads=["q_f", "c_id"], writes=[PKa(2, bk)])
                        ctx.emit("S", lambda s: s.copy(
                            out=qT[:, dc * 512 + lc * 128: dc * 512 + (lc + 1) * 128],
                            in_=P[2][:, pcol:pcol + 128]),
                            reads=[PKa(2, bk)], writes=["qT"])
                ctx.dma("S", wqo[:].rearrange("p (fc f) -> p fc f", f=1024),
                        ocache.rearrange("(fc p) f -> p fc f", p=128),
                        reads=["ocache"], writes=["wqo"])
                for c in range(8):
                    cp = c % 2
                    pq, pqk = (P[2], PKa(2, 1)) if cp == 0 else (P[3], PKa(3, 1))
                    ctx.emit("T", lambda t: t.matmul(
                        out=pq[0:2, 512:1024], lhsT=scb[:, 2 * c:2 * c + 2],
                        rhs=qT[:, c * 512:(c + 1) * 512], start=True, stop=True),
                        reads=["scb", "qT"], writes=[pqk])
                    ctx.emit("V", lambda v: v.tensor_scalar(
                        out=r2af[cp][:], in0=pq[0:2, 512:1024], scalar1=ATT_EPS,
                        scalar2=None, op0=OP.add),
                        reads=[pqk], writes=["r2af%d" % cp])
                    ctx.emit("V", lambda v: v.reciprocal(
                        out=r2af[cp][:], in_=r2af[cp][:]),
                        reads=["r2af%d" % cp], writes=["r2af%d" % cp])
                    ctx.emit("V", lambda v: v.tensor_copy(
                        out=r2ab[cp][:], in_=r2af[cp][:]),
                        reads=["r2af%d" % cp], writes=["r2ab%d" % cp])
                    ctx.emit("T", lambda t: t.matmul(
                        out=P[1][:, cp * 512:cp * 512 + 512],
                        lhsT=c_s2[:], rhs=r2ab[cp][:],
                        start=True, stop=True),
                        reads=["c_s2", "r2ab%d" % cp], writes=[PKa(1, cp)])
                    ctx.emit("V", lambda v: v.tensor_copy(
                        out=r2ts[cp][:], in_=P[1][:, cp * 512:cp * 512 + 512]),
                        reads=[PKa(1, cp)], writes=["r2t%d" % cp])
                    ctx.emit("T", lambda t: t.matmul(
                        out=P[0][:, cp * 512:cp * 512 + 512],
                        lhsT=kvb[:, c * 128:(c + 1) * 128],
                        rhs=qT[:, c * 512:(c + 1) * 512], start=True, stop=True),
                        reads=["kvb", "qT"], writes=[PKa(0, cp)])
                    ctx.emit("V", lambda v: v.tensor_tensor(
                        out=att_fm[:, c * 512:(c + 1) * 512],
                        in0=P[0][:, cp * 512:cp * 512 + 512],
                        in1=r2ts[cp][:], op=OP.mult),
                        reads=[PKa(0, cp), "r2t%d" % cp], writes=["att_fm"])
                for lc in range(4):
                    tc = it * 4 + lc
                    for h in range(2):
                        for fc in range(DC):
                            ctx.emit("T", lambda t: t.matmul(
                                out=P[3][:, h * 512:(h + 1) * 512],
                                lhsT=att_fm[:, fc * 512 + lc * 128:
                                            fc * 512 + (lc + 1) * 128],
                                rhs=wqo[:, fc * 1024 + h * 512:
                                        fc * 1024 + h * 512 + 512],
                                start=(fc == 0), stop=(fc == DC - 1)),
                                reads=["att_fm", "wqo"], writes=[PKa(3, h)],
                                accum=(fc > 0))
                        ctx.emit("V", lambda v: v.scalar_tensor_tensor(
                            out=s_tok[:, tc * 1024 + h * 512:
                                      tc * 1024 + (h + 1) * 512],
                            in0=P[3][:, h * 512:(h + 1) * 512], scalar=1.0,
                            in1=bout_b[:, h * 512:(h + 1) * 512],
                            op0=OP.mult, op1=OP.add),
                            reads=[PKa(3, h), "bout_b"], writes=["s%d" % tc])
            ctx.barrier()
        stAB.close()
        _mp2 = ctx.emit("V", lambda v: v.nop())
        _mp2.then_inc(ctx.sems["MP"], 1)

        ctx._wait("G", "MP", 2)
        for e_ in range(E):
            ctx.dma("G", dcache[e_], down_p[e_], writes=["dcache%d" % e_],
                    slot="dcw%d" % e_)
            ctx.sticky["dcache%d" % e_] = ctx.bufs["dcache%d" % e_]["w"]
            ctx.noburst.add("D:dcw%d" % e_)

        with ExitStack() as stM:
            h_w = SB(stM, "h_w", [128, FC * 512], BF16)
            gacc = SB(stM, "gacc", [128, 4 * 1024], F32)
            gbufs = [SB(stM, "gbuf%d" % i, [128, DC * 256], BF16) for i in range(2)]
            ubufs = [SB(stM, "ubuf%d" % i, [128, DC * 256], BF16) for i in range(2)]
            dbufs = [SB(stM, "dbuf%d" % i, [128, 2 * 1024], BF16) for i in range(2)]
            gsils = [xn_f[:, 0:512], xn_f[:, 512:1024]]
            gtmp, gtmp_k = lndump, "lndump"
            GOFF = [2 * i for i in range(11)]
            DGROUPS = [2] * 11
            GOFFD = [2 * i for i in range(11)]

            units = []
            for tb in range(NTB):
                for e in range(E):
                    for g in range(len(FGROUPS)):
                        units.append(("gu", tb, e, g))
                    for g in range(len(DGROUPS)):
                        units.append(("dn", tb, e, g))
                units.append(("fin", tb, 0, 0))

            gu_ct = [0]
            dn_ct = [0]

            def issue_dma(u):
                kind, tb, e, g = u
                if kind == "fin":
                    return
                if kind == "gu":
                    ng = FGROUPS[g]
                    fc0 = GOFF[g]
                    w_ = ng * 128
                    sl = gu_ct[0] % 2
                    gu_ct[0] += 1
                    gb, ub = gbufs[sl], ubufs[sl]
                    gk, uk = "gbuf%d" % sl, "ubuf%d" % sl
                    ctx.dma("Q", gb[:, 0:DC * w_].rearrange(
                        "p (dc f) -> p dc f", f=w_),
                        gcache.rearrange("e (dc p) f -> e p dc f", p=128)
                        [e, :, :, fc0 * 128:(fc0 + ng) * 128],
                        reads=["gcache%d" % e], writes=[gk], slot=gk)
                    ctx.dma("S", ub[:, 0:DC * w_].rearrange(
                        "p (dc f) -> p dc f", f=w_),
                        ucache.rearrange("e (dc p) f -> e p dc f", p=128)
                        [e, :, :, fc0 * 128:(fc0 + ng) * 128],
                        reads=["ucache%d" % e], writes=[uk], slot=uk)
                else:
                    ng = DGROUPS[g]
                    fc0 = GOFFD[g]
                    sl = dn_ct[0] % 2
                    dn_ct[0] += 1
                    db = dbufs[sl]
                    dk = "dbuf%d" % sl
                    ctx.dma("S" if g % 2 else "Q",
                            db[:, 0:ng * 1024].rearrange(
                                "p (fc f) -> p fc f", f=1024),
                            dcache.rearrange("e (fc p) f -> e p fc f", p=128)
                            [e, :, fc0:fc0 + ng, :],
                            reads=["dcache%d" % e], writes=[dk], slot=dk)

            cg_ct = [0]
            cd_ct = [0]

            def emit_compute(u):
                kind, tb, e, g = u
                if kind == "fin":
                    for lc in range(4):
                        tc = tb * 4 + lc
                        ctx.emit("S", lambda s: s.activation(
                            out=gtmp[:], in_=gacc[:, lc * 1024:(lc + 1) * 1024],
                            func=AF.Silu), reads=["gacc"], writes=[gtmp_k])
                        ctx.emit("V", lambda v: v.tensor_tensor(
                            out=gtmp[:], in0=gtmp[:],
                            in1=s_tok[:, tc * 1024:(tc + 1) * 1024], op=OP.mult),
                            reads=[gtmp_k, "s%d" % tc], writes=[gtmp_k])
                        ctx.emit("V", lambda v: v.tensor_tensor(
                            out=x[:, tc * 1024:(tc + 1) * 1024],
                            in0=x[:, tc * 1024:(tc + 1) * 1024], in1=gtmp[:],
                            op=OP.add), reads=[gtmp_k, "x%d" % tc],
                            writes=["x%d" % tc])
                    return
                if kind == "gu":
                    if tb == 0 and g == 0:
                        _m = ctx.emit("V", lambda v: v.nop())
                        _m.then_inc(ctx.sems["MP"], 1)
                    ng = FGROUPS[g]
                    fc0 = GOFF[g]
                    w_ = ng * 128
                    sl = cg_ct[0] % 2
                    cg_ct[0] += 1
                    gb, ub = gbufs[sl], ubufs[sl]
                    gk, uk = "gbuf%d" % sl, "ubuf%d" % sl
                    for fl in range(ng):
                        fc = fc0 + fl
                        pcg = (fc % 2) * 512
                        gsb = gsils[fc % 2]
                        gsk = "xn_f:%d" % (fc % 2)
                        for dc in range(DC):
                            ctx.emit("T", lambda t: t.matmul(
                                out=P[0][:, pcg:pcg + 512],
                                lhsT=gb[:, dc * w_ + fl * 128: dc * w_ + (fl + 1) * 128],
                                rhs=xnT[:, dc * T + tb * 512: dc * T + (tb + 1) * 512],
                                start=(dc == 0), stop=(dc == DC - 1)),
                                reads=[gk, "xnT"], writes=[PKa(0, fc % 2)],
                                accum=(dc > 0))
                        ctx.emit("S", lambda s: s.activation(
                            out=gsb, in_=P[0][:, pcg:pcg + 512], func=AF.Silu),
                            reads=[PKa(0, fc % 2)], writes=[gsk])
                        for dc in range(DC):
                            ctx.emit("T", lambda t: t.matmul(
                                out=P[1][:, pcg:pcg + 512],
                                lhsT=ub[:, dc * w_ + fl * 128: dc * w_ + (fl + 1) * 128],
                                rhs=xnT[:, dc * T + tb * 512: dc * T + (tb + 1) * 512],
                                start=(dc == 0), stop=(dc == DC - 1)),
                                reads=[uk, "xnT"], writes=[PKa(1, fc % 2)],
                                accum=(dc > 0))
                        ctx.emit("V", lambda v: v.tensor_tensor(
                            out=h_w[:, fc * 512:(fc + 1) * 512],
                            in0=P[1][:, pcg:pcg + 512], in1=gsb, op=OP.mult),
                            reads=[PKa(1, fc % 2), gsk], writes=["h_w"])
                else:
                    ng = DGROUPS[g]
                    fc0 = GOFFD[g]
                    sl = cd_ct[0] % 2
                    cd_ct[0] += 1
                    db = dbufs[sl]
                    dk = "dbuf%d" % sl
                    for fl in range(ng):
                        fc = fc0 + fl
                        for lc in range(4):
                            for h in range(2):
                                ctx.emit("T", lambda t: t.matmul(
                                    out=P[lc][:, h * 512:(h + 1) * 512],
                                    lhsT=h_w[:, fc * 512 + lc * 128:
                                             fc * 512 + (lc + 1) * 128],
                                    rhs=db[:, fl * 1024 + h * 512:
                                           fl * 1024 + h * 512 + 512],
                                    start=(fc == 0), stop=(fc == FC - 1)),
                                    reads=["h_w", dk], writes=[PKa(lc, h)],
                                    accum=(fc > 0))
                    if g == len(DGROUPS) - 1:
                        for lc in range(4):
                            tc = tb * 4 + lc
                            wecol = ew[:, tc * E + e: tc * E + e + 1]
                            if e == 0:
                                ctx.emit("V", lambda v: v.tensor_scalar(
                                    out=gacc[:, lc * 1024:(lc + 1) * 1024],
                                    in0=P[lc][:], scalar1=wecol, scalar2=None,
                                    op0=OP.mult),
                                    reads=PKf(lc) + ["ew"], writes=["gacc"])
                            else:
                                ctx.emit("V", lambda v: v.scalar_tensor_tensor(
                                    out=gacc[:, lc * 1024:(lc + 1) * 1024],
                                    in0=P[lc][:], scalar=wecol,
                                    in1=gacc[:, lc * 1024:(lc + 1) * 1024],
                                    op0=OP.mult, op1=OP.add),
                                    reads=PKf(lc) + ["ew", "gacc"], writes=["gacc"])

            pend = 0
            for i in range(len(units)):
                while pend <= i + 1 and pend < len(units):
                    issue_dma(units[pend])
                    pend += 1
                emit_compute(units[i])
            ctx.barrier()

        c2f = math.sqrt(128.0) / STD
        c3f = MU / STD
        ln_stats()
        ctx.dma("Q", wsb2[:], RAP(tensor=wscale.tensor, offset=0,
                                  ap=[[0, 128], [1, 1]]), writes=["wsb2"])
        with ExitStack() as stb:
            c_ohT = SB(stb, "c_ohT", [NB, S + 768], F32)
            ctx.dma("Q", c_ohT[:], ohmat, writes=["c_ohT"])
            relb = SB(stb, "relb", [NB, 1], F32)
            ctx.dma("Q", relb[:], rel_bias.rearrange("(b o) -> b o", o=1),
                    writes=["relb"])
            btmp = SB(stb, "btmp", [128, 128], F32)
            bt2 = SB(stb, "bt2", [128, 128], F32)

            def biasext_gen():
                ctx.emit("V", lambda v: v.memset(btmp[:], 0.0), writes=["btmp"])
                yield
                for ch in range(NBE):
                    ctx.emit("T", lambda t: t.matmul(
                        out=P[1][:, 0:1], lhsT=c_ohT[:, ch * 128:(ch + 1) * 128],
                        rhs=relb[:], start=True, stop=True),
                        reads=["c_ohT", "relb"], writes=[PKa(1, 0)])
                    yield
                    ctx.emit("V", lambda v: v.tensor_scalar(
                        out=btmp[:, ch:ch + 1], in0=P[1][:, 0:1], scalar1=c2f,
                        scalar2=-c3f, op0=OP.mult, op1=OP.add),
                        reads=[PKa(1, 0)], writes=["btmp"])
                    yield
                ctx.emit("T", lambda t: t.transpose(
                    out=P[1][:, 0:128], in_=btmp[:], identity=c_id[:]),
                    reads=["btmp", "c_id"], writes=[PKa(1, 0)])
                yield
                ctx.emit("V", lambda v: v.tensor_copy(
                    out=bt2[:], in_=P[1][:, 0:128]), reads=[PKa(1, 0)], writes=["bt2"])
                yield
                ctx.dma("Q", bias_ext.rearrange("(c p) -> c p", p=128),
                        bt2[0:NBE, :], reads=["bt2"], writes=["bias_ext"])
                yield

            def qstats_gen():
                for tc in range(TC):
                    xs = x[:, tc * 1024:(tc + 1) * 1024]
                    ctx.emit("V", lambda v: v.tensor_scalar(
                        out=lndump[:], in0=xs, scalar1=m_st[:, tc:tc + 1],
                        scalar2=r_st[:, tc:tc + 1], op0=OP.subtract, op1=OP.mult),
                        reads=["x%d" % tc, "m_st", "r_st"], writes=["lndump"])
                    yield
                    ctx.emit("V", lambda v: v.tensor_reduce(
                        out=amx_a[:, tc:tc + 1], in_=lndump[:], axis=AX.X,
                        op=OP.max, apply_absolute_value=True),
                        reads=["lndump"], writes=["amx_a"])
                    yield
                    ctx.emit("S", lambda s: s.activation(
                        out=lndump[:], in_=lndump[:], func=AF.Square,
                        accum_out=stot_a[:, tc:tc + 1]),
                        reads=["lndump"], writes=["lndump", "stot_a"])
                    yield
                ctx.emit("V", lambda v: v.tensor_scalar(
                    out=stot_a[:], in0=stot_a[:], scalar1=1.0 / 1024.0,
                    scalar2=RMS_EPS, op0=OP.mult, op1=OP.add),
                    reads=["stot_a"], writes=["stot_a"])
                yield
                ctx.emit("V", lambda v: v.reciprocal(out=stot_a[:], in_=stot_a[:]),
                         reads=["stot_a"], writes=["stot_a"])
                yield
                ctx.emit("S", lambda s: s.activation(
                    out=stot_a[:], in_=stot_a[:], func=AF.Sqrt),
                    reads=["stot_a"], writes=["stot_a"])
                yield
                ctx.emit("V", lambda v: v.tensor_tensor(
                    out=ct_a[:], in0=amx_a[:], in1=stot_a[:], op=OP.mult),
                    reads=["amx_a", "stot_a"], writes=["ct_a"])
                yield
                ctx.emit("V", lambda v: v.tensor_scalar(
                    out=ct_a[:], in0=ct_a[:], scalar1=ACT_EPS, scalar2=None,
                    op0=OP.max), reads=["ct_a"], writes=["ct_a"])
                yield
                ctx.emit("V", lambda v: v.reciprocal(out=amx_a[:], in_=ct_a[:]),
                         reads=["ct_a"], writes=["amx_a"])
                yield
                ctx.emit("V", lambda v: v.tensor_tensor(
                    out=stot_a[:], in0=stot_a[:], in1=amx_a[:], op=OP.mult),
                    reads=["stot_a", "amx_a"], writes=["stot_a"])
                yield
                ctx.emit("V", lambda v: v.tensor_scalar(
                    out=stot_a[:], in0=stot_a[:], scalar1=127.0, scalar2=None,
                    op0=OP.mult), reads=["stot_a"], writes=["stot_a"])
                yield
                ctx.emit("V", lambda v: v.tensor_scalar(
                    out=cq2_a[:], in0=ct_a[:], scalar1=wsb2[:], scalar2=None,
                    op0=OP.mult), reads=["ct_a", "wsb2"], writes=["cq2_a"])
                yield
                ctx.emit("V", lambda v: v.tensor_scalar(
                    out=cq2_a[:], in0=cq2_a[:], scalar1=1.0 / 127.0, scalar2=None,
                    op0=OP.mult), reads=["cq2_a"], writes=["cq2_a"])
                yield
                ctx.emit("V", lambda v: v.tensor_tensor(
                    out=amx_a[:], in0=r_st[:], in1=stot_a[:], op=OP.mult),
                    reads=["r_st", "stot_a"], writes=["amx_a"])
                yield
                ctx.emit("V", lambda v: v.scalar_tensor_tensor(
                    out=ct_a[:], in0=m_st[:], scalar=-1.0, in1=amx_a[:],
                    op0=OP.mult, op1=OP.mult),
                    reads=["m_st", "amx_a"], writes=["ct_a"])
                yield

            def ln2_gen():
                for tc in range(TC):
                    for _ in ln_xn_gen(tc, pingpong=True):
                        yield

            weave([ln2_gen(), biasext_gen(), qstats_gen()])
            ctx.barrier()

        with ExitStack() as st1:

            bias2_b = SB(st1, "bias2_b", [128, D], F32)
            stAtt = st1.enter_context(ExitStack())
            qf = SB(stAtt, "qf", [128, T], BF16)
            kf = SB(stAtt, "kf", [128, T], BF16)
            with ExitStack() as stq:
                qk_f = SB(stq, "qk_f", [128, T], F32)
                wqkb = SB(stq, "wqkb", [128, DC * DQK], BF16)
                ctx.dma("Q", wqkb[:].rearrange("p (dc f) -> p dc f", f=DQK),
                        qkcache.rearrange("(dc p) f -> p dc f", p=128),
                        reads=["qkcache"], writes=["wqkb"])
                bqk_t = SB(stq, "bqk_t", [128, 1], F32)
                ctx.dma("Q", bqk_t[:], bqk.rearrange("(p o) -> p o", o=1),
                        writes=["bqk_t"])
                g0 = SB(stq, "g0", [128, 1], F32)
                g1 = SB(stq, "g1", [128, 1], F32)
                b0 = SB(stq, "b0", [128, 1], F32)
                b1 = SB(stq, "b1", [128, 1], F32)
                ctx.dma("Q", g0[:], RAP(tensor=gamma.tensor, offset=0,
                                        ap=[[1, 128], [1, 1]]), writes=["g0"])
                ctx.dma("Q", g1[:], RAP(tensor=gamma.tensor, offset=128,
                                        ap=[[1, 128], [1, 1]]), writes=["g1"])
                ctx.dma("Q", b0[:], RAP(tensor=beta.tensor, offset=0,
                                        ap=[[1, 128], [1, 1]]), writes=["b0"])
                ctx.dma("Q", b1[:], RAP(tensor=beta.tensor, offset=128,
                                        ap=[[1, 128], [1, 1]]), writes=["b1"])
                bcast_row_into(bias2_b, stq, "bias2_b",
                               bias2.rearrange("(o d) -> o d", o=1), D, 3)
                cqs = 1.0 / (float(S) * STD)
                ctx.emit("V", lambda v: v.tensor_scalar(
                    out=g0[:], in0=g0[:], scalar1=cqs, scalar2=None, op0=OP.mult),
                    reads=["g0"], writes=["g0"])
                ctx.emit("V", lambda v: v.tensor_scalar(
                    out=b0[:], in0=b0[:], scalar1=cqs, scalar2=None, op0=OP.mult),
                    reads=["b0"], writes=["b0"])
                for it in range(NIT):
                    for dc in range(DC):
                        ctx.emit("T", lambda t: t.matmul(
                            out=P[0][:, 0:512],
                            lhsT=wqkb[:, dc * DQK:(dc + 1) * DQK],
                            rhs=xnT[:, dc * T + it * 512: dc * T + (it + 1) * 512],
                            start=(dc == 0), stop=(dc == DC - 1)),
                            reads=["wqkb", "xnT"], writes=[PKa(0, 0)], accum=(dc > 0))
                    ctx.emit("S", lambda s: s.activation(
                        out=qk_f[:, it * 512:(it + 1) * 512], in_=P[0][:, 0:512],
                        func=AF.Silu, bias=bqk_t[:], scale=1.0),
                        reads=[PKa(0, 0), "bqk_t"], writes=["qk_f"])
                ctx.emit("V", lambda v: v.tensor_scalar(
                    out=qf[:], in0=qk_f[:], scalar1=g0[:], scalar2=b0[:],
                    op0=OP.mult, op1=OP.add),
                    reads=["qk_f", "g0", "b0"], writes=["qf"])
                ctx.emit("V", lambda v: v.tensor_scalar(
                    out=kf[:], in0=qk_f[:], scalar1=g1[:], scalar2=b1[:],
                    op0=OP.mult, op1=OP.add),
                    reads=["qk_f", "g1", "b1"], writes=["kf"])
                ctx.barrier()

            v_tok = SB(stAtt, "v_tok", [128, TC * 1024], BF16)
            with ExitStack() as stv:
                bv_b = bcast_row(stv, "bv_b", bv.rearrange("(o d) -> o d", o=1),
                                 DV, 2)
                wv_h = SB(stv, "wv_h", [128, DC * 512], BF16)
                vtmp = SB(stv, "vtmp", [128, 512], F32)
                for h in range(2):
                    ctx.dma("Q", wv_h[:].rearrange("p (dc f) -> p dc f", f=512),
                            vcache.rearrange("(dc p) f -> p dc f", p=128)
                            [:, :, h * 512:(h + 1) * 512],
                            reads=["vcache"], writes=["wv_h"])
                    for tc in range(TC):
                        for dc in range(DC):
                            ctx.emit("T", lambda t: t.matmul(
                                out=P[h][:, 0:512],
                                lhsT=xnT[:, dc * T + tc * 128: dc * T + (tc + 1) * 128],
                                rhs=wv_h[:, dc * 512:(dc + 1) * 512],
                                start=(dc == 0), stop=(dc == DC - 1)),
                                reads=["xnT", "wv_h"], writes=[PKa(h, 0)], accum=(dc > 0))
                        ctx.emit("V", lambda v: v.scalar_tensor_tensor(
                            out=vtmp[:], in0=P[h][:, 0:512], scalar=1.0,
                            in1=bv_b[:, h * 512:(h + 1) * 512],
                            op0=OP.mult, op1=OP.add),
                            reads=[PKa(h, 0), "bv_b"], writes=["vtmp"])
                        ctx.emit("S", lambda s: s.activation(
                            out=v_tok[:, tc * 1024 + h * 512:
                                      tc * 1024 + (h + 1) * 512],
                            in_=vtmp[:], func=AF.Silu),
                            reads=["vtmp"], writes=["v_tok"])
                ctx.barrier()

            xnT_stk.close()
            wTb = SB(st1, "wTb", [128, DC * 1024], BF16)

            def w8_gen():
                w8f, w8f_k = yq, "yq"
                for oc in range(DC):
                    ctx.dma("G", w8f[:],
                            w_int8.rearrange("(oc p) d -> oc p d", p=128)[oc],
                            writes=[w8f_k])
                    yield
                    for dc in range(DC):
                        bk = dc % 2
                        pcol = bk * 512 + ((dc % 4) // 2) * 128
                        ctx.emit("T", lambda t: t.transpose(
                            out=P[3][:, pcol:pcol + 128],
                            in_=w8f[:, dc * 128:(dc + 1) * 128], identity=c_id[:]),
                            reads=[w8f_k, "c_id"], writes=[PKa(3, bk)])
                        yield
                        ctx.emit("S", lambda s: s.copy(
                            out=wTb[:, dc * 1024 + oc * 128:
                                    dc * 1024 + (oc + 1) * 128],
                            in_=P[3][:, pcol:pcol + 128]),
                            reads=[PKa(3, bk)], writes=["wTb"])
                        yield

            yq = SB(st1, "yq", [128, 1024], F32)
            ti32 = SB(st1, "ti32", [128, 1024], I32)
            xqT = SB(st1, "xqT", [128, DC * 128], BF16)
            yq1 = SB(st1, "yq1", [128, 1024], F32)
            yq_s = [(yq, "yq"), (yq1, "yq1")]
            ti_s = [(ti32, "ti32"), (ti32, "ti32")]
            dif_s = [(xn_f, "xn_f"), (lndump, "lndump")]
            xqT_s = [(xqT, "xqT"), (xqT, "xqT")]
            ctx.barrier()

            def ffn_quant(tc):
                p = tc % 2
                yqb, yq_k = yq_s[p]
                tib, ti_k = ti_s[p]
                difb, dif_k = dif_s[p]
                ctx.emit("S", lambda s: s.activation(
                    out=yqb[:], in_=x[:, tc * 1024:(tc + 1) * 1024],
                    func=AF.Identity,
                    scale=amx_a[:, tc:tc + 1], bias=ct_a[:, tc:tc + 1]),
                    reads=["x%d" % tc, "amx_a", "ct_a"], writes=[yq_k])
                yield
                ctx.emit("S", lambda s: s.copy(out=tib[:], in_=yqb[:]),
                         reads=[yq_k], writes=[ti_k])
                yield
                ctx.emit("V", lambda v: v.scalar_tensor_tensor(
                    out=difb[:], in0=tib[:], scalar=-1.0, in1=yqb[:],
                    op0=OP.mult, op1=OP.add), reads=[ti_k, yq_k], writes=[dif_k])
                yield
                ctx.emit("V", lambda v: v.tensor_scalar(
                    out=yqb[:], in0=difb[:], scalar1=-0.5, scalar2=None,
                    op0=OP.is_le), reads=[dif_k], writes=[yq_k])
                yield
                ctx.emit("V", lambda v: v.tensor_scalar(
                    out=difb[:], in0=difb[:], scalar1=0.5, scalar2=None,
                    op0=OP.is_ge), reads=[dif_k], writes=[dif_k])
                yield
                ctx.emit("V", lambda v: v.scalar_tensor_tensor(
                    out=difb[:], in0=yqb[:], scalar=-1.0, in1=difb[:],
                    op0=OP.mult, op1=OP.add), reads=[yq_k, dif_k], writes=[dif_k])
                yield
                ctx.emit("V", lambda v: v.tensor_tensor(
                    out=yqb[:], in0=tib[:], in1=difb[:], op=OP.add),
                    reads=[ti_k, dif_k], writes=[yq_k])
                yield
                ctx.emit("V", lambda v: v.tensor_scalar(
                    out=yqb[:], in0=yqb[:], scalar1=-128.0, scalar2=127.0,
                    op0=OP.max, op1=OP.min), reads=[yq_k], writes=[yq_k])
                yield
                ctx.emit("V", lambda v: v.tensor_scalar(
                    out=yqb[:], in0=yqb[:], scalar1=cq2_a[:, tc:tc + 1],
                    scalar2=None, op0=OP.mult),
                    reads=[yq_k, "cq2_a"], writes=[yq_k])
                yield

            def ffn_mm(tc):
                p = tc % 2
                yqb, yq_k = yq_s[p]
                difb, dif_k = dif_s[p]
                xqTb, xqT_k = xqT_s[p]
                g2t = difb[:, 0:512]
                g2t_k = dif_k
                for dc in range(DC):
                    bk = dc % 2
                    pcol = bk * 512 + ((dc % 4) // 2) * 128
                    ctx.emit("T", lambda t: t.transpose(
                        out=P[3][:, pcol:pcol + 128],
                        in_=yqb[:, dc * 128:(dc + 1) * 128], identity=c_id[:]),
                        reads=[yq_k, "c_id"], writes=[PKa(3, bk)])
                    yield
                    ctx.emit("S", lambda s: s.copy(
                        out=xqTb[:, dc * 128:(dc + 1) * 128],
                        in_=P[3][:, pcol:pcol + 128]),
                        reads=[PKa(3, bk)], writes=[xqT_k])
                    yield
                for h in range(2):
                    for dc in range(DC):
                        ctx.emit("T", lambda t: t.matmul(
                            out=P[3][:, h * 512:(h + 1) * 512],
                            lhsT=xqTb[:, dc * 128:(dc + 1) * 128],
                            rhs=wTb[:, dc * 1024 + h * 512:
                                    dc * 1024 + h * 512 + 512],
                            start=(dc == 0), stop=(dc == DC - 1)),
                            reads=[xqT_k, "wTb"],
                            writes=[PKa(3, h)], accum=(dc > 0))
                        yield
                    ctx.emit("V", lambda v: v.scalar_tensor_tensor(
                        out=g2t, in0=P[3][:, h * 512:(h + 1) * 512], scalar=1.0,
                        in1=bias2_b[:, h * 512:(h + 1) * 512],
                        op0=OP.mult, op1=OP.add),
                        reads=[PKa(3, h), "bias2_b"], writes=[g2t_k])
                    yield
                    ctx.emit("S", lambda s: s.activation(
                        out=g2t, in_=g2t, func=AF.Silu),
                        reads=[g2t_k], writes=[g2t_k])
                    yield
                    ctx.emit("V", lambda v: v.tensor_tensor(
                        out=g2t, in0=g2t,
                        in1=s_tok[:, tc * 1024 + h * 512:
                                  tc * 1024 + (h + 1) * 512], op=OP.mult),
                        reads=[g2t_k, "s%d" % tc], writes=[g2t_k])
                    yield
                    ctx.emit("V", lambda v: v.tensor_tensor(
                        out=x[:, tc * 1024 + h * 512: tc * 1024 + (h + 1) * 512],
                        in0=x[:, tc * 1024 + h * 512: tc * 1024 + (h + 1) * 512],
                        in1=g2t, op=OP.add),
                        reads=[g2t_k, "x%d" % tc], writes=["x%d" % tc])
                    yield

            def interleave(*gs):
                gs = [g for g in gs if g is not None]
                while gs:
                    alive = []
                    for g in gs:
                        try:
                            next(g)
                            alive.append(g)
                        except StopIteration:
                            pass
                    gs = alive
                    yield

            with ExitStack() as sts:
                c_j = SB(sts, "c_j", [128, 128], F32)
                ctx.dma("Q", c_j[:], jrev, writes=["c_j"])
                c_mk = SB(sts, "c_mk", [128, 4 * 512], BF16)
                ctx.dma("Q", c_mk[:], cmasks, writes=["c_mk"])
                strip = SB(sts, "strip", [128, TC * 512], BF16)
                t1b = SB(sts, "t1b", [128, 512], F32)
                biasT = SB(sts, "biasT", [128, 512], F32)
                zt = SB(sts, "zt", [128, 512], F32)
                dent, dent_k = biasT, "biasT"
                def simav_gen(it):
                    i0 = it * 512
                    jcmax = min(TC - 1, it * 4 + 3)
                    for jc in range(jcmax + 1):
                        base = (i0 - jc * 128) + 513
                        ctx.dma("S", t1b[:],
                                RAP(tensor=bias_ext.tensor, offset=base,
                                    ap=[[1, 128], [1, 512]]),
                                reads=["bias_ext"], writes=["t1b"])
                        jp = (jc % 2) * 512
                        ctx.emit("T", lambda t: t.matmul(
                            out=P[1][:, jp:jp + 512], lhsT=c_j[:], rhs=t1b[:],
                            start=True, stop=True),
                            reads=["c_j", "t1b"], writes=[PKa(1, jc % 2)])
                        ctx.emit("S", lambda s: s.copy(
                            out=biasT[:], in_=P[1][:, jp:jp + 512]),
                            reads=[PKa(1, jc % 2)], writes=["biasT"])
                        ctx.emit("T", lambda t: t.matmul(
                            out=P[0][:, jp:jp + 512],
                            lhsT=kf[:, jc * 128:(jc + 1) * 128],
                            rhs=qf[:, it * 512:(it + 1) * 512],
                            start=True, stop=True),
                            reads=["kf", "qf"], writes=[PKa(0, jc % 2)])
                        ctx.emit("V", lambda v: v.scalar_tensor_tensor(
                            out=zt[:], in0=P[0][:, jp:jp + 512], scalar=1.0,
                            in1=biasT[:], op0=OP.mult, op1=OP.add),
                            reads=[PKa(0, jc % 2), "biasT"], writes=["zt"])
                        ctx.emit("S", lambda s: s.activation(
                            out=strip[:, jc * 512:(jc + 1) * 512], in_=zt[:],
                            func=AF.Gelu),
                            reads=["zt"], writes=["strip"])
                        ctx.emit("V", lambda v: v.scalar_tensor_tensor(
                            out=dent[:], in0=zt[:], scalar=0.0, in1=zt[:],
                            op0=OP.is_equal, op1=OP.add),
                            reads=["zt"], writes=[dent_k])
                        ctx.emit("V", lambda v: v.reciprocal(
                            out=dent[:], in_=dent[:]), reads=[dent_k], writes=[dent_k])
                        ctx.emit("V", lambda v: v.tensor_tensor(
                            out=strip[:, jc * 512:(jc + 1) * 512],
                            in0=strip[:, jc * 512:(jc + 1) * 512],
                            in1=dent[:], op=OP.mult),
                            reads=[dent_k, "strip"], writes=["strip"])
                        mq = it * 4 - jc
                        if mq < 1:
                            mi = -mq
                            ctx.emit("V", lambda v: v.tensor_tensor(
                                out=strip[:, jc * 512:(jc + 1) * 512],
                                in0=strip[:, jc * 512:(jc + 1) * 512],
                                in1=c_mk[:, mi * 512:(mi + 1) * 512], op=OP.mult),
                                reads=["c_mk", "strip"], writes=["strip"])
                    for lc in range(4):
                        tci = it * 4 + lc
                        for jc in range(tci + 1):
                            for h in range(2):
                                ctx.emit("T", lambda t: t.matmul(
                                    out=P[2][:, h * 512:(h + 1) * 512],
                                    lhsT=strip[:, jc * 512 + lc * 128:
                                               jc * 512 + (lc + 1) * 128],
                                    rhs=v_tok[:, jc * 1024 + h * 512:
                                              jc * 1024 + h * 512 + 512],
                                    start=(jc == 0), stop=(jc == tci)),
                                    reads=["strip", "v_tok"], writes=PKf(2),
                                    accum=(jc > 0))
                                yield
                        ctx.emit("S", lambda s: s.copy(
                            out=s_tok[:, tci * 1024:(tci + 1) * 1024], in_=P[2][:]),
                            reads=PKf(2), writes=["s%d" % tci])
                        yield

                def ffn_chain(it):
                    for lc in range(4):
                        tc = it * 4 + lc
                        if tc == 0:
                            for _ in ffn_quant(0):
                                yield
                        nq = ffn_quant(tc + 1) if tc + 1 < TC else None
                        for _ in interleave(ffn_mm(tc), nq):
                            yield

                pend = []
                for it in range(NIT):
                    gens = [simav_gen(it)] + pend
                    if it == 0:
                        gens.append(w8_gen())
                    weave(gens)
                    pend = [ffn_chain(it)]
                weave(pend)
        for tc in range(TC):
            ctx.dma("Q", x_out.rearrange("(tc p) d -> p tc d", p=128)[:, tc],
                    x[:, tc * 1024:(tc + 1) * 1024],
                    reads=["x%d" % tc], writes=[], slot="xout")
        ctx.eng["Q"].wait_ge(ctx.sems["D:xout"], ctx.dmaval["D:xout"])


def _host_consts(S):
    consts = {}
    consts["c_ident"] = np.eye(128, dtype=np.float32)
    consts["c_jrev"] = np.eye(128, dtype=np.float32)[::-1].copy()
    consts["c_ones1"] = np.ones((1, 128), dtype=np.float32)
    consts["c_onesk"] = np.ones((128, 1), dtype=ml_dtypes.bfloat16)
    consts["c_iota8"] = np.broadcast_to(
        np.arange(E, dtype=np.float32), (128, E)).copy()
    consts["c_iotaB"] = consts["c_iota8"] + np.float32(BIG)
    sel2 = np.zeros((2, 128), dtype=ml_dtypes.bfloat16)
    sel2[0, 0:64] = 1.0
    sel2[1, 64:128] = 1.0
    consts["c_sel2"] = sel2
    s16 = np.zeros((16, 8 * 128), dtype=np.float32)
    for c in range(8):
        for p in range(128):
            s16[2 * c + p // 64, c * 128 + p] = 1.0
    consts["c_sel16"] = s16.astype(ml_dtypes.bfloat16)
    masks = np.zeros((128, 4 * 512), dtype=np.float32)
    jj = np.arange(128)[:, None]
    ii = np.arange(512)[None, :]
    for mi in range(4):
        masks[:, mi * 512:(mi + 1) * 512] = (jj <= ii - mi * 128).astype(np.float32)
    consts["c_masks"] = masks.astype(ml_dtypes.bfloat16)
    n_ext = S + 768
    oh = np.zeros((NB, n_ext), dtype=np.float32)
    max_exact = NB // 2
    for dp in range(n_ext):
        delta = dp - 640
        if delta < 0 or delta >= S:
            continue
        if delta < max_exact:
            b = delta
        else:
            n_safe = np.float32(max(delta, 1))
            val = max_exact + int(
                np.float32(np.log(n_safe / np.float32(max_exact))
                           / np.float32(math.log(MAXD / max_exact))
                           * np.float32(NB - max_exact)))
            b = min(val, NB - 1)
        oh[b, dp] = 1.0
    consts["c_oh"] = oh
    return consts


def kernel(L0_attn_x, L0_attn_Wqkv, L0_attn_Wout, L0_attn_bout,
           L0_ffn_router_w, L0_ffn_gate_proj, L0_ffn_up_proj, L0_ffn_down_proj,
           L1_attn_wqk, L1_attn_bqk, L1_attn_gamma, L1_attn_beta,
           L1_attn_wv, L1_attn_bv, L1_attn_rel_bias,
           L1_ffn_w_int8, L1_ffn_weight_scale, L1_ffn_bias):
    x = np.ascontiguousarray(np.asarray(L0_attn_x, dtype=np.float32))
    B, T, D_ = x.shape
    assert B == NCORES and D_ == D
    nc = bass.Bass("TRN2", target_bir_lowering=False)
    build_program(nc, T)
    consts = _host_consts(T)
    common = dict(
        L0_Wqkv=np.ascontiguousarray(np.asarray(L0_attn_Wqkv, np.float32)),
        L0_Wout=np.ascontiguousarray(np.asarray(L0_attn_Wout, np.float32)),
        L0_bout=np.ascontiguousarray(np.asarray(L0_attn_bout, np.float32)),
        L0_router=np.ascontiguousarray(np.asarray(L0_ffn_router_w, np.float32)),
        L0_gate=np.ascontiguousarray(np.asarray(L0_ffn_gate_proj, np.float32)),
        L0_up=np.ascontiguousarray(np.asarray(L0_ffn_up_proj, np.float32)),
        L0_down=np.ascontiguousarray(np.asarray(L0_ffn_down_proj, np.float32)),
        L1_wqk=np.ascontiguousarray(np.asarray(L1_attn_wqk, np.float32)),
        L1_bqk=np.ascontiguousarray(np.asarray(L1_attn_bqk, np.float32)),
        L1_gamma=np.ascontiguousarray(np.asarray(L1_attn_gamma, np.float32)),
        L1_beta=np.ascontiguousarray(np.asarray(L1_attn_beta, np.float32)),
        L1_wv=np.ascontiguousarray(np.asarray(L1_attn_wv, np.float32)),
        L1_bv=np.ascontiguousarray(np.asarray(L1_attn_bv, np.float32)),
        L1_rel=np.ascontiguousarray(np.asarray(L1_attn_rel_bias, np.float32)),
        L1_w8=np.ascontiguousarray(np.asarray(L1_ffn_w_int8, np.int32)),
        L1_ws=np.asarray(L1_ffn_weight_scale, np.float32).reshape(1),
        L1_bias=np.ascontiguousarray(np.asarray(L1_ffn_bias, np.float32)),
        **consts,
    )
    in_maps = [dict(common, x_in=np.ascontiguousarray(x[c])) for c in range(NCORES)]
    res = run_bass_kernel_spmd(nc, in_maps, core_ids=list(range(NCORES)))
    out = np.stack([np.asarray(res.results[c]["x_out"]) for c in range(NCORES)],
                   axis=0)
    return out.astype(np.float32)


# version 23
# speedup vs baseline: 1.1205x; 1.0058x over previous
import math
import numpy as np
import ml_dtypes
import concourse.bass as bass
import concourse.mybir as mybir
from concourse.bass_utils import run_bass_kernel_spmd
from concourse.alu_op_type import AluOpType as OP
from concourse import dve_ops
from bass_rust import AP as RAP
from contextlib import ExitStack

F32 = mybir.dt.float32
BF16 = mybir.dt.bfloat16
I32 = mybir.dt.int32
I16 = mybir.dt.int16
AF = mybir.ActivationFunctionType
AX = mybir.AxisListType

NCORES = 8
D = 1024
DC = D // 128
E, FF = 8, 2816
FC = FF // 128
DQK, DV = 128, 1024
NB, MAXD = 32, 128
LN_EPS = 1e-5
ATT_EPS = 1e-6
RMS_EPS = 1e-6
ACT_EPS = 1e-5
MU = math.sqrt(0.5)
STD = math.sqrt(0.25 * math.pi)
FGROUPS = [2] * 11
BIG = 65536.0


class Ctx:

    def __init__(self, nc, sems, mksem):
        self.nc = nc
        self.eng = {"T": nc.tensor, "V": nc.vector, "S": nc.scalar,
                    "G": nc.gpsimd, "Q": nc.sync}
        self.sems = sems
        self.mksem = mksem
        self.count = {k: 0 for k in "TVSGQ"}
        self.ilog = {k: [] for k in "TVSGQ"}
        self.incs = {k: [] for k in "TVSGQ"}
        self.semval = {k: 0 for k in "TVSGQ"}
        self.dmaval = {}
        self.wm = {k: {} for k in "TVSGQ"}
        self.lastc = {k: None for k in "TVSGQ"}
        self.bufs = {}
        self.noburst = set()
        self.sticky = {}

    def _prod_sem(self, prod):
        kind, idx = prod
        if kind.startswith("D:"):
            return (kind, idx)
        for (i2, v2) in self.incs[kind]:
            if i2 >= idx:
                return (kind, v2)
        v = self.semval[kind] + 1
        self.semval[kind] = v
        self.ilog[kind][idx].then_inc(self.sems[kind], 1)
        self.incs[kind].append((idx, v))
        return (kind, v)

    def _wait(self, ename, semkey, val):
        if self.wm[ename].get(semkey, 0) >= val:
            return
        self.eng[ename].wait_ge(self.sems[semkey], val)
        self.wm[ename][semkey] = val

    def _collect(self, ename, reads, writes, needed):
        for b in reads:
            s = self.bufs.get(b)
            if s and s["w"] is not None:
                p = s["w"]
                if ename == "T" and p[0] == "T":
                    continue
                k, v = self._prod_sem(p)
                needed[k] = max(needed.get(k, 0), v)
        for b in writes:
            s = self.bufs.get(b)
            if s is None:
                continue
            deps = list(s["r"])
            if s["w"] is not None:
                deps.append(s["w"])
            for p in deps:
                if ename == "T" and p[0] == "T":
                    continue
                k, v = self._prod_sem(p)
                needed[k] = max(needed.get(k, 0), v)

    def emit(self, ename, fn, reads=(), writes=(), accum=False):
        needed = {}
        self._collect(ename, reads, () if accum else writes, needed)
        for k in sorted(needed):
            self._wait(ename, k, needed[k])
        instr = fn(self.eng[ename])
        idx = self.count[ename]
        self.count[ename] += 1
        self.ilog[ename].append(instr)
        self.lastc[ename] = idx
        prod = (ename, idx)
        for b in writes:
            if accum and b in self.bufs:
                self.bufs[b]["w"] = prod
            else:
                self.bufs[b] = {"w": prod, "r": []}
        for b in reads:
            self.bufs.setdefault(b, {"w": None, "r": []})["r"].append(prod)
        return instr

    def dma(self, qname, out, in_, reads=(), writes=(), slot=None):
        if slot is None:
            slot = writes[0] if writes else "store"
        semkey = "D:" + slot
        if semkey not in self.sems:
            self.sems[semkey] = self.mksem(slot)
            self.dmaval[semkey] = 0
        needed = {}
        self._collect(qname, reads, writes, needed)
        for k in sorted(needed):
            self._wait(qname, k, needed[k])
        instr = self.eng[qname].dma_start(out=out, in_=in_)
        idx = self.count[qname]
        self.count[qname] += 1
        self.ilog[qname].append(instr)
        self.dmaval[semkey] += 16
        v = self.dmaval[semkey]
        instr.then_inc(self.sems[semkey], 16)
        prod = (semkey, v)
        for b in writes:
            self.bufs[b] = {"w": prod, "r": []}
        for b in reads:
            self.bufs.setdefault(b, {"w": None, "r": []})["r"].append(prod)
        return instr

    def barrier(self):
        for k in "TVSGQ":
            if self.lastc[k] is not None:
                self._prod_sem((k, self.lastc[k]))
        targets = [(k, self.semval[k]) for k in "TVSGQ" if self.semval[k] > 0]
        targets += [(k, v) for k, v in self.dmaval.items()
                    if v > 0 and k not in self.noburst]
        for ename in "TVSGQ":
            for k, v in targets:
                self._wait(ename, k, v)
        self.bufs = {}
        for k, p in self.sticky.items():
            self.bufs[k] = {"w": p, "r": []}


def weave(gens):
    gens = [g for g in gens if g is not None]
    while gens:
        alive = []
        for g in gens:
            try:
                next(g)
                alive.append(g)
            except StopIteration:
                continue
        gens = alive


def c_hi_lo(x):
    hi = np.float32(x)
    lo = np.float32(x - float(hi))
    return float(hi), float(lo)


def build_program(nc, T):
    TC = T // 128
    NIT = T // 512
    NTB = T // 512
    S = T
    NBE = (S + 768) // 128

    dt = lambda name, shape, dty, kind: nc.dram_tensor(name, shape, dty, kind=kind).ap()
    x_in = dt("x_in", [T, D], F32, "ExternalInput")
    Wqkv = dt("L0_Wqkv", [D, 3 * D], F32, "ExternalInput")
    Wout = dt("L0_Wout", [D, D], F32, "ExternalInput")
    bout = dt("L0_bout", [D], F32, "ExternalInput")
    router_w = dt("L0_router", [D, E], F32, "ExternalInput")
    gate_p = dt("L0_gate", [E, D, FF], F32, "ExternalInput")
    up_p = dt("L0_up", [E, D, FF], F32, "ExternalInput")
    down_p = dt("L0_down", [E, FF, D], F32, "ExternalInput")
    wqk = dt("L1_wqk", [D, DQK], F32, "ExternalInput")
    bqk = dt("L1_bqk", [DQK], F32, "ExternalInput")
    gamma = dt("L1_gamma", [2, DQK], F32, "ExternalInput")
    beta = dt("L1_beta", [2, DQK], F32, "ExternalInput")
    wv = dt("L1_wv", [D, DV], F32, "ExternalInput")
    bv = dt("L1_bv", [DV], F32, "ExternalInput")
    rel_bias = dt("L1_rel", [NB], F32, "ExternalInput")
    w_int8 = dt("L1_w8", [D, D], I32, "ExternalInput")
    wscale = dt("L1_ws", [1], F32, "ExternalInput")
    bias2 = dt("L1_bias", [D], F32, "ExternalInput")
    ident = dt("c_ident", [128, 128], F32, "ExternalInput")
    jrev = dt("c_jrev", [128, 128], F32, "ExternalInput")
    ones1 = dt("c_ones1", [1, 128], F32, "ExternalInput")
    onesk = dt("c_onesk", [128, 1], BF16, "ExternalInput")
    iota8 = dt("c_iota8", [128, E], F32, "ExternalInput")
    iotaB = dt("c_iotaB", [128, E], F32, "ExternalInput")
    sel2 = dt("c_sel2", [2, 128], BF16, "ExternalInput")
    sel16 = dt("c_sel16", [16, 8 * 128], BF16, "ExternalInput")
    cmasks = dt("c_masks", [128, 4 * 512], BF16, "ExternalInput")
    ohmat = dt("c_oh", [NB, S + 768], F32, "ExternalInput")
    x_out = dt("x_out", [T, D], F32, "ExternalOutput")
    bias_ext = dt("s_bias", [S + 768], F32, "Internal")
    gcache = dt("s_gcache", [E, D, FF], BF16, "Internal")
    ucache = dt("s_ucache", [E, D, FF], BF16, "Internal")
    dcache = dt("s_dcache", [E, FF, D], BF16, "Internal")
    qkvcache = dt("s_qkvc", [D, 3 * D], BF16, "Internal")
    ocache = dt("s_oc", [D, D], BF16, "Internal")
    vcache = dt("s_vc", [D, DV], BF16, "Internal")
    qkcache = dt("s_qkc", [D, DQK], BF16, "Internal")
    rcache = dt("s_rc", [D, E], BF16, "Internal")

    with ExitStack() as stk:
        sems = {}
        for k in ["T", "V", "S", "G"]:
            sems[k] = stk.enter_context(nc.semaphore("sem_" + k))
        mksem = lambda slot: stk.enter_context(nc.semaphore("dsem_" + slot.replace(":", "_")))
        ctx = Ctx(nc, sems, mksem)

        def SB(st, name, shape, dty):
            return st.enter_context(nc.sbuf_tensor(name, shape, dty))

        x = SB(stk, "x", [128, TC * 1024], F32)
        xnT_stk = stk.enter_context(ExitStack())
        xnT = xnT_stk.enter_context(nc.sbuf_tensor("xnT", [128, DC * T], BF16,
                                                   side="right"))
        s_tok = SB(stk, "s_tok", [128, TC * 1024], BF16)
        m_st = SB(stk, "m_st", [128, TC], F32)
        r_st = SB(stk, "r_st", [128, TC], F32)
        ew = SB(stk, "ew", [128, TC * E], F32)
        c_id = SB(stk, "c_id", [128, 128], F32)
        c_o1 = SB(stk, "c_o1", [1, 128], F32)
        c_ok = SB(stk, "c_ok", [128, 1], BF16)
        xn_f = SB(stk, "xn_f", [128, 1024], F32)
        lndump = SB(stk, "lndump", [128, 1024], F32)
        ssum = SB(stk, "ssum", [128, 1], F32)
        ssq = SB(stk, "ssq", [128, 1], F32)
        varv = SB(stk, "varv", [128, 1], F32)
        P = [stk.enter_context(nc.psum_tensor("P%d" % i, [128, 1024], F32))
             for i in range(4)]
        PKa = lambda i, h: "P%d:%d" % (i, h)
        PKf = lambda i: ["P%d:0" % i, "P%d:1" % i]
        PSL = lambda i, s: "P%d:0s%d" % (i, s)
        PSLall = lambda i: [PSL(i, s) for s in range(4)]

        blk = stk.enter_context(nc.Block())

        ctx.dma("Q", c_id[:], ident, writes=["c_id"])
        ctx.dma("Q", c_o1[:], ones1, writes=["c_o1"])
        ctx.dma("G", c_ok[:], onesk, writes=["c_ok"])

        ctx.dma("Q", x[:].rearrange("p (tc d) -> p tc d", d=1024),
                x_in.rearrange("(tc p) d -> p tc d", p=128),
                writes=["x%d" % t for t in range(TC)], slot="xld")
        ctx._wait("G", "D:xld", 16)
        ctx.dma("G", rcache, router_w, writes=["rcache"], slot="rcw")
        ctx.dma("G", qkvcache, Wqkv, writes=["qkvcache"], slot="qkvcw")
        ctx.dma("G", ocache, Wout, writes=["ocache"], slot="ocw")
        ctx.dma("G", qkcache, wqk, writes=["qkcache"], slot="qkcw")
        ctx.dma("G", vcache, wv, writes=["vcache"], slot="vcw")
        ctx.sems["MP"] = ctx.mksem("MP")

        def bcast_row(st, name, row_ap, n, pi):
            t = SB(st, name, [128, n], F32)
            bcast_row_into(t, st, name, row_ap, n, pi)
            return t

        def bcast_row_into(t, st, name, row_ap, n, pi):
            rrow = SB(st, name + "_r", [1, n], F32)
            ctx.dma("Q", rrow[:], row_ap, writes=[name + "_r"])
            for h in range(0, n, 512):
                w = min(512, n - h)
                pb = PKa(pi, (h % 1024) // 512)
                ctx.emit("T", lambda t_: t_.matmul(
                    out=P[pi][0:128, h % 1024:h % 1024 + w], lhsT=c_o1[:],
                    rhs=rrow[0:1, h:h + w], start=True, stop=True),
                    reads=["c_o1", name + "_r"], writes=[pb])
                ctx.emit("V", lambda v: v.tensor_copy(
                    out=t[:, h:h + w], in_=P[pi][0:128, h % 1024:h % 1024 + w]),
                    reads=[pb], writes=[name])
            return t

        var16 = SB(stk, "var16", [128, TC], F32)
        stot_a = SB(stk, "stot_a", [128, TC], F32)
        cq2_a = SB(stk, "cq2_a", [128, TC], F32)
        amx_a = SB(stk, "amx_a", [128, TC], F32)
        ct_a = SB(stk, "ct_a", [128, TC], F32)
        wsb2 = SB(stk, "wsb2", [128, 1], F32)

        def ln_stats():
            ctx.emit("V", lambda v: v.tensor_reduce(
                out=m_st[:], in_=x[:].rearrange("p (tc d) -> p tc d", d=1024),
                axis=AX.X, op=OP.add),
                reads=["x%d" % t for t in range(TC)], writes=["m_st"])
            for tc in range(TC):
                ctx.emit("S", lambda s: s.activation(
                    out=lndump[:], in_=x[:, tc * 1024:(tc + 1) * 1024],
                    func=AF.Square, accum_out=r_st[:, tc:tc + 1]),
                    reads=["x%d" % tc], writes=["lndump", "r_st"])
            ctx.emit("V", lambda v: v.tensor_scalar(
                out=m_st[:], in0=m_st[:], scalar1=1.0 / 1024.0, scalar2=None,
                op0=OP.mult), reads=["m_st"], writes=["m_st"])
            ctx.emit("V", lambda v: v.tensor_scalar(
                out=r_st[:], in0=r_st[:], scalar1=1.0 / 1024.0, scalar2=None,
                op0=OP.mult), reads=["r_st"], writes=["r_st"])
            ctx.emit("V", lambda v: v.tensor_tensor(
                out=var16[:], in0=m_st[:], in1=m_st[:], op=OP.mult),
                reads=["m_st"], writes=["var16"])
            ctx.emit("V", lambda v: v.tensor_tensor(
                out=var16[:], in0=r_st[:], in1=var16[:], op=OP.subtract),
                reads=["r_st", "var16"], writes=["var16"])
            ctx.emit("V", lambda v: v.tensor_scalar(
                out=var16[:], in0=var16[:], scalar1=1.0, scalar2=LN_EPS,
                op0=OP.mult, op1=OP.add), reads=["var16"], writes=["var16"])
            ctx.emit("V", lambda v: v.reciprocal(out=var16[:], in_=var16[:]),
                     reads=["var16"], writes=["var16"])
            ctx.emit("S", lambda s: s.activation(
                out=r_st[:], in_=var16[:], func=AF.Sqrt),
                reads=["var16"], writes=["r_st"])

        def ln_xn_gen(tc, rb=None, sc_all=None, sc_key=None, pingpong=False):
            xs = x[:, tc * 1024:(tc + 1) * 1024]
            ctx.emit("V", lambda v: v.tensor_scalar(
                out=xn_f[:], in0=xs, scalar1=m_st[:, tc:tc + 1],
                scalar2=r_st[:, tc:tc + 1], op0=OP.subtract, op1=OP.mult),
                reads=["x%d" % tc, "m_st", "r_st"], writes=["xn_f"])
            yield
            for dc in range(DC):
                if pingpong:
                    bk = dc % 2
                    pcol = bk * 512 + (dc % 4) // 2 * 128
                else:
                    bk = 0
                    pcol = (dc % 4) * 128
                ctx.emit("T", lambda t: t.transpose(
                    out=P[2][:, pcol:pcol + 128],
                    in_=xn_f[:, dc * 128:(dc + 1) * 128], identity=c_id[:]),
                    reads=["xn_f", "c_id"], writes=[PKa(2, bk)])
                yield
                ctx.emit("S", lambda s: s.copy(
                    out=xnT[:, dc * T + tc * 128: dc * T + (tc + 1) * 128],
                    in_=P[2][:, pcol:pcol + 128]),
                    reads=[PKa(2, bk)], writes=["xnT"])
                yield
                if rb is not None:
                    ctx.emit("T", lambda t: t.matmul(
                        out=P[2][0:128, 512:512 + E],
                        lhsT=xnT[:, dc * T + tc * 128: dc * T + (tc + 1) * 128],
                        rhs=rb[:, dc * E:(dc + 1) * E],
                        start=(dc == 0), stop=(dc == DC - 1)),
                        reads=["xnT", "rb"], writes=[PKa(2, 1)], accum=(dc > 0))
                    yield
            if rb is not None:
                ctx.emit("V", lambda v: v.tensor_copy(
                    out=sc_all[:, tc * E:(tc + 1) * E],
                    in_=P[2][0:128, 512:512 + E]),
                    reads=[PKa(2, 1)], writes=[sc_key])
                yield

        def ln_xn(tc, rb=None, sc_all=None, sc_key=None):
            for _ in ln_xn_gen(tc, rb, sc_all, sc_key):
                pass

        if True:
            st0 = stk
            rb = SB(st0, "rb", [128, DC * E], BF16)
            sc_all = SB(st0, "sc_all", [128, TC * E], F32)
            ctx.dma("Q", rb[:].rearrange("p (dc e) -> p dc e", e=E),
                    rcache.rearrange("(dc p) e -> p dc e", p=128),
                    reads=["rcache"], writes=["rb"])
            m1n = SB(st0, "m1n", [128, 1], F32)
            m2n = SB(st0, "m2n", [128, 1], F32)
            eqt = SB(st0, "eqt", [128, E], F32)
            at = SB(st0, "at", [128, E], F32)
            i1t = SB(st0, "i1t", [128, 1], F32)
            mk1 = SB(st0, "mk1", [128, E], F32)
            mk2 = SB(st0, "mk2", [128, E], F32)
            smk = SB(st0, "smk", [128, E], F32)
            e2t = SB(st0, "e2t", [128, 1], F32)
            w1t = SB(st0, "w1t", [128, 1], F32)
            w2t = SB(st0, "w2t", [128, 1], F32)
            ewt = SB(st0, "ewt", [128, E], F32)
            c_i8 = SB(st0, "c_i8", [128, E], F32)
            c_iB = SB(st0, "c_iB", [128, E], F32)
            ctx.dma("Q", c_i8[:], iota8, writes=["c_i8"])
            ctx.dma("Q", c_iB[:], iotaB, writes=["c_iB"])

            def topk_tc(tc):
                scs = sc_all[:, tc * E:(tc + 1) * E]
                ctx.emit("V", lambda v: v.tensor_reduce(
                    out=m1n[:], in_=scs, axis=AX.X, op=OP.max, negate=True),
                    reads=["sc_all"], writes=["m1n"])
                ctx.emit("V", lambda v: v.tensor_scalar(
                    out=eqt[:], in0=scs, scalar1=m1n[:], scalar2=0.0,
                    op0=OP.add, op1=OP.is_equal),
                    reads=["sc_all", "m1n"], writes=["eqt"])
                ctx.emit("V", lambda v: v.scalar_tensor_tensor(
                    out=at[:], in0=eqt[:], scalar=-BIG, in1=c_iB[:],
                    op0=OP.mult, op1=OP.add), reads=["eqt", "c_iB"], writes=["at"])
                ctx.emit("V", lambda v: v.tensor_reduce(
                    out=i1t[:], in_=at[:], axis=AX.X, op=OP.min),
                    reads=["at"], writes=["i1t"])
                ctx.emit("V", lambda v: v.tensor_scalar(
                    out=mk1[:], in0=c_i8[:], scalar1=i1t[:], scalar2=0.0,
                    op0=OP.subtract, op1=OP.is_equal),
                    reads=["c_i8", "i1t"], writes=["mk1"])
                ctx.emit("V", lambda v: v.scalar_tensor_tensor(
                    out=smk[:], in0=mk1[:], scalar=-BIG, in1=scs,
                    op0=OP.mult, op1=OP.add), reads=["mk1", "sc_all"], writes=["smk"])
                ctx.emit("V", lambda v: v.tensor_reduce(
                    out=m2n[:], in_=smk[:], axis=AX.X, op=OP.max, negate=True),
                    reads=["smk"], writes=["m2n"])
                ctx.emit("V", lambda v: v.tensor_scalar(
                    out=eqt[:], in0=smk[:], scalar1=m2n[:], scalar2=0.0,
                    op0=OP.add, op1=OP.is_equal), reads=["smk", "m2n"], writes=["eqt"])
                ctx.emit("V", lambda v: v.scalar_tensor_tensor(
                    out=at[:], in0=eqt[:], scalar=-BIG, in1=c_iB[:],
                    op0=OP.mult, op1=OP.add), reads=["eqt", "c_iB"], writes=["at"])
                ctx.emit("V", lambda v: v.tensor_reduce(
                    out=i1t[:], in_=at[:], axis=AX.X, op=OP.min),
                    reads=["at"], writes=["i1t"])
                ctx.emit("V", lambda v: v.tensor_scalar(
                    out=mk2[:], in0=c_i8[:], scalar1=i1t[:], scalar2=0.0,
                    op0=OP.subtract, op1=OP.is_equal),
                    reads=["c_i8", "i1t"], writes=["mk2"])
                ctx.emit("V", lambda v: v.tensor_scalar(
                    out=e2t[:], in0=m1n[:], scalar1=m2n[:], scalar2=None,
                    op0=OP.subtract), reads=["m1n", "m2n"], writes=["e2t"])
                ctx.emit("S", lambda s: s.activation(
                    out=e2t[:], in_=e2t[:], func=AF.Exp),
                    reads=["e2t"], writes=["e2t"])
                ctx.emit("V", lambda v: v.tensor_scalar(
                    out=w1t[:], in0=e2t[:], scalar1=1.0, scalar2=None, op0=OP.add),
                    reads=["e2t"], writes=["w1t"])
                ctx.emit("V", lambda v: v.reciprocal(out=w1t[:], in_=w1t[:]),
                         reads=["w1t"], writes=["w1t"])
                ctx.emit("V", lambda v: v.tensor_scalar(
                    out=w2t[:], in0=e2t[:], scalar1=w1t[:], scalar2=None,
                    op0=OP.mult), reads=["e2t", "w1t"], writes=["w2t"])
                ctx.emit("V", lambda v: v.tensor_scalar(
                    out=ewt[:], in0=mk2[:], scalar1=w2t[:], scalar2=None,
                    op0=OP.mult), reads=["mk2", "w2t"], writes=["ewt"])
                ctx.emit("V", lambda v: v.scalar_tensor_tensor(
                    out=ew[:, tc * E:(tc + 1) * E], in0=mk1[:], scalar=w1t[:],
                    in1=ewt[:], op0=OP.mult, op1=OP.add),
                    reads=["mk1", "w1t", "ewt"], writes=["ew"])


        stAB = stk.enter_context(ExitStack())
        kvb = SB(stAB, "kvb", [128, 8 * 128], BF16)
        scb = SB(stAB, "scb", [128, 16], BF16)
        with ExitStack() as stA:
            wkv = SB(stA, "wkv", [128, DC * 2048], BF16)
            ctx.dma("Q", wkv[:].rearrange("p (dc f) -> p dc f", f=2048),
                    qkvcache.rearrange("(dc p) f -> p dc f", p=128)[:, :, 1024:3072],
                    reads=["qkvcache"], writes=["wkv"])
            ctx._wait("G", "D:wkv", 16)
            for e_ in range(E):
                ctx.dma("G", gcache[e_], gate_p[e_], writes=["gcache%d" % e_],
                        slot="gcw%d" % e_)
                ctx.dma("G", ucache[e_], up_p[e_], writes=["ucache%d" % e_],
                        slot="ucw%d" % e_)
                for kk in ("gcache%d" % e_, "ucache%d" % e_):
                    ctx.sticky[kk] = ctx.bufs[kk]["w"]
                ctx.noburst.update({"D:gcw%d" % e_, "D:ucw%d" % e_})
            k_bfs = [SB(stA, "k_bf%d" % i, [128, 1024], BF16) for i in range(2)]
            v_bfs = [SB(stA, "v_bf%d" % i, [128, 1024], BF16) for i in range(2)]
            eltmps = [SB(stA, "eltmp%d" % i, [128, 512], F32) for i in range(2)]
            eltmp2s = [SB(stA, "eltmp2%d" % i, [128, 512], F32) for i in range(2)]
            ksum = SB(stA, "ksum", [1, 1024], F32)
            ctx.emit("V", lambda v: v.memset(ksum[:], 0.0), writes=["ksum"])

            def projkv_gen(tc):
                k_bf, k_bfk = k_bfs[tc % 2], "k_bf%d" % (tc % 2)
                v_bf, v_bfk = v_bfs[tc % 2], "v_bf%d" % (tc % 2)
                for part, dst, dstk in ((0, k_bf, k_bfk), (1, v_bf, v_bfk)):
                    for h in range(2):
                        pt, ptk = P[part], PKa(part, h)
                        pcol = h * 512
                        eltmp, eltmp_k = eltmps[h], "eltmp%d" % h
                        eltmp2, eltmp2_k = eltmp2s[h], "eltmp2%d" % h
                        for dc in range(DC):
                            ctx.emit("T", lambda t: t.matmul(
                                out=pt[:, pcol:pcol + 512],
                                lhsT=xnT[:, dc * T + tc * 128: dc * T + (tc + 1) * 128],
                                rhs=wkv[:, dc * 2048 + part * 1024 + pcol:
                                        dc * 2048 + part * 1024 + pcol + 512],
                                start=(dc == 0), stop=(dc == DC - 1)),
                                reads=["xnT", "wkv"], writes=[ptk], accum=(dc > 0))
                            yield
                        if part == 0:
                            ctx.emit("V", lambda v: v.tensor_scalar(
                                out=eltmp[:], in0=pt[:, pcol:pcol + 512],
                                scalar1=0.0, scalar2=None, op0=OP.min),
                                reads=[ptk], writes=[eltmp_k])
                            yield
                            ctx.emit("S", lambda s: s.activation(
                                out=eltmp2[:], in_=eltmp[:], func=AF.Exp),
                                reads=[eltmp_k], writes=[eltmp2_k])
                            yield
                            ctx.emit("V", lambda v: v.scalar_tensor_tensor(
                                out=dst[:, pcol:pcol + 512],
                                in0=pt[:, pcol:pcol + 512], scalar=0.0,
                                in1=eltmp2[:], op0=OP.max, op1=OP.add),
                                reads=[ptk, eltmp2_k], writes=[dstk])
                            yield
                        else:
                            ctx.emit("S", lambda s: s.copy(
                                out=dst[:, pcol:pcol + 512],
                                in_=pt[:, pcol:pcol + 512]),
                                reads=[ptk], writes=[dstk])
                            yield
                for c in range(8):
                    st_ = (tc == 0 and c in (0, 4))
                    sp_ = (tc == TC - 1 and c in (3, 7))
                    ctx.emit("T", lambda t: t.matmul(
                        out=P[3][:, c * 128:(c + 1) * 128],
                        lhsT=k_bf[:, c * 128:(c + 1) * 128],
                        rhs=v_bf[:, c * 128:(c + 1) * 128],
                        start=st_, stop=sp_),
                        reads=[k_bfk, v_bfk], writes=PKf(3),
                        accum=not (tc == 0 and c == 0))
                    yield
                for h in range(2):
                    ctx.emit("T", lambda t: t.matmul(
                        out=P[2][0:1, 0:512], lhsT=c_ok[:],
                        rhs=k_bf[:, h * 512:(h + 1) * 512], start=True, stop=True),
                        reads=["c_ok", k_bfk], writes=[PKa(2, 0)])
                    yield
                    ctx.emit("V", lambda v: v.scalar_tensor_tensor(
                        out=ksum[0:1, h * 512:(h + 1) * 512],
                        in0=P[2][0:1, 0:512], scalar=1.0,
                        in1=ksum[0:1, h * 512:(h + 1) * 512],
                        op0=OP.mult, op1=OP.add),
                        reads=[PKa(2, 0), "ksum"], writes=["ksum"])
                    yield
                topk_tc(tc)

            ln_stats()
            ln_xn(0, rb=rb, sc_all=sc_all, sc_key="sc_all")
            for tc in range(TC):
                nxt = (ln_xn_gen(tc + 1, rb, sc_all, "sc_all")
                       if tc + 1 < TC else None)
                weave([nxt, projkv_gen(tc)])
            ctx.emit("V", lambda v: v.memset(kvb[:], 0.0), writes=["kvb"])
            for c in range(8):
                ctx.emit("V", lambda v: v.tensor_copy(
                    out=kvb[0:64, c * 128:c * 128 + 64],
                    in_=P[3][0:64, c * 128:c * 128 + 64]),
                    reads=[PKa(3, c // 4)], writes=["kvb"])
                ctx.emit("V", lambda v: v.tensor_copy(
                    out=kvb[64:128, c * 128 + 64:c * 128 + 128],
                    in_=P[3][64:128, c * 128 + 64:c * 128 + 128]),
                    reads=[PKa(3, c // 4)], writes=["kvb"])
            ksb = SB(stA, "ksb", [1, 1024], BF16)
            ctx.emit("V", lambda v: v.tensor_copy(out=ksb[:], in_=ksum[:]),
                     reads=["ksum"], writes=["ksb"])
            ctx.emit("V", lambda v: v.memset(scb[:], 0.0), writes=["scb"])
            for c in range(8):
                ctx.dma("Q", scb[0:64, 2 * c:2 * c + 1],
                        ksb[0:1, c * 128:c * 128 + 64],
                        reads=["ksb"], writes=["scb"])
                ctx.dma("Q", scb[64:128, 2 * c + 1:2 * c + 2],
                        ksb[0:1, c * 128 + 64:c * 128 + 128],
                        reads=["ksb"], writes=["scb"])
            ctx.barrier()
            _mp = ctx.emit("V", lambda v: v.nop())
            _mp.then_inc(ctx.sems["MP"], 1)

        with ExitStack() as stB:
            c_s2 = SB(stB, "c_s2", [2, 128], BF16)
            ctx.dma("Q", c_s2[:], sel2, writes=["c_s2"])
            wqo = SB(stB, "wqo", [128, DC * 1024], BF16)
            bout_b = bcast_row(stB, "bout_b", bout.rearrange("(o d) -> o d", o=1),
                               D, 2)
            q_f = SB(stB, "q_f", [128, 1024], F32)
            qT = SB(stB, "qT", [128, DC * 512], BF16)
            att_fm = SB(stB, "att_fm", [128, 8 * 512], BF16)
            r2ab = [SB(stB, "r2ab%d" % i, [2, 512], BF16) for i in range(2)]
            r2af = [SB(stB, "r2af%d" % i, [2, 512], F32) for i in range(2)]
            r2ts = [SB(stB, "r2t%d" % i, [128, 512], F32) for i in range(2)]
            eltmp = SB(stB, "eltmpB", [128, 512], F32)
            eltmp2 = SB(stB, "eltmpB2", [128, 512], F32)
            for it in range(NIT):
                ctx.dma("Q", wqo[:].rearrange("p (dc f) -> p dc f", f=1024),
                        qkvcache.rearrange("(dc p) f -> p dc f", p=128)[:, :, 0:1024],
                        reads=["qkvcache"], writes=["wqo"])
                for lc in range(4):
                    tc = it * 4 + lc
                    for h in range(2):
                        for dc in range(DC):
                            ctx.emit("T", lambda t: t.matmul(
                                out=P[h][:, 0:512],
                                lhsT=xnT[:, dc * T + tc * 128: dc * T + (tc + 1) * 128],
                                rhs=wqo[:, dc * 1024 + h * 512: dc * 1024 + h * 512 + 512],
                                start=(dc == 0), stop=(dc == DC - 1)),
                                reads=["xnT", "wqo"], writes=[PKa(h, 0)], accum=(dc > 0))
                        ctx.emit("V", lambda v: v.tensor_scalar(
                            out=eltmp[:], in0=P[h][:, 0:512], scalar1=0.0,
                            scalar2=None, op0=OP.min), reads=[PKa(h, 0)], writes=["eltmpB"])
                        ctx.emit("S", lambda s: s.activation(
                            out=eltmp2[:], in_=eltmp[:], func=AF.Exp),
                            reads=["eltmpB"], writes=["eltmpB2"])
                        ctx.emit("V", lambda v: v.scalar_tensor_tensor(
                            out=q_f[:, h * 512:(h + 1) * 512], in0=P[h][:, 0:512],
                            scalar=0.0, in1=eltmp2[:], op0=OP.max, op1=OP.add),
                            reads=[PKa(h, 0), "eltmpB2"], writes=["q_f"])
                    for dc in range(DC):
                        bk = dc % 2
                        pcol = bk * 512 + ((dc % 4) // 2) * 128
                        ctx.emit("T", lambda t: t.transpose(
                            out=P[2][:, pcol:pcol + 128],
                            in_=q_f[:, dc * 128:(dc + 1) * 128], identity=c_id[:]),
                            reads=["q_f", "c_id"], writes=[PKa(2, bk)])
                        ctx.emit("S", lambda s: s.copy(
                            out=qT[:, dc * 512 + lc * 128: dc * 512 + (lc + 1) * 128],
                            in_=P[2][:, pcol:pcol + 128]),
                            reads=[PKa(2, bk)], writes=["qT"])
                ctx.dma("S", wqo[:].rearrange("p (fc f) -> p fc f", f=1024),
                        ocache.rearrange("(fc p) f -> p fc f", p=128),
                        reads=["ocache"], writes=["wqo"])
                for c in range(8):
                    cp = c % 2
                    pq, pqk = (P[2], PKa(2, 1)) if cp == 0 else (P[3], PKa(3, 1))
                    ctx.emit("T", lambda t: t.matmul(
                        out=pq[0:2, 512:1024], lhsT=scb[:, 2 * c:2 * c + 2],
                        rhs=qT[:, c * 512:(c + 1) * 512], start=True, stop=True),
                        reads=["scb", "qT"], writes=[pqk])
                    ctx.emit("V", lambda v: v.tensor_scalar(
                        out=r2af[cp][:], in0=pq[0:2, 512:1024], scalar1=ATT_EPS,
                        scalar2=None, op0=OP.add),
                        reads=[pqk], writes=["r2af%d" % cp])
                    ctx.emit("V", lambda v: v.reciprocal(
                        out=r2af[cp][:], in_=r2af[cp][:]),
                        reads=["r2af%d" % cp], writes=["r2af%d" % cp])
                    ctx.emit("V", lambda v: v.tensor_copy(
                        out=r2ab[cp][:], in_=r2af[cp][:]),
                        reads=["r2af%d" % cp], writes=["r2ab%d" % cp])
                    ctx.emit("T", lambda t: t.matmul(
                        out=P[1][:, cp * 512:cp * 512 + 512],
                        lhsT=c_s2[:], rhs=r2ab[cp][:],
                        start=True, stop=True),
                        reads=["c_s2", "r2ab%d" % cp], writes=[PKa(1, cp)])
                    ctx.emit("V", lambda v: v.tensor_copy(
                        out=r2ts[cp][:], in_=P[1][:, cp * 512:cp * 512 + 512]),
                        reads=[PKa(1, cp)], writes=["r2t%d" % cp])
                    ctx.emit("T", lambda t: t.matmul(
                        out=P[0][:, cp * 512:cp * 512 + 512],
                        lhsT=kvb[:, c * 128:(c + 1) * 128],
                        rhs=qT[:, c * 512:(c + 1) * 512], start=True, stop=True),
                        reads=["kvb", "qT"], writes=[PKa(0, cp)])
                    ctx.emit("V", lambda v: v.tensor_tensor(
                        out=att_fm[:, c * 512:(c + 1) * 512],
                        in0=P[0][:, cp * 512:cp * 512 + 512],
                        in1=r2ts[cp][:], op=OP.mult),
                        reads=[PKa(0, cp), "r2t%d" % cp], writes=["att_fm"])
                for lc in range(4):
                    tc = it * 4 + lc
                    for h in range(2):
                        for fc in range(DC):
                            ctx.emit("T", lambda t: t.matmul(
                                out=P[3][:, h * 512:(h + 1) * 512],
                                lhsT=att_fm[:, fc * 512 + lc * 128:
                                            fc * 512 + (lc + 1) * 128],
                                rhs=wqo[:, fc * 1024 + h * 512:
                                        fc * 1024 + h * 512 + 512],
                                start=(fc == 0), stop=(fc == DC - 1)),
                                reads=["att_fm", "wqo"], writes=[PKa(3, h)],
                                accum=(fc > 0))
                        ctx.emit("V", lambda v: v.scalar_tensor_tensor(
                            out=s_tok[:, tc * 1024 + h * 512:
                                      tc * 1024 + (h + 1) * 512],
                            in0=P[3][:, h * 512:(h + 1) * 512], scalar=1.0,
                            in1=bout_b[:, h * 512:(h + 1) * 512],
                            op0=OP.mult, op1=OP.add),
                            reads=[PKa(3, h), "bout_b"], writes=["s%d" % tc])
            ctx.barrier()
        stAB.close()
        _mp2 = ctx.emit("V", lambda v: v.nop())
        _mp2.then_inc(ctx.sems["MP"], 1)

        for e_ in range(E):
            ctx._wait("G", "MP", max(2, 2 + e_))
            ctx.dma("G", dcache[e_], down_p[e_], writes=["dcache%d" % e_],
                    slot="dcw%d" % e_)
            ctx.sticky["dcache%d" % e_] = ctx.bufs["dcache%d" % e_]["w"]
            ctx.noburst.add("D:dcw%d" % e_)

        with ExitStack() as stM:
            h_w = SB(stM, "h_w", [128, FC * 512], BF16)
            gacc = SB(stM, "gacc", [128, 4 * 1024], F32)
            gbufs = [SB(stM, "gbuf%d" % i, [128, DC * 256], BF16) for i in range(2)]
            ubufs = [SB(stM, "ubuf%d" % i, [128, DC * 256], BF16) for i in range(2)]
            dbufs = [SB(stM, "dbuf%d" % i, [128, 2 * 1024], BF16) for i in range(2)]
            gsils = [xn_f[:, 0:512], xn_f[:, 512:1024]]
            gtmp, gtmp_k = lndump, "lndump"
            GOFF = [2 * i for i in range(11)]
            DGROUPS = [2] * 11
            GOFFD = [2 * i for i in range(11)]

            units = []
            for tb in range(NTB):
                for e in range(E):
                    for g in range(len(FGROUPS)):
                        units.append(("gu", tb, e, g))
                    for g in range(len(DGROUPS)):
                        units.append(("dn", tb, e, g))
                units.append(("fin", tb, 0, 0))

            gu_ct = [0]
            dn_ct = [0]

            def issue_dma(u):
                kind, tb, e, g = u
                if kind == "fin":
                    return
                if kind == "gu":
                    ng = FGROUPS[g]
                    fc0 = GOFF[g]
                    w_ = ng * 128
                    sl = gu_ct[0] % 2
                    gu_ct[0] += 1
                    gb, ub = gbufs[sl], ubufs[sl]
                    gk, uk = "gbuf%d" % sl, "ubuf%d" % sl
                    ctx.dma("Q", gb[:, 0:DC * w_].rearrange(
                        "p (dc f) -> p dc f", f=w_),
                        gcache.rearrange("e (dc p) f -> e p dc f", p=128)
                        [e, :, :, fc0 * 128:(fc0 + ng) * 128],
                        reads=["gcache%d" % e], writes=[gk], slot=gk)
                    ctx.dma("S", ub[:, 0:DC * w_].rearrange(
                        "p (dc f) -> p dc f", f=w_),
                        ucache.rearrange("e (dc p) f -> e p dc f", p=128)
                        [e, :, :, fc0 * 128:(fc0 + ng) * 128],
                        reads=["ucache%d" % e], writes=[uk], slot=uk)
                else:
                    ng = DGROUPS[g]
                    fc0 = GOFFD[g]
                    sl = dn_ct[0] % 2
                    dn_ct[0] += 1
                    db = dbufs[sl]
                    dk = "dbuf%d" % sl
                    ctx.dma("S" if g % 2 else "Q",
                            db[:, 0:ng * 1024].rearrange(
                                "p (fc f) -> p fc f", f=1024),
                            dcache.rearrange("e (fc p) f -> e p fc f", p=128)
                            [e, :, fc0:fc0 + ng, :],
                            reads=["dcache%d" % e], writes=[dk], slot=dk)

            cg_ct = [0]
            cd_ct = [0]

            def emit_compute(u):
                kind, tb, e, g = u
                if kind == "fin":
                    for lc in range(4):
                        tc = tb * 4 + lc
                        ctx.emit("S", lambda s: s.activation(
                            out=gtmp[:], in_=gacc[:, lc * 1024:(lc + 1) * 1024],
                            func=AF.Silu), reads=["gacc"], writes=[gtmp_k])
                        ctx.emit("V", lambda v: v.tensor_tensor(
                            out=gtmp[:], in0=gtmp[:],
                            in1=s_tok[:, tc * 1024:(tc + 1) * 1024], op=OP.mult),
                            reads=[gtmp_k, "s%d" % tc], writes=[gtmp_k])
                        ctx.emit("V", lambda v: v.tensor_tensor(
                            out=x[:, tc * 1024:(tc + 1) * 1024],
                            in0=x[:, tc * 1024:(tc + 1) * 1024], in1=gtmp[:],
                            op=OP.add), reads=[gtmp_k, "x%d" % tc],
                            writes=["x%d" % tc])
                    return
                if kind == "gu":
                    if tb == 0 and g == 0:
                        _m = ctx.emit("V", lambda v: v.nop())
                        _m.then_inc(ctx.sems["MP"], 1)
                    ng = FGROUPS[g]
                    fc0 = GOFF[g]
                    w_ = ng * 128
                    sl = cg_ct[0] % 2
                    cg_ct[0] += 1
                    gb, ub = gbufs[sl], ubufs[sl]
                    gk, uk = "gbuf%d" % sl, "ubuf%d" % sl
                    for fl in range(ng):
                        fc = fc0 + fl
                        pcg = (fc % 2) * 512
                        gsb = gsils[fc % 2]
                        gsk = "xn_f:%d" % (fc % 2)
                        for dc in range(DC):
                            ctx.emit("T", lambda t: t.matmul(
                                out=P[0][:, pcg:pcg + 512],
                                lhsT=gb[:, dc * w_ + fl * 128: dc * w_ + (fl + 1) * 128],
                                rhs=xnT[:, dc * T + tb * 512: dc * T + (tb + 1) * 512],
                                start=(dc == 0), stop=(dc == DC - 1)),
                                reads=[gk, "xnT"], writes=[PKa(0, fc % 2)],
                                accum=(dc > 0))
                        ctx.emit("S", lambda s: s.activation(
                            out=gsb, in_=P[0][:, pcg:pcg + 512], func=AF.Silu),
                            reads=[PKa(0, fc % 2)], writes=[gsk])
                        for dc in range(DC):
                            ctx.emit("T", lambda t: t.matmul(
                                out=P[1][:, pcg:pcg + 512],
                                lhsT=ub[:, dc * w_ + fl * 128: dc * w_ + (fl + 1) * 128],
                                rhs=xnT[:, dc * T + tb * 512: dc * T + (tb + 1) * 512],
                                start=(dc == 0), stop=(dc == DC - 1)),
                                reads=[uk, "xnT"], writes=[PKa(1, fc % 2)],
                                accum=(dc > 0))
                        ctx.emit("V", lambda v: v.tensor_tensor(
                            out=h_w[:, fc * 512:(fc + 1) * 512],
                            in0=P[1][:, pcg:pcg + 512], in1=gsb, op=OP.mult),
                            reads=[PKa(1, fc % 2), gsk], writes=["h_w"])
                else:
                    ng = DGROUPS[g]
                    fc0 = GOFFD[g]
                    sl = cd_ct[0] % 2
                    cd_ct[0] += 1
                    db = dbufs[sl]
                    dk = "dbuf%d" % sl
                    for fl in range(ng):
                        fc = fc0 + fl
                        for lc in range(4):
                            for h in range(2):
                                ctx.emit("T", lambda t: t.matmul(
                                    out=P[lc][:, h * 512:(h + 1) * 512],
                                    lhsT=h_w[:, fc * 512 + lc * 128:
                                             fc * 512 + (lc + 1) * 128],
                                    rhs=db[:, fl * 1024 + h * 512:
                                           fl * 1024 + h * 512 + 512],
                                    start=(fc == 0), stop=(fc == FC - 1)),
                                    reads=["h_w", dk], writes=[PKa(lc, h)],
                                    accum=(fc > 0))
                    if g == len(DGROUPS) - 1:
                        for lc in range(4):
                            tc = tb * 4 + lc
                            wecol = ew[:, tc * E + e: tc * E + e + 1]
                            if e == 0:
                                ctx.emit("V", lambda v: v.tensor_scalar(
                                    out=gacc[:, lc * 1024:(lc + 1) * 1024],
                                    in0=P[lc][:], scalar1=wecol, scalar2=None,
                                    op0=OP.mult),
                                    reads=PKf(lc) + ["ew"], writes=["gacc"])
                            else:
                                ctx.emit("V", lambda v: v.scalar_tensor_tensor(
                                    out=gacc[:, lc * 1024:(lc + 1) * 1024],
                                    in0=P[lc][:], scalar=wecol,
                                    in1=gacc[:, lc * 1024:(lc + 1) * 1024],
                                    op0=OP.mult, op1=OP.add),
                                    reads=PKf(lc) + ["ew", "gacc"], writes=["gacc"])

            pend = 0
            for i in range(len(units)):
                while pend <= i + 1 and pend < len(units):
                    issue_dma(units[pend])
                    pend += 1
                emit_compute(units[i])
            ctx.barrier()

        c2f = math.sqrt(128.0) / STD
        c3f = MU / STD
        ln_stats()
        ctx.dma("Q", wsb2[:], RAP(tensor=wscale.tensor, offset=0,
                                  ap=[[0, 128], [1, 1]]), writes=["wsb2"])
        with ExitStack() as stb:
            c_ohT = SB(stb, "c_ohT", [NB, S + 768], F32)
            ctx.dma("Q", c_ohT[:], ohmat, writes=["c_ohT"])
            relb = SB(stb, "relb", [NB, 1], F32)
            ctx.dma("Q", relb[:], rel_bias.rearrange("(b o) -> b o", o=1),
                    writes=["relb"])
            btmp = SB(stb, "btmp", [128, 128], F32)
            bt2 = SB(stb, "bt2", [128, 128], F32)

            def biasext_gen():
                ctx.emit("V", lambda v: v.memset(btmp[:], 0.0), writes=["btmp"])
                yield
                for ch in range(NBE):
                    ctx.emit("T", lambda t: t.matmul(
                        out=P[1][:, 0:1], lhsT=c_ohT[:, ch * 128:(ch + 1) * 128],
                        rhs=relb[:], start=True, stop=True),
                        reads=["c_ohT", "relb"], writes=[PKa(1, 0)])
                    yield
                    ctx.emit("V", lambda v: v.tensor_scalar(
                        out=btmp[:, ch:ch + 1], in0=P[1][:, 0:1], scalar1=c2f,
                        scalar2=-c3f, op0=OP.mult, op1=OP.add),
                        reads=[PKa(1, 0)], writes=["btmp"])
                    yield
                ctx.emit("T", lambda t: t.transpose(
                    out=P[1][:, 0:128], in_=btmp[:], identity=c_id[:]),
                    reads=["btmp", "c_id"], writes=[PKa(1, 0)])
                yield
                ctx.emit("V", lambda v: v.tensor_copy(
                    out=bt2[:], in_=P[1][:, 0:128]), reads=[PKa(1, 0)], writes=["bt2"])
                yield
                ctx.dma("Q", bias_ext.rearrange("(c p) -> c p", p=128),
                        bt2[0:NBE, :], reads=["bt2"], writes=["bias_ext"])
                yield

            def qstats_gen():
                for tc in range(TC):
                    xs = x[:, tc * 1024:(tc + 1) * 1024]
                    ctx.emit("V", lambda v: v.tensor_scalar(
                        out=lndump[:], in0=xs, scalar1=m_st[:, tc:tc + 1],
                        scalar2=r_st[:, tc:tc + 1], op0=OP.subtract, op1=OP.mult),
                        reads=["x%d" % tc, "m_st", "r_st"], writes=["lndump"])
                    yield
                    ctx.emit("V", lambda v: v.tensor_reduce(
                        out=amx_a[:, tc:tc + 1], in_=lndump[:], axis=AX.X,
                        op=OP.max, apply_absolute_value=True),
                        reads=["lndump"], writes=["amx_a"])
                    yield
                    ctx.emit("S", lambda s: s.activation(
                        out=lndump[:], in_=lndump[:], func=AF.Square,
                        accum_out=stot_a[:, tc:tc + 1]),
                        reads=["lndump"], writes=["lndump", "stot_a"])
                    yield
                ctx.emit("V", lambda v: v.tensor_scalar(
                    out=stot_a[:], in0=stot_a[:], scalar1=1.0 / 1024.0,
                    scalar2=RMS_EPS, op0=OP.mult, op1=OP.add),
                    reads=["stot_a"], writes=["stot_a"])
                yield
                ctx.emit("V", lambda v: v.reciprocal(out=stot_a[:], in_=stot_a[:]),
                         reads=["stot_a"], writes=["stot_a"])
                yield
                ctx.emit("S", lambda s: s.activation(
                    out=stot_a[:], in_=stot_a[:], func=AF.Sqrt),
                    reads=["stot_a"], writes=["stot_a"])
                yield
                ctx.emit("V", lambda v: v.tensor_tensor(
                    out=ct_a[:], in0=amx_a[:], in1=stot_a[:], op=OP.mult),
                    reads=["amx_a", "stot_a"], writes=["ct_a"])
                yield
                ctx.emit("V", lambda v: v.tensor_scalar(
                    out=ct_a[:], in0=ct_a[:], scalar1=ACT_EPS, scalar2=None,
                    op0=OP.max), reads=["ct_a"], writes=["ct_a"])
                yield
                ctx.emit("V", lambda v: v.reciprocal(out=amx_a[:], in_=ct_a[:]),
                         reads=["ct_a"], writes=["amx_a"])
                yield
                ctx.emit("V", lambda v: v.tensor_tensor(
                    out=stot_a[:], in0=stot_a[:], in1=amx_a[:], op=OP.mult),
                    reads=["stot_a", "amx_a"], writes=["stot_a"])
                yield
                ctx.emit("V", lambda v: v.tensor_scalar(
                    out=stot_a[:], in0=stot_a[:], scalar1=127.0, scalar2=None,
                    op0=OP.mult), reads=["stot_a"], writes=["stot_a"])
                yield
                ctx.emit("V", lambda v: v.tensor_scalar(
                    out=cq2_a[:], in0=ct_a[:], scalar1=wsb2[:], scalar2=None,
                    op0=OP.mult), reads=["ct_a", "wsb2"], writes=["cq2_a"])
                yield
                ctx.emit("V", lambda v: v.tensor_scalar(
                    out=cq2_a[:], in0=cq2_a[:], scalar1=1.0 / 127.0, scalar2=None,
                    op0=OP.mult), reads=["cq2_a"], writes=["cq2_a"])
                yield
                ctx.emit("V", lambda v: v.tensor_tensor(
                    out=amx_a[:], in0=r_st[:], in1=stot_a[:], op=OP.mult),
                    reads=["r_st", "stot_a"], writes=["amx_a"])
                yield
                ctx.emit("V", lambda v: v.scalar_tensor_tensor(
                    out=ct_a[:], in0=m_st[:], scalar=-1.0, in1=amx_a[:],
                    op0=OP.mult, op1=OP.mult),
                    reads=["m_st", "amx_a"], writes=["ct_a"])
                yield

            def ln2_gen():
                for tc in range(TC):
                    for _ in ln_xn_gen(tc, pingpong=True):
                        yield

            weave([ln2_gen(), biasext_gen(), qstats_gen()])
            ctx.barrier()

        with ExitStack() as st1:

            bias2_b = SB(st1, "bias2_b", [128, D], F32)
            stAtt = st1.enter_context(ExitStack())
            qf = SB(stAtt, "qf", [128, T], BF16)
            kf = SB(stAtt, "kf", [128, T], BF16)
            with ExitStack() as stq:
                qk_f = SB(stq, "qk_f", [128, T], F32)
                wqkb = SB(stq, "wqkb", [128, DC * DQK], BF16)
                ctx.dma("Q", wqkb[:].rearrange("p (dc f) -> p dc f", f=DQK),
                        qkcache.rearrange("(dc p) f -> p dc f", p=128),
                        reads=["qkcache"], writes=["wqkb"])
                bqk_t = SB(stq, "bqk_t", [128, 1], F32)
                ctx.dma("Q", bqk_t[:], bqk.rearrange("(p o) -> p o", o=1),
                        writes=["bqk_t"])
                g0 = SB(stq, "g0", [128, 1], F32)
                g1 = SB(stq, "g1", [128, 1], F32)
                b0 = SB(stq, "b0", [128, 1], F32)
                b1 = SB(stq, "b1", [128, 1], F32)
                ctx.dma("Q", g0[:], RAP(tensor=gamma.tensor, offset=0,
                                        ap=[[1, 128], [1, 1]]), writes=["g0"])
                ctx.dma("Q", g1[:], RAP(tensor=gamma.tensor, offset=128,
                                        ap=[[1, 128], [1, 1]]), writes=["g1"])
                ctx.dma("Q", b0[:], RAP(tensor=beta.tensor, offset=0,
                                        ap=[[1, 128], [1, 1]]), writes=["b0"])
                ctx.dma("Q", b1[:], RAP(tensor=beta.tensor, offset=128,
                                        ap=[[1, 128], [1, 1]]), writes=["b1"])
                bcast_row_into(bias2_b, stq, "bias2_b",
                               bias2.rearrange("(o d) -> o d", o=1), D, 3)
                cqs = 1.0 / (float(S) * STD)
                ctx.emit("V", lambda v: v.tensor_scalar(
                    out=g0[:], in0=g0[:], scalar1=cqs, scalar2=None, op0=OP.mult),
                    reads=["g0"], writes=["g0"])
                ctx.emit("V", lambda v: v.tensor_scalar(
                    out=b0[:], in0=b0[:], scalar1=cqs, scalar2=None, op0=OP.mult),
                    reads=["b0"], writes=["b0"])
                for it in range(NIT):
                    for dc in range(DC):
                        ctx.emit("T", lambda t: t.matmul(
                            out=P[0][:, 0:512],
                            lhsT=wqkb[:, dc * DQK:(dc + 1) * DQK],
                            rhs=xnT[:, dc * T + it * 512: dc * T + (it + 1) * 512],
                            start=(dc == 0), stop=(dc == DC - 1)),
                            reads=["wqkb", "xnT"], writes=[PKa(0, 0)], accum=(dc > 0))
                    ctx.emit("S", lambda s: s.activation(
                        out=qk_f[:, it * 512:(it + 1) * 512], in_=P[0][:, 0:512],
                        func=AF.Silu, bias=bqk_t[:], scale=1.0),
                        reads=[PKa(0, 0), "bqk_t"], writes=["qk_f"])
                ctx.emit("V", lambda v: v.tensor_scalar(
                    out=qf[:], in0=qk_f[:], scalar1=g0[:], scalar2=b0[:],
                    op0=OP.mult, op1=OP.add),
                    reads=["qk_f", "g0", "b0"], writes=["qf"])
                ctx.emit("V", lambda v: v.tensor_scalar(
                    out=kf[:], in0=qk_f[:], scalar1=g1[:], scalar2=b1[:],
                    op0=OP.mult, op1=OP.add),
                    reads=["qk_f", "g1", "b1"], writes=["kf"])
                ctx.barrier()

            v_tok = SB(stAtt, "v_tok", [128, TC * 1024], BF16)
            with ExitStack() as stv:
                bv_b = bcast_row(stv, "bv_b", bv.rearrange("(o d) -> o d", o=1),
                                 DV, 2)
                wv_h = SB(stv, "wv_h", [128, DC * 512], BF16)
                vtmp = SB(stv, "vtmp", [128, 512], F32)
                for h in range(2):
                    ctx.dma("Q", wv_h[:].rearrange("p (dc f) -> p dc f", f=512),
                            vcache.rearrange("(dc p) f -> p dc f", p=128)
                            [:, :, h * 512:(h + 1) * 512],
                            reads=["vcache"], writes=["wv_h"])
                    for tc in range(TC):
                        for dc in range(DC):
                            ctx.emit("T", lambda t: t.matmul(
                                out=P[h][:, 0:512],
                                lhsT=xnT[:, dc * T + tc * 128: dc * T + (tc + 1) * 128],
                                rhs=wv_h[:, dc * 512:(dc + 1) * 512],
                                start=(dc == 0), stop=(dc == DC - 1)),
                                reads=["xnT", "wv_h"], writes=[PKa(h, 0)], accum=(dc > 0))
                        ctx.emit("V", lambda v: v.scalar_tensor_tensor(
                            out=vtmp[:], in0=P[h][:, 0:512], scalar=1.0,
                            in1=bv_b[:, h * 512:(h + 1) * 512],
                            op0=OP.mult, op1=OP.add),
                            reads=[PKa(h, 0), "bv_b"], writes=["vtmp"])
                        ctx.emit("S", lambda s: s.activation(
                            out=v_tok[:, tc * 1024 + h * 512:
                                      tc * 1024 + (h + 1) * 512],
                            in_=vtmp[:], func=AF.Silu),
                            reads=["vtmp"], writes=["v_tok"])
                ctx.barrier()

            xnT_stk.close()
            wTb = SB(st1, "wTb", [128, DC * 1024], BF16)

            def w8_gen():
                w8f, w8f_k = yq, "yq"
                for oc in range(DC):
                    ctx.dma("G", w8f[:],
                            w_int8.rearrange("(oc p) d -> oc p d", p=128)[oc],
                            writes=[w8f_k])
                    yield
                    for dc in range(DC):
                        bk = dc % 2
                        pcol = bk * 512 + ((dc % 4) // 2) * 128
                        ctx.emit("T", lambda t: t.transpose(
                            out=P[3][:, pcol:pcol + 128],
                            in_=w8f[:, dc * 128:(dc + 1) * 128], identity=c_id[:]),
                            reads=[w8f_k, "c_id"], writes=[PKa(3, bk)])
                        yield
                        ctx.emit("S", lambda s: s.copy(
                            out=wTb[:, dc * 1024 + oc * 128:
                                    dc * 1024 + (oc + 1) * 128],
                            in_=P[3][:, pcol:pcol + 128]),
                            reads=[PKa(3, bk)], writes=["wTb"])
                        yield

            yq = SB(st1, "yq", [128, 1024], F32)
            ti32 = SB(st1, "ti32", [128, 1024], I32)
            xqT = SB(st1, "xqT", [128, DC * 128], BF16)
            yq1 = SB(st1, "yq1", [128, 1024], F32)
            yq_s = [(yq, "yq"), (yq1, "yq1")]
            ti_s = [(ti32, "ti32"), (ti32, "ti32")]
            dif_s = [(xn_f, "xn_f"), (lndump, "lndump")]
            xqT_s = [(xqT, "xqT"), (xqT, "xqT")]
            ctx.barrier()

            def ffn_quant(tc):
                p = tc % 2
                yqb, yq_k = yq_s[p]
                tib, ti_k = ti_s[p]
                difb, dif_k = dif_s[p]
                ctx.emit("S", lambda s: s.activation(
                    out=yqb[:], in_=x[:, tc * 1024:(tc + 1) * 1024],
                    func=AF.Identity,
                    scale=amx_a[:, tc:tc + 1], bias=ct_a[:, tc:tc + 1]),
                    reads=["x%d" % tc, "amx_a", "ct_a"], writes=[yq_k])
                yield
                ctx.emit("S", lambda s: s.copy(out=tib[:], in_=yqb[:]),
                         reads=[yq_k], writes=[ti_k])
                yield
                ctx.emit("V", lambda v: v.scalar_tensor_tensor(
                    out=difb[:], in0=tib[:], scalar=-1.0, in1=yqb[:],
                    op0=OP.mult, op1=OP.add), reads=[ti_k, yq_k], writes=[dif_k])
                yield
                ctx.emit("V", lambda v: v.tensor_scalar(
                    out=yqb[:], in0=difb[:], scalar1=-0.5, scalar2=None,
                    op0=OP.is_le), reads=[dif_k], writes=[yq_k])
                yield
                ctx.emit("V", lambda v: v.tensor_scalar(
                    out=difb[:], in0=difb[:], scalar1=0.5, scalar2=None,
                    op0=OP.is_ge), reads=[dif_k], writes=[dif_k])
                yield
                ctx.emit("V", lambda v: v.scalar_tensor_tensor(
                    out=difb[:], in0=yqb[:], scalar=-1.0, in1=difb[:],
                    op0=OP.mult, op1=OP.add), reads=[yq_k, dif_k], writes=[dif_k])
                yield
                ctx.emit("V", lambda v: v.tensor_tensor(
                    out=yqb[:], in0=tib[:], in1=difb[:], op=OP.add),
                    reads=[ti_k, dif_k], writes=[yq_k])
                yield
                ctx.emit("V", lambda v: v.tensor_scalar(
                    out=yqb[:], in0=yqb[:], scalar1=-128.0, scalar2=127.0,
                    op0=OP.max, op1=OP.min), reads=[yq_k], writes=[yq_k])
                yield
                ctx.emit("V", lambda v: v.tensor_scalar(
                    out=yqb[:], in0=yqb[:], scalar1=cq2_a[:, tc:tc + 1],
                    scalar2=None, op0=OP.mult),
                    reads=[yq_k, "cq2_a"], writes=[yq_k])
                yield

            def ffn_mm(tc):
                p = tc % 2
                yqb, yq_k = yq_s[p]
                difb, dif_k = dif_s[p]
                xqTb, xqT_k = xqT_s[p]
                g2t = difb[:, 0:512]
                g2t_k = dif_k
                for dc in range(DC):
                    bk = dc % 2
                    pcol = bk * 512 + ((dc % 4) // 2) * 128
                    ctx.emit("T", lambda t: t.transpose(
                        out=P[3][:, pcol:pcol + 128],
                        in_=yqb[:, dc * 128:(dc + 1) * 128], identity=c_id[:]),
                        reads=[yq_k, "c_id"], writes=[PKa(3, bk)])
                    yield
                    ctx.emit("S", lambda s: s.copy(
                        out=xqTb[:, dc * 128:(dc + 1) * 128],
                        in_=P[3][:, pcol:pcol + 128]),
                        reads=[PKa(3, bk)], writes=[xqT_k])
                    yield
                for h in range(2):
                    for dc in range(DC):
                        ctx.emit("T", lambda t: t.matmul(
                            out=P[3][:, h * 512:(h + 1) * 512],
                            lhsT=xqTb[:, dc * 128:(dc + 1) * 128],
                            rhs=wTb[:, dc * 1024 + h * 512:
                                    dc * 1024 + h * 512 + 512],
                            start=(dc == 0), stop=(dc == DC - 1)),
                            reads=[xqT_k, "wTb"],
                            writes=[PKa(3, h)], accum=(dc > 0))
                        yield
                    ctx.emit("V", lambda v: v.scalar_tensor_tensor(
                        out=g2t, in0=P[3][:, h * 512:(h + 1) * 512], scalar=1.0,
                        in1=bias2_b[:, h * 512:(h + 1) * 512],
                        op0=OP.mult, op1=OP.add),
                        reads=[PKa(3, h), "bias2_b"], writes=[g2t_k])
                    yield
                    ctx.emit("S", lambda s: s.activation(
                        out=g2t, in_=g2t, func=AF.Silu),
                        reads=[g2t_k], writes=[g2t_k])
                    yield
                    ctx.emit("V", lambda v: v.tensor_tensor(
                        out=g2t, in0=g2t,
                        in1=s_tok[:, tc * 1024 + h * 512:
                                  tc * 1024 + (h + 1) * 512], op=OP.mult),
                        reads=[g2t_k, "s%d" % tc], writes=[g2t_k])
                    yield
                    ctx.emit("V", lambda v: v.tensor_tensor(
                        out=x[:, tc * 1024 + h * 512: tc * 1024 + (h + 1) * 512],
                        in0=x[:, tc * 1024 + h * 512: tc * 1024 + (h + 1) * 512],
                        in1=g2t, op=OP.add),
                        reads=[g2t_k, "x%d" % tc], writes=["x%d" % tc])
                    yield

            def interleave(*gs):
                gs = [g for g in gs if g is not None]
                while gs:
                    alive = []
                    for g in gs:
                        try:
                            next(g)
                            alive.append(g)
                        except StopIteration:
                            pass
                    gs = alive
                    yield

            with ExitStack() as sts:
                c_j = SB(sts, "c_j", [128, 128], F32)
                ctx.dma("Q", c_j[:], jrev, writes=["c_j"])
                c_mk = SB(sts, "c_mk", [128, 4 * 512], BF16)
                ctx.dma("Q", c_mk[:], cmasks, writes=["c_mk"])
                strip = SB(sts, "strip", [128, TC * 512], BF16)
                t1b = SB(sts, "t1b", [128, 512], F32)
                biasT = SB(sts, "biasT", [128, 512], F32)
                zt = SB(sts, "zt", [128, 512], F32)
                dent, dent_k = biasT, "biasT"
                def simav_gen(it):
                    i0 = it * 512
                    jcmax = min(TC - 1, it * 4 + 3)
                    for jc in range(jcmax + 1):
                        base = (i0 - jc * 128) + 513
                        ctx.dma("S", t1b[:],
                                RAP(tensor=bias_ext.tensor, offset=base,
                                    ap=[[1, 128], [1, 512]]),
                                reads=["bias_ext"], writes=["t1b"])
                        jp = (jc % 2) * 512
                        ctx.emit("T", lambda t: t.matmul(
                            out=P[1][:, jp:jp + 512], lhsT=c_j[:], rhs=t1b[:],
                            start=True, stop=True),
                            reads=["c_j", "t1b"], writes=[PKa(1, jc % 2)])
                        ctx.emit("S", lambda s: s.copy(
                            out=biasT[:], in_=P[1][:, jp:jp + 512]),
                            reads=[PKa(1, jc % 2)], writes=["biasT"])
                        ctx.emit("T", lambda t: t.matmul(
                            out=P[0][:, jp:jp + 512],
                            lhsT=kf[:, jc * 128:(jc + 1) * 128],
                            rhs=qf[:, it * 512:(it + 1) * 512],
                            start=True, stop=True),
                            reads=["kf", "qf"], writes=[PKa(0, jc % 2)])
                        ctx.emit("V", lambda v: v.scalar_tensor_tensor(
                            out=zt[:], in0=P[0][:, jp:jp + 512], scalar=1.0,
                            in1=biasT[:], op0=OP.mult, op1=OP.add),
                            reads=[PKa(0, jc % 2), "biasT"], writes=["zt"])
                        ctx.emit("S", lambda s: s.activation(
                            out=strip[:, jc * 512:(jc + 1) * 512], in_=zt[:],
                            func=AF.Gelu),
                            reads=["zt"], writes=["strip"])
                        ctx.emit("V", lambda v: v.scalar_tensor_tensor(
                            out=dent[:], in0=zt[:], scalar=0.0, in1=zt[:],
                            op0=OP.is_equal, op1=OP.add),
                            reads=["zt"], writes=[dent_k])
                        ctx.emit("V", lambda v: v.reciprocal(
                            out=dent[:], in_=dent[:]), reads=[dent_k], writes=[dent_k])
                        ctx.emit("V", lambda v: v.tensor_tensor(
                            out=strip[:, jc * 512:(jc + 1) * 512],
                            in0=strip[:, jc * 512:(jc + 1) * 512],
                            in1=dent[:], op=OP.mult),
                            reads=[dent_k, "strip"], writes=["strip"])
                        mq = it * 4 - jc
                        if mq < 1:
                            mi = -mq
                            ctx.emit("V", lambda v: v.tensor_tensor(
                                out=strip[:, jc * 512:(jc + 1) * 512],
                                in0=strip[:, jc * 512:(jc + 1) * 512],
                                in1=c_mk[:, mi * 512:(mi + 1) * 512], op=OP.mult),
                                reads=["c_mk", "strip"], writes=["strip"])
                    for lc in range(4):
                        tci = it * 4 + lc
                        for jc in range(tci + 1):
                            for h in range(2):
                                ctx.emit("T", lambda t: t.matmul(
                                    out=P[2][:, h * 512:(h + 1) * 512],
                                    lhsT=strip[:, jc * 512 + lc * 128:
                                               jc * 512 + (lc + 1) * 128],
                                    rhs=v_tok[:, jc * 1024 + h * 512:
                                              jc * 1024 + h * 512 + 512],
                                    start=(jc == 0), stop=(jc == tci)),
                                    reads=["strip", "v_tok"], writes=PKf(2),
                                    accum=(jc > 0))
                                yield
                        ctx.emit("S", lambda s: s.copy(
                            out=s_tok[:, tci * 1024:(tci + 1) * 1024], in_=P[2][:]),
                            reads=PKf(2), writes=["s%d" % tci])
                        yield

                def ffn_chain(it):
                    for lc in range(4):
                        tc = it * 4 + lc
                        if tc == 0:
                            for _ in ffn_quant(0):
                                yield
                        nq = ffn_quant(tc + 1) if tc + 1 < TC else None
                        for _ in interleave(ffn_mm(tc), nq):
                            yield

                pend = []
                for it in range(NIT):
                    gens = [simav_gen(it)] + pend
                    if it == 0:
                        gens.append(w8_gen())
                    weave(gens)
                    pend = [ffn_chain(it)]
                weave(pend)
        for tc in range(TC):
            ctx.dma("Q", x_out.rearrange("(tc p) d -> p tc d", p=128)[:, tc],
                    x[:, tc * 1024:(tc + 1) * 1024],
                    reads=["x%d" % tc], writes=[], slot="xout")
        ctx.eng["Q"].wait_ge(ctx.sems["D:xout"], ctx.dmaval["D:xout"])


def _host_consts(S):
    consts = {}
    consts["c_ident"] = np.eye(128, dtype=np.float32)
    consts["c_jrev"] = np.eye(128, dtype=np.float32)[::-1].copy()
    consts["c_ones1"] = np.ones((1, 128), dtype=np.float32)
    consts["c_onesk"] = np.ones((128, 1), dtype=ml_dtypes.bfloat16)
    consts["c_iota8"] = np.broadcast_to(
        np.arange(E, dtype=np.float32), (128, E)).copy()
    consts["c_iotaB"] = consts["c_iota8"] + np.float32(BIG)
    sel2 = np.zeros((2, 128), dtype=ml_dtypes.bfloat16)
    sel2[0, 0:64] = 1.0
    sel2[1, 64:128] = 1.0
    consts["c_sel2"] = sel2
    s16 = np.zeros((16, 8 * 128), dtype=np.float32)
    for c in range(8):
        for p in range(128):
            s16[2 * c + p // 64, c * 128 + p] = 1.0
    consts["c_sel16"] = s16.astype(ml_dtypes.bfloat16)
    masks = np.zeros((128, 4 * 512), dtype=np.float32)
    jj = np.arange(128)[:, None]
    ii = np.arange(512)[None, :]
    for mi in range(4):
        masks[:, mi * 512:(mi + 1) * 512] = (jj <= ii - mi * 128).astype(np.float32)
    consts["c_masks"] = masks.astype(ml_dtypes.bfloat16)
    n_ext = S + 768
    oh = np.zeros((NB, n_ext), dtype=np.float32)
    max_exact = NB // 2
    for dp in range(n_ext):
        delta = dp - 640
        if delta < 0 or delta >= S:
            continue
        if delta < max_exact:
            b = delta
        else:
            n_safe = np.float32(max(delta, 1))
            val = max_exact + int(
                np.float32(np.log(n_safe / np.float32(max_exact))
                           / np.float32(math.log(MAXD / max_exact))
                           * np.float32(NB - max_exact)))
            b = min(val, NB - 1)
        oh[b, dp] = 1.0
    consts["c_oh"] = oh
    return consts


def kernel(L0_attn_x, L0_attn_Wqkv, L0_attn_Wout, L0_attn_bout,
           L0_ffn_router_w, L0_ffn_gate_proj, L0_ffn_up_proj, L0_ffn_down_proj,
           L1_attn_wqk, L1_attn_bqk, L1_attn_gamma, L1_attn_beta,
           L1_attn_wv, L1_attn_bv, L1_attn_rel_bias,
           L1_ffn_w_int8, L1_ffn_weight_scale, L1_ffn_bias):
    x = np.ascontiguousarray(np.asarray(L0_attn_x, dtype=np.float32))
    B, T, D_ = x.shape
    assert B == NCORES and D_ == D
    nc = bass.Bass("TRN2", target_bir_lowering=False)
    build_program(nc, T)
    consts = _host_consts(T)
    common = dict(
        L0_Wqkv=np.ascontiguousarray(np.asarray(L0_attn_Wqkv, np.float32)),
        L0_Wout=np.ascontiguousarray(np.asarray(L0_attn_Wout, np.float32)),
        L0_bout=np.ascontiguousarray(np.asarray(L0_attn_bout, np.float32)),
        L0_router=np.ascontiguousarray(np.asarray(L0_ffn_router_w, np.float32)),
        L0_gate=np.ascontiguousarray(np.asarray(L0_ffn_gate_proj, np.float32)),
        L0_up=np.ascontiguousarray(np.asarray(L0_ffn_up_proj, np.float32)),
        L0_down=np.ascontiguousarray(np.asarray(L0_ffn_down_proj, np.float32)),
        L1_wqk=np.ascontiguousarray(np.asarray(L1_attn_wqk, np.float32)),
        L1_bqk=np.ascontiguousarray(np.asarray(L1_attn_bqk, np.float32)),
        L1_gamma=np.ascontiguousarray(np.asarray(L1_attn_gamma, np.float32)),
        L1_beta=np.ascontiguousarray(np.asarray(L1_attn_beta, np.float32)),
        L1_wv=np.ascontiguousarray(np.asarray(L1_attn_wv, np.float32)),
        L1_bv=np.ascontiguousarray(np.asarray(L1_attn_bv, np.float32)),
        L1_rel=np.ascontiguousarray(np.asarray(L1_attn_rel_bias, np.float32)),
        L1_w8=np.ascontiguousarray(np.asarray(L1_ffn_w_int8, np.int32)),
        L1_ws=np.asarray(L1_ffn_weight_scale, np.float32).reshape(1),
        L1_bias=np.ascontiguousarray(np.asarray(L1_ffn_bias, np.float32)),
        **consts,
    )
    in_maps = [dict(common, x_in=np.ascontiguousarray(x[c])) for c in range(NCORES)]
    res = run_bass_kernel_spmd(nc, in_maps, core_ids=list(range(NCORES)))
    out = np.stack([np.asarray(res.results[c]["x_out"]) for c in range(NCORES)],
                   axis=0)
    return out.astype(np.float32)


# version 25
# speedup vs baseline: 1.1469x; 1.0235x over previous
import math
import numpy as np
import ml_dtypes
import concourse.bass as bass
import concourse.mybir as mybir
from concourse.bass_utils import run_bass_kernel_spmd
from concourse.alu_op_type import AluOpType as OP
from concourse import dve_ops
from bass_rust import AP as RAP
from contextlib import ExitStack

F32 = mybir.dt.float32
BF16 = mybir.dt.bfloat16
I32 = mybir.dt.int32
I16 = mybir.dt.int16
AF = mybir.ActivationFunctionType
AX = mybir.AxisListType

NCORES = 8
D = 1024
DC = D // 128
E, FF = 8, 2816
FC = FF // 128
DQK, DV = 128, 1024
NB, MAXD = 32, 128
LN_EPS = 1e-5
ATT_EPS = 1e-6
RMS_EPS = 1e-6
ACT_EPS = 1e-5
MU = math.sqrt(0.5)
STD = math.sqrt(0.25 * math.pi)
FGROUPS = [2] * 11
BIG = 65536.0


class Ctx:

    def __init__(self, nc, sems, mksem):
        self.nc = nc
        self.eng = {"T": nc.tensor, "V": nc.vector, "S": nc.scalar,
                    "G": nc.gpsimd, "Q": nc.sync}
        self.sems = sems
        self.mksem = mksem
        self.count = {k: 0 for k in "TVSGQ"}
        self.ilog = {k: [] for k in "TVSGQ"}
        self.incs = {k: [] for k in "TVSGQ"}
        self.semval = {k: 0 for k in "TVSGQ"}
        self.dmaval = {}
        self.wm = {k: {} for k in "TVSGQ"}
        self.lastc = {k: None for k in "TVSGQ"}
        self.bufs = {}
        self.noburst = set()
        self.sticky = {}

    def _prod_sem(self, prod):
        kind, idx = prod
        if kind.startswith("D:"):
            return (kind, idx)
        for (i2, v2) in self.incs[kind]:
            if i2 >= idx:
                return (kind, v2)
        v = self.semval[kind] + 1
        self.semval[kind] = v
        self.ilog[kind][idx].then_inc(self.sems[kind], 1)
        self.incs[kind].append((idx, v))
        return (kind, v)

    def _wait(self, ename, semkey, val):
        if self.wm[ename].get(semkey, 0) >= val:
            return
        self.eng[ename].wait_ge(self.sems[semkey], val)
        self.wm[ename][semkey] = val

    def _collect(self, ename, reads, writes, needed):
        for b in reads:
            s = self.bufs.get(b)
            if s and s["w"] is not None:
                p = s["w"]
                if ename == "T" and p[0] == "T":
                    continue
                k, v = self._prod_sem(p)
                needed[k] = max(needed.get(k, 0), v)
        for b in writes:
            s = self.bufs.get(b)
            if s is None:
                continue
            deps = list(s["r"])
            if s["w"] is not None:
                deps.append(s["w"])
            for p in deps:
                if ename == "T" and p[0] == "T":
                    continue
                k, v = self._prod_sem(p)
                needed[k] = max(needed.get(k, 0), v)

    def emit(self, ename, fn, reads=(), writes=(), accum=False):
        needed = {}
        self._collect(ename, reads, () if accum else writes, needed)
        for k in sorted(needed):
            self._wait(ename, k, needed[k])
        instr = fn(self.eng[ename])
        idx = self.count[ename]
        self.count[ename] += 1
        self.ilog[ename].append(instr)
        self.lastc[ename] = idx
        prod = (ename, idx)
        for b in writes:
            if accum and b in self.bufs:
                self.bufs[b]["w"] = prod
            else:
                self.bufs[b] = {"w": prod, "r": []}
        for b in reads:
            self.bufs.setdefault(b, {"w": None, "r": []})["r"].append(prod)
        return instr

    def dma(self, qname, out, in_, reads=(), writes=(), slot=None):
        if slot is None:
            slot = writes[0] if writes else "store"
        semkey = "D:" + slot
        if semkey not in self.sems:
            self.sems[semkey] = self.mksem(slot)
            self.dmaval[semkey] = 0
        needed = {}
        self._collect(qname, reads, writes, needed)
        for k in sorted(needed):
            self._wait(qname, k, needed[k])
        instr = self.eng[qname].dma_start(out=out, in_=in_)
        idx = self.count[qname]
        self.count[qname] += 1
        self.ilog[qname].append(instr)
        self.dmaval[semkey] += 16
        v = self.dmaval[semkey]
        instr.then_inc(self.sems[semkey], 16)
        prod = (semkey, v)
        for b in writes:
            self.bufs[b] = {"w": prod, "r": []}
        for b in reads:
            self.bufs.setdefault(b, {"w": None, "r": []})["r"].append(prod)
        return instr

    def barrier(self):
        for k in "TVSGQ":
            if self.lastc[k] is not None:
                self._prod_sem((k, self.lastc[k]))
        targets = [(k, self.semval[k]) for k in "TVSGQ" if self.semval[k] > 0]
        targets += [(k, v) for k, v in self.dmaval.items()
                    if v > 0 and k not in self.noburst]
        for ename in "TVSGQ":
            for k, v in targets:
                self._wait(ename, k, v)
        self.bufs = {}
        for k, p in self.sticky.items():
            self.bufs[k] = {"w": p, "r": []}


def weave(gens):
    gens = [g for g in gens if g is not None]
    while gens:
        alive = []
        for g in gens:
            try:
                next(g)
                alive.append(g)
            except StopIteration:
                continue
        gens = alive


def c_hi_lo(x):
    hi = np.float32(x)
    lo = np.float32(x - float(hi))
    return float(hi), float(lo)


def build_program(nc, T):
    TC = T // 128
    NIT = T // 512
    NTB = T // 512
    S = T
    NBE = (S + 768) // 128

    dt = lambda name, shape, dty, kind: nc.dram_tensor(name, shape, dty, kind=kind).ap()
    x_in = dt("x_in", [T, D], F32, "ExternalInput")
    Wqkv = dt("L0_Wqkv", [D, 3 * D], F32, "ExternalInput")
    Wout = dt("L0_Wout", [D, D], F32, "ExternalInput")
    bout = dt("L0_bout", [D], F32, "ExternalInput")
    router_w = dt("L0_router", [D, E], F32, "ExternalInput")
    gate_p = dt("L0_gate", [E, D, FF], F32, "ExternalInput")
    up_p = dt("L0_up", [E, D, FF], F32, "ExternalInput")
    down_p = dt("L0_down", [E, FF, D], F32, "ExternalInput")
    wqk = dt("L1_wqk", [D, DQK], F32, "ExternalInput")
    bqk = dt("L1_bqk", [DQK], F32, "ExternalInput")
    gamma = dt("L1_gamma", [2, DQK], F32, "ExternalInput")
    beta = dt("L1_beta", [2, DQK], F32, "ExternalInput")
    wv = dt("L1_wv", [D, DV], F32, "ExternalInput")
    bv = dt("L1_bv", [DV], F32, "ExternalInput")
    rel_bias = dt("L1_rel", [NB], F32, "ExternalInput")
    w_int8 = dt("L1_w8", [D, D], I32, "ExternalInput")
    wscale = dt("L1_ws", [1], F32, "ExternalInput")
    bias2 = dt("L1_bias", [D], F32, "ExternalInput")
    ident = dt("c_ident", [128, 128], F32, "ExternalInput")
    jrev = dt("c_jrev", [128, 128], F32, "ExternalInput")
    ones1 = dt("c_ones1", [1, 128], F32, "ExternalInput")
    onesk = dt("c_onesk", [128, 1], BF16, "ExternalInput")
    iota8 = dt("c_iota8", [128, E], F32, "ExternalInput")
    iotaB = dt("c_iotaB", [128, E], F32, "ExternalInput")
    sel2 = dt("c_sel2", [2, 128], BF16, "ExternalInput")
    sel16 = dt("c_sel16", [16, 8 * 128], BF16, "ExternalInput")
    cmasks = dt("c_masks", [128, 896], BF16, "ExternalInput")
    ohmat = dt("c_oh", [NB, S + 768], F32, "ExternalInput")
    x_out = dt("x_out", [T, D], F32, "ExternalOutput")
    bias_ext = dt("s_bias", [S + 768], F32, "Internal")
    gcache = dt("s_gcache", [E, D, FF], BF16, "Internal")
    ucache = dt("s_ucache", [E, D, FF], BF16, "Internal")
    dcache = dt("s_dcache", [E, FF, D], BF16, "Internal")
    qkvcache = dt("s_qkvc", [D, 3 * D], BF16, "Internal")
    ocache = dt("s_oc", [D, D], BF16, "Internal")
    vcache = dt("s_vc", [D, DV], BF16, "Internal")
    qkcache = dt("s_qkc", [D, DQK], BF16, "Internal")
    rcache = dt("s_rc", [D, E], BF16, "Internal")

    with ExitStack() as stk:
        sems = {}
        for k in ["T", "V", "S", "G"]:
            sems[k] = stk.enter_context(nc.semaphore("sem_" + k))
        mksem = lambda slot: stk.enter_context(nc.semaphore("dsem_" + slot.replace(":", "_")))
        ctx = Ctx(nc, sems, mksem)

        def SB(st, name, shape, dty):
            return st.enter_context(nc.sbuf_tensor(name, shape, dty))

        x = SB(stk, "x", [128, TC * 1024], F32)
        xnT_stk = stk.enter_context(ExitStack())
        xnT = xnT_stk.enter_context(nc.sbuf_tensor("xnT", [128, DC * T], BF16,
                                                   side="right"))
        s_tok = SB(stk, "s_tok", [128, TC * 1024], BF16)
        m_st = SB(stk, "m_st", [128, TC], F32)
        r_st = SB(stk, "r_st", [128, TC], F32)
        ew = SB(stk, "ew", [128, TC * E], F32)
        c_id = SB(stk, "c_id", [128, 128], F32)
        c_o1 = SB(stk, "c_o1", [1, 128], F32)
        c_ok = SB(stk, "c_ok", [128, 1], BF16)
        xn_f = SB(stk, "xn_f", [128, 1024], F32)
        lndump = SB(stk, "lndump", [128, 1024], F32)
        ssum = SB(stk, "ssum", [128, 1], F32)
        ssq = SB(stk, "ssq", [128, 1], F32)
        varv = SB(stk, "varv", [128, 1], F32)
        P = [stk.enter_context(nc.psum_tensor("P%d" % i, [128, 1024], F32))
             for i in range(4)]
        PKa = lambda i, h: "P%d:%d" % (i, h)
        PKf = lambda i: ["P%d:0" % i, "P%d:1" % i]
        PSL = lambda i, s: "P%d:0s%d" % (i, s)
        PSLall = lambda i: [PSL(i, s) for s in range(4)]

        blk = stk.enter_context(nc.Block())

        ctx.dma("Q", c_id[:], ident, writes=["c_id"])
        ctx.dma("Q", c_o1[:], ones1, writes=["c_o1"])
        ctx.dma("G", c_ok[:], onesk, writes=["c_ok"])

        ctx.dma("Q", x[:].rearrange("p (tc d) -> p tc d", d=1024),
                x_in.rearrange("(tc p) d -> p tc d", p=128),
                writes=["x%d" % t for t in range(TC)], slot="xld")
        ctx._wait("G", "D:xld", 16)
        ctx.dma("G", rcache, router_w, writes=["rcache"], slot="rcw")
        ctx.dma("G", qkvcache, Wqkv, writes=["qkvcache"], slot="qkvcw")
        ctx.dma("G", ocache, Wout, writes=["ocache"], slot="ocw")
        ctx.dma("G", qkcache, wqk, writes=["qkcache"], slot="qkcw")
        ctx.dma("G", vcache, wv, writes=["vcache"], slot="vcw")
        ctx.sems["MP"] = ctx.mksem("MP")

        def bcast_row(st, name, row_ap, n, pi):
            t = SB(st, name, [128, n], F32)
            bcast_row_into(t, st, name, row_ap, n, pi)
            return t

        def bcast_row_into(t, st, name, row_ap, n, pi):
            rrow = SB(st, name + "_r", [1, n], F32)
            ctx.dma("Q", rrow[:], row_ap, writes=[name + "_r"])
            for h in range(0, n, 512):
                w = min(512, n - h)
                pb = PKa(pi, (h % 1024) // 512)
                ctx.emit("T", lambda t_: t_.matmul(
                    out=P[pi][0:128, h % 1024:h % 1024 + w], lhsT=c_o1[:],
                    rhs=rrow[0:1, h:h + w], start=True, stop=True),
                    reads=["c_o1", name + "_r"], writes=[pb])
                ctx.emit("V", lambda v: v.tensor_copy(
                    out=t[:, h:h + w], in_=P[pi][0:128, h % 1024:h % 1024 + w]),
                    reads=[pb], writes=[name])
            return t

        var16 = SB(stk, "var16", [128, TC], F32)
        stot_a = SB(stk, "stot_a", [128, TC], F32)
        cq2_a = SB(stk, "cq2_a", [128, TC], F32)
        amx_a = SB(stk, "amx_a", [128, TC], F32)
        ct_a = SB(stk, "ct_a", [128, TC], F32)
        wsb2 = SB(stk, "wsb2", [128, 1], F32)

        def ln_stats():
            ctx.emit("V", lambda v: v.tensor_reduce(
                out=m_st[:], in_=x[:].rearrange("p (tc d) -> p tc d", d=1024),
                axis=AX.X, op=OP.add),
                reads=["x%d" % t for t in range(TC)], writes=["m_st"])
            for tc in range(TC):
                ctx.emit("S", lambda s: s.activation(
                    out=lndump[:], in_=x[:, tc * 1024:(tc + 1) * 1024],
                    func=AF.Square, accum_out=r_st[:, tc:tc + 1]),
                    reads=["x%d" % tc], writes=["lndump", "r_st"])
            ctx.emit("V", lambda v: v.tensor_scalar(
                out=m_st[:], in0=m_st[:], scalar1=1.0 / 1024.0, scalar2=None,
                op0=OP.mult), reads=["m_st"], writes=["m_st"])
            ctx.emit("V", lambda v: v.tensor_scalar(
                out=r_st[:], in0=r_st[:], scalar1=1.0 / 1024.0, scalar2=None,
                op0=OP.mult), reads=["r_st"], writes=["r_st"])
            ctx.emit("V", lambda v: v.tensor_tensor(
                out=var16[:], in0=m_st[:], in1=m_st[:], op=OP.mult),
                reads=["m_st"], writes=["var16"])
            ctx.emit("V", lambda v: v.tensor_tensor(
                out=var16[:], in0=r_st[:], in1=var16[:], op=OP.subtract),
                reads=["r_st", "var16"], writes=["var16"])
            ctx.emit("V", lambda v: v.tensor_scalar(
                out=var16[:], in0=var16[:], scalar1=1.0, scalar2=LN_EPS,
                op0=OP.mult, op1=OP.add), reads=["var16"], writes=["var16"])
            ctx.emit("V", lambda v: v.reciprocal(out=var16[:], in_=var16[:]),
                     reads=["var16"], writes=["var16"])
            ctx.emit("S", lambda s: s.activation(
                out=r_st[:], in_=var16[:], func=AF.Sqrt),
                reads=["var16"], writes=["r_st"])

        def ln_xn_gen(tc, rb=None, sc_all=None, sc_key=None, pingpong=False):
            xs = x[:, tc * 1024:(tc + 1) * 1024]
            ctx.emit("V", lambda v: v.tensor_scalar(
                out=xn_f[:], in0=xs, scalar1=m_st[:, tc:tc + 1],
                scalar2=r_st[:, tc:tc + 1], op0=OP.subtract, op1=OP.mult),
                reads=["x%d" % tc, "m_st", "r_st"], writes=["xn_f"])
            yield
            for dc in range(DC):
                if pingpong:
                    bk = dc % 2
                    pcol = bk * 512 + (dc % 4) // 2 * 128
                else:
                    bk = 0
                    pcol = (dc % 4) * 128
                ctx.emit("T", lambda t: t.transpose(
                    out=P[2][:, pcol:pcol + 128],
                    in_=xn_f[:, dc * 128:(dc + 1) * 128], identity=c_id[:]),
                    reads=["xn_f", "c_id"], writes=[PKa(2, bk)])
                yield
                ctx.emit("S", lambda s: s.copy(
                    out=xnT[:, dc * T + tc * 128: dc * T + (tc + 1) * 128],
                    in_=P[2][:, pcol:pcol + 128]),
                    reads=[PKa(2, bk)], writes=["xnT"])
                yield
                if rb is not None:
                    ctx.emit("T", lambda t: t.matmul(
                        out=P[2][0:128, 512:512 + E],
                        lhsT=xnT[:, dc * T + tc * 128: dc * T + (tc + 1) * 128],
                        rhs=rb[:, dc * E:(dc + 1) * E],
                        start=(dc == 0), stop=(dc == DC - 1)),
                        reads=["xnT", "rb"], writes=[PKa(2, 1)], accum=(dc > 0))
                    yield
            if rb is not None:
                ctx.emit("V", lambda v: v.tensor_copy(
                    out=sc_all[:, tc * E:(tc + 1) * E],
                    in_=P[2][0:128, 512:512 + E]),
                    reads=[PKa(2, 1)], writes=[sc_key])
                yield

        def ln_xn(tc, rb=None, sc_all=None, sc_key=None):
            for _ in ln_xn_gen(tc, rb, sc_all, sc_key):
                pass

        if True:
            st0 = stk
            rb = SB(st0, "rb", [128, DC * E], BF16)
            sc_all = SB(st0, "sc_all", [128, TC * E], F32)
            ctx.dma("Q", rb[:].rearrange("p (dc e) -> p dc e", e=E),
                    rcache.rearrange("(dc p) e -> p dc e", p=128),
                    reads=["rcache"], writes=["rb"])
            m1n = SB(st0, "m1n", [128, 1], F32)
            m2n = SB(st0, "m2n", [128, 1], F32)
            eqt = SB(st0, "eqt", [128, E], F32)
            at = SB(st0, "at", [128, E], F32)
            i1t = SB(st0, "i1t", [128, 1], F32)
            mk1 = SB(st0, "mk1", [128, E], F32)
            mk2 = SB(st0, "mk2", [128, E], F32)
            smk = SB(st0, "smk", [128, E], F32)
            e2t = SB(st0, "e2t", [128, 1], F32)
            w1t = SB(st0, "w1t", [128, 1], F32)
            w2t = SB(st0, "w2t", [128, 1], F32)
            ewt = SB(st0, "ewt", [128, E], F32)
            c_i8 = SB(st0, "c_i8", [128, E], F32)
            c_iB = SB(st0, "c_iB", [128, E], F32)
            ctx.dma("Q", c_i8[:], iota8, writes=["c_i8"])
            ctx.dma("Q", c_iB[:], iotaB, writes=["c_iB"])

            def topk_tc(tc):
                scs = sc_all[:, tc * E:(tc + 1) * E]
                ctx.emit("V", lambda v: v.tensor_reduce(
                    out=m1n[:], in_=scs, axis=AX.X, op=OP.max, negate=True),
                    reads=["sc_all"], writes=["m1n"])
                ctx.emit("V", lambda v: v.tensor_scalar(
                    out=eqt[:], in0=scs, scalar1=m1n[:], scalar2=0.0,
                    op0=OP.add, op1=OP.is_equal),
                    reads=["sc_all", "m1n"], writes=["eqt"])
                ctx.emit("V", lambda v: v.scalar_tensor_tensor(
                    out=at[:], in0=eqt[:], scalar=-BIG, in1=c_iB[:],
                    op0=OP.mult, op1=OP.add), reads=["eqt", "c_iB"], writes=["at"])
                ctx.emit("V", lambda v: v.tensor_reduce(
                    out=i1t[:], in_=at[:], axis=AX.X, op=OP.min),
                    reads=["at"], writes=["i1t"])
                ctx.emit("V", lambda v: v.tensor_scalar(
                    out=mk1[:], in0=c_i8[:], scalar1=i1t[:], scalar2=0.0,
                    op0=OP.subtract, op1=OP.is_equal),
                    reads=["c_i8", "i1t"], writes=["mk1"])
                ctx.emit("V", lambda v: v.scalar_tensor_tensor(
                    out=smk[:], in0=mk1[:], scalar=-BIG, in1=scs,
                    op0=OP.mult, op1=OP.add), reads=["mk1", "sc_all"], writes=["smk"])
                ctx.emit("V", lambda v: v.tensor_reduce(
                    out=m2n[:], in_=smk[:], axis=AX.X, op=OP.max, negate=True),
                    reads=["smk"], writes=["m2n"])
                ctx.emit("V", lambda v: v.tensor_scalar(
                    out=eqt[:], in0=smk[:], scalar1=m2n[:], scalar2=0.0,
                    op0=OP.add, op1=OP.is_equal), reads=["smk", "m2n"], writes=["eqt"])
                ctx.emit("V", lambda v: v.scalar_tensor_tensor(
                    out=at[:], in0=eqt[:], scalar=-BIG, in1=c_iB[:],
                    op0=OP.mult, op1=OP.add), reads=["eqt", "c_iB"], writes=["at"])
                ctx.emit("V", lambda v: v.tensor_reduce(
                    out=i1t[:], in_=at[:], axis=AX.X, op=OP.min),
                    reads=["at"], writes=["i1t"])
                ctx.emit("V", lambda v: v.tensor_scalar(
                    out=mk2[:], in0=c_i8[:], scalar1=i1t[:], scalar2=0.0,
                    op0=OP.subtract, op1=OP.is_equal),
                    reads=["c_i8", "i1t"], writes=["mk2"])
                ctx.emit("V", lambda v: v.tensor_scalar(
                    out=e2t[:], in0=m1n[:], scalar1=m2n[:], scalar2=None,
                    op0=OP.subtract), reads=["m1n", "m2n"], writes=["e2t"])
                ctx.emit("S", lambda s: s.activation(
                    out=e2t[:], in_=e2t[:], func=AF.Exp),
                    reads=["e2t"], writes=["e2t"])
                ctx.emit("V", lambda v: v.tensor_scalar(
                    out=w1t[:], in0=e2t[:], scalar1=1.0, scalar2=None, op0=OP.add),
                    reads=["e2t"], writes=["w1t"])
                ctx.emit("V", lambda v: v.reciprocal(out=w1t[:], in_=w1t[:]),
                         reads=["w1t"], writes=["w1t"])
                ctx.emit("V", lambda v: v.tensor_scalar(
                    out=w2t[:], in0=e2t[:], scalar1=w1t[:], scalar2=None,
                    op0=OP.mult), reads=["e2t", "w1t"], writes=["w2t"])
                ctx.emit("V", lambda v: v.tensor_scalar(
                    out=ewt[:], in0=mk2[:], scalar1=w2t[:], scalar2=None,
                    op0=OP.mult), reads=["mk2", "w2t"], writes=["ewt"])
                ctx.emit("V", lambda v: v.scalar_tensor_tensor(
                    out=ew[:, tc * E:(tc + 1) * E], in0=mk1[:], scalar=w1t[:],
                    in1=ewt[:], op0=OP.mult, op1=OP.add),
                    reads=["mk1", "w1t", "ewt"], writes=["ew"])


        stAB = stk.enter_context(ExitStack())
        kvb = SB(stAB, "kvb", [128, 8 * 128], BF16)
        scb = SB(stAB, "scb", [128, 16], BF16)
        with ExitStack() as stA:
            wkv = SB(stA, "wkv", [128, DC * 2048], BF16)
            ctx.dma("Q", wkv[:].rearrange("p (dc f) -> p dc f", f=2048),
                    qkvcache.rearrange("(dc p) f -> p dc f", p=128)[:, :, 1024:3072],
                    reads=["qkvcache"], writes=["wkv"])
            ctx._wait("G", "D:wkv", 16)
            for e_ in range(E):
                ctx.dma("G", gcache[e_], gate_p[e_], writes=["gcache%d" % e_],
                        slot="gcw%d" % e_)
                ctx.dma("G", ucache[e_], up_p[e_], writes=["ucache%d" % e_],
                        slot="ucw%d" % e_)
                for kk in ("gcache%d" % e_, "ucache%d" % e_):
                    ctx.sticky[kk] = ctx.bufs[kk]["w"]
                ctx.noburst.update({"D:gcw%d" % e_, "D:ucw%d" % e_})
            k_bfs = [SB(stA, "k_bf%d" % i, [128, 1024], BF16) for i in range(2)]
            v_bfs = [SB(stA, "v_bf%d" % i, [128, 1024], BF16) for i in range(2)]
            eltmps = [SB(stA, "eltmp%d" % i, [128, 512], F32) for i in range(2)]
            eltmp2s = [SB(stA, "eltmp2%d" % i, [128, 512], F32) for i in range(2)]
            ksum = SB(stA, "ksum", [1, 1024], F32)
            ctx.emit("V", lambda v: v.memset(ksum[:], 0.0), writes=["ksum"])

            def projkv_gen(tc):
                k_bf, k_bfk = k_bfs[tc % 2], "k_bf%d" % (tc % 2)
                v_bf, v_bfk = v_bfs[tc % 2], "v_bf%d" % (tc % 2)
                for part, dst, dstk in ((0, k_bf, k_bfk), (1, v_bf, v_bfk)):
                    for h in range(2):
                        pt, ptk = P[part], PKa(part, h)
                        pcol = h * 512
                        eltmp, eltmp_k = eltmps[h], "eltmp%d" % h
                        eltmp2, eltmp2_k = eltmp2s[h], "eltmp2%d" % h
                        for dc in range(DC):
                            ctx.emit("T", lambda t: t.matmul(
                                out=pt[:, pcol:pcol + 512],
                                lhsT=xnT[:, dc * T + tc * 128: dc * T + (tc + 1) * 128],
                                rhs=wkv[:, dc * 2048 + part * 1024 + pcol:
                                        dc * 2048 + part * 1024 + pcol + 512],
                                start=(dc == 0), stop=(dc == DC - 1)),
                                reads=["xnT", "wkv"], writes=[ptk], accum=(dc > 0))
                            yield
                        if part == 0:
                            ctx.emit("V", lambda v: v.tensor_scalar(
                                out=eltmp[:], in0=pt[:, pcol:pcol + 512],
                                scalar1=0.0, scalar2=None, op0=OP.min),
                                reads=[ptk], writes=[eltmp_k])
                            yield
                            ctx.emit("S", lambda s: s.activation(
                                out=eltmp2[:], in_=eltmp[:], func=AF.Exp),
                                reads=[eltmp_k], writes=[eltmp2_k])
                            yield
                            ctx.emit("V", lambda v: v.scalar_tensor_tensor(
                                out=dst[:, pcol:pcol + 512],
                                in0=pt[:, pcol:pcol + 512], scalar=0.0,
                                in1=eltmp2[:], op0=OP.max, op1=OP.add),
                                reads=[ptk, eltmp2_k], writes=[dstk])
                            yield
                        else:
                            ctx.emit("S", lambda s: s.copy(
                                out=dst[:, pcol:pcol + 512],
                                in_=pt[:, pcol:pcol + 512]),
                                reads=[ptk], writes=[dstk])
                            yield
                for c in range(8):
                    st_ = (tc == 0 and c in (0, 4))
                    sp_ = (tc == TC - 1 and c in (3, 7))
                    ctx.emit("T", lambda t: t.matmul(
                        out=P[3][:, c * 128:(c + 1) * 128],
                        lhsT=k_bf[:, c * 128:(c + 1) * 128],
                        rhs=v_bf[:, c * 128:(c + 1) * 128],
                        start=st_, stop=sp_),
                        reads=[k_bfk, v_bfk], writes=PKf(3),
                        accum=not (tc == 0 and c == 0))
                    yield
                for h in range(2):
                    ctx.emit("T", lambda t: t.matmul(
                        out=P[2][0:1, 0:512], lhsT=c_ok[:],
                        rhs=k_bf[:, h * 512:(h + 1) * 512], start=True, stop=True),
                        reads=["c_ok", k_bfk], writes=[PKa(2, 0)])
                    yield
                    ctx.emit("V", lambda v: v.scalar_tensor_tensor(
                        out=ksum[0:1, h * 512:(h + 1) * 512],
                        in0=P[2][0:1, 0:512], scalar=1.0,
                        in1=ksum[0:1, h * 512:(h + 1) * 512],
                        op0=OP.mult, op1=OP.add),
                        reads=[PKa(2, 0), "ksum"], writes=["ksum"])
                    yield
                topk_tc(tc)

            ln_stats()
            ln_xn(0, rb=rb, sc_all=sc_all, sc_key="sc_all")
            for tc in range(TC):
                nxt = (ln_xn_gen(tc + 1, rb, sc_all, "sc_all")
                       if tc + 1 < TC else None)
                weave([nxt, projkv_gen(tc)])
            ctx.emit("V", lambda v: v.memset(kvb[:], 0.0), writes=["kvb"])
            for c in range(8):
                ctx.emit("V", lambda v: v.tensor_copy(
                    out=kvb[0:64, c * 128:c * 128 + 64],
                    in_=P[3][0:64, c * 128:c * 128 + 64]),
                    reads=[PKa(3, c // 4)], writes=["kvb"])
                ctx.emit("V", lambda v: v.tensor_copy(
                    out=kvb[64:128, c * 128 + 64:c * 128 + 128],
                    in_=P[3][64:128, c * 128 + 64:c * 128 + 128]),
                    reads=[PKa(3, c // 4)], writes=["kvb"])
            ksb = SB(stA, "ksb", [1, 1024], BF16)
            ctx.emit("V", lambda v: v.tensor_copy(out=ksb[:], in_=ksum[:]),
                     reads=["ksum"], writes=["ksb"])
            ctx.emit("V", lambda v: v.memset(scb[:], 0.0), writes=["scb"])
            for c in range(8):
                ctx.dma("Q", scb[0:64, 2 * c:2 * c + 1],
                        ksb[0:1, c * 128:c * 128 + 64],
                        reads=["ksb"], writes=["scb"])
                ctx.dma("Q", scb[64:128, 2 * c + 1:2 * c + 2],
                        ksb[0:1, c * 128 + 64:c * 128 + 128],
                        reads=["ksb"], writes=["scb"])
            ctx.barrier()
            _mp = ctx.emit("V", lambda v: v.nop())
            _mp.then_inc(ctx.sems["MP"], 1)

        with ExitStack() as stB:
            c_s2 = SB(stB, "c_s2", [2, 128], BF16)
            ctx.dma("Q", c_s2[:], sel2, writes=["c_s2"])
            wqo = SB(stB, "wqo", [128, DC * 1024], BF16)
            bout_b = bcast_row(stB, "bout_b", bout.rearrange("(o d) -> o d", o=1),
                               D, 2)
            q_f = SB(stB, "q_f", [128, 1024], F32)
            qT = SB(stB, "qT", [128, DC * 512], BF16)
            att_fm = SB(stB, "att_fm", [128, 8 * 512], BF16)
            r2ab = [SB(stB, "r2ab%d" % i, [2, 512], BF16) for i in range(2)]
            r2af = [SB(stB, "r2af%d" % i, [2, 512], F32) for i in range(2)]
            r2ts = [SB(stB, "r2t%d" % i, [128, 512], F32) for i in range(2)]
            eltmp = SB(stB, "eltmpB", [128, 512], F32)
            eltmp2 = SB(stB, "eltmpB2", [128, 512], F32)
            for it in range(NIT):
                ctx.dma("Q", wqo[:].rearrange("p (dc f) -> p dc f", f=1024),
                        qkvcache.rearrange("(dc p) f -> p dc f", p=128)[:, :, 0:1024],
                        reads=["qkvcache"], writes=["wqo"])
                for lc in range(4):
                    tc = it * 4 + lc
                    for h in range(2):
                        for dc in range(DC):
                            ctx.emit("T", lambda t: t.matmul(
                                out=P[h][:, 0:512],
                                lhsT=xnT[:, dc * T + tc * 128: dc * T + (tc + 1) * 128],
                                rhs=wqo[:, dc * 1024 + h * 512: dc * 1024 + h * 512 + 512],
                                start=(dc == 0), stop=(dc == DC - 1)),
                                reads=["xnT", "wqo"], writes=[PKa(h, 0)], accum=(dc > 0))
                        ctx.emit("V", lambda v: v.tensor_scalar(
                            out=eltmp[:], in0=P[h][:, 0:512], scalar1=0.0,
                            scalar2=None, op0=OP.min), reads=[PKa(h, 0)], writes=["eltmpB"])
                        ctx.emit("S", lambda s: s.activation(
                            out=eltmp2[:], in_=eltmp[:], func=AF.Exp),
                            reads=["eltmpB"], writes=["eltmpB2"])
                        ctx.emit("V", lambda v: v.scalar_tensor_tensor(
                            out=q_f[:, h * 512:(h + 1) * 512], in0=P[h][:, 0:512],
                            scalar=0.0, in1=eltmp2[:], op0=OP.max, op1=OP.add),
                            reads=[PKa(h, 0), "eltmpB2"], writes=["q_f"])
                    for dc in range(DC):
                        bk = dc % 2
                        pcol = bk * 512 + ((dc % 4) // 2) * 128
                        ctx.emit("T", lambda t: t.transpose(
                            out=P[2][:, pcol:pcol + 128],
                            in_=q_f[:, dc * 128:(dc + 1) * 128], identity=c_id[:]),
                            reads=["q_f", "c_id"], writes=[PKa(2, bk)])
                        ctx.emit("S", lambda s: s.copy(
                            out=qT[:, dc * 512 + lc * 128: dc * 512 + (lc + 1) * 128],
                            in_=P[2][:, pcol:pcol + 128]),
                            reads=[PKa(2, bk)], writes=["qT"])
                ctx.dma("S", wqo[:].rearrange("p (fc f) -> p fc f", f=1024),
                        ocache.rearrange("(fc p) f -> p fc f", p=128),
                        reads=["ocache"], writes=["wqo"])
                for c in range(8):
                    cp = c % 2
                    pq, pqk = (P[2], PKa(2, 1)) if cp == 0 else (P[3], PKa(3, 1))
                    ctx.emit("T", lambda t: t.matmul(
                        out=pq[0:2, 512:1024], lhsT=scb[:, 2 * c:2 * c + 2],
                        rhs=qT[:, c * 512:(c + 1) * 512], start=True, stop=True),
                        reads=["scb", "qT"], writes=[pqk])
                    ctx.emit("V", lambda v: v.tensor_scalar(
                        out=r2af[cp][:], in0=pq[0:2, 512:1024], scalar1=ATT_EPS,
                        scalar2=None, op0=OP.add),
                        reads=[pqk], writes=["r2af%d" % cp])
                    ctx.emit("V", lambda v: v.reciprocal(
                        out=r2af[cp][:], in_=r2af[cp][:]),
                        reads=["r2af%d" % cp], writes=["r2af%d" % cp])
                    ctx.emit("V", lambda v: v.tensor_copy(
                        out=r2ab[cp][:], in_=r2af[cp][:]),
                        reads=["r2af%d" % cp], writes=["r2ab%d" % cp])
                    ctx.emit("T", lambda t: t.matmul(
                        out=P[1][:, cp * 512:cp * 512 + 512],
                        lhsT=c_s2[:], rhs=r2ab[cp][:],
                        start=True, stop=True),
                        reads=["c_s2", "r2ab%d" % cp], writes=[PKa(1, cp)])
                    ctx.emit("V", lambda v: v.tensor_copy(
                        out=r2ts[cp][:], in_=P[1][:, cp * 512:cp * 512 + 512]),
                        reads=[PKa(1, cp)], writes=["r2t%d" % cp])
                    ctx.emit("T", lambda t: t.matmul(
                        out=P[0][:, cp * 512:cp * 512 + 512],
                        lhsT=kvb[:, c * 128:(c + 1) * 128],
                        rhs=qT[:, c * 512:(c + 1) * 512], start=True, stop=True),
                        reads=["kvb", "qT"], writes=[PKa(0, cp)])
                    ctx.emit("V", lambda v: v.tensor_tensor(
                        out=att_fm[:, c * 512:(c + 1) * 512],
                        in0=P[0][:, cp * 512:cp * 512 + 512],
                        in1=r2ts[cp][:], op=OP.mult),
                        reads=[PKa(0, cp), "r2t%d" % cp], writes=["att_fm"])
                for lc in range(4):
                    tc = it * 4 + lc
                    for h in range(2):
                        for fc in range(DC):
                            ctx.emit("T", lambda t: t.matmul(
                                out=P[3][:, h * 512:(h + 1) * 512],
                                lhsT=att_fm[:, fc * 512 + lc * 128:
                                            fc * 512 + (lc + 1) * 128],
                                rhs=wqo[:, fc * 1024 + h * 512:
                                        fc * 1024 + h * 512 + 512],
                                start=(fc == 0), stop=(fc == DC - 1)),
                                reads=["att_fm", "wqo"], writes=[PKa(3, h)],
                                accum=(fc > 0))
                        ctx.emit("V", lambda v: v.scalar_tensor_tensor(
                            out=s_tok[:, tc * 1024 + h * 512:
                                      tc * 1024 + (h + 1) * 512],
                            in0=P[3][:, h * 512:(h + 1) * 512], scalar=1.0,
                            in1=bout_b[:, h * 512:(h + 1) * 512],
                            op0=OP.mult, op1=OP.add),
                            reads=[PKa(3, h), "bout_b"], writes=["s%d" % tc])
            ctx.barrier()
        stAB.close()
        _mp2 = ctx.emit("V", lambda v: v.nop())
        _mp2.then_inc(ctx.sems["MP"], 1)

        ctx._wait("G", "MP", 2)
        for e_ in range(E):
            ctx.dma("G", dcache[e_], down_p[e_], writes=["dcache%d" % e_],
                    slot="dcw%d" % e_)
            ctx.sticky["dcache%d" % e_] = ctx.bufs["dcache%d" % e_]["w"]
            ctx.noburst.add("D:dcw%d" % e_)

        with ExitStack() as stM:
            h_w = SB(stM, "h_w", [128, FC * 512], BF16)
            gacc = SB(stM, "gacc", [128, 4 * 1024], F32)
            gbufs = [SB(stM, "gbuf%d" % i, [128, DC * 256], BF16) for i in range(2)]
            ubufs = [SB(stM, "ubuf%d" % i, [128, DC * 256], BF16) for i in range(2)]
            dbufs = [SB(stM, "dbuf%d" % i, [128, 2 * 1024], BF16) for i in range(2)]
            gsils = [xn_f[:, 0:512], xn_f[:, 512:1024]]
            gtmp, gtmp_k = lndump, "lndump"
            GOFF = [2 * i for i in range(11)]
            DGROUPS = [2] * 11
            GOFFD = [2 * i for i in range(11)]

            units = []
            for tb in range(NTB):
                for e in range(E):
                    for g in range(len(FGROUPS)):
                        units.append(("gu", tb, e, g))
                    for g in range(len(DGROUPS)):
                        units.append(("dn", tb, e, g))
                units.append(("fin", tb, 0, 0))

            gu_ct = [0]
            dn_ct = [0]

            def issue_dma(u):
                kind, tb, e, g = u
                if kind == "fin":
                    return
                if kind == "gu":
                    ng = FGROUPS[g]
                    fc0 = GOFF[g]
                    w_ = ng * 128
                    sl = gu_ct[0] % 2
                    gu_ct[0] += 1
                    gb, ub = gbufs[sl], ubufs[sl]
                    gk, uk = "gbuf%d" % sl, "ubuf%d" % sl
                    ctx.dma("Q", gb[:, 0:DC * w_].rearrange(
                        "p (dc f) -> p dc f", f=w_),
                        gcache.rearrange("e (dc p) f -> e p dc f", p=128)
                        [e, :, :, fc0 * 128:(fc0 + ng) * 128],
                        reads=["gcache%d" % e], writes=[gk], slot=gk)
                    ctx.dma("S", ub[:, 0:DC * w_].rearrange(
                        "p (dc f) -> p dc f", f=w_),
                        ucache.rearrange("e (dc p) f -> e p dc f", p=128)
                        [e, :, :, fc0 * 128:(fc0 + ng) * 128],
                        reads=["ucache%d" % e], writes=[uk], slot=uk)
                else:
                    ng = DGROUPS[g]
                    fc0 = GOFFD[g]
                    sl = dn_ct[0] % 2
                    dn_ct[0] += 1
                    db = dbufs[sl]
                    dk = "dbuf%d" % sl
                    ctx.dma("S" if g % 2 else "Q",
                            db[:, 0:ng * 1024].rearrange(
                                "p (fc f) -> p fc f", f=1024),
                            dcache.rearrange("e (fc p) f -> e p fc f", p=128)
                            [e, :, fc0:fc0 + ng, :],
                            reads=["dcache%d" % e], writes=[dk], slot=dk)

            cg_ct = [0]
            cd_ct = [0]

            def emit_compute(u):
                kind, tb, e, g = u
                if kind == "fin":
                    for lc in range(4):
                        tc = tb * 4 + lc
                        ctx.emit("S", lambda s: s.activation(
                            out=gtmp[:], in_=gacc[:, lc * 1024:(lc + 1) * 1024],
                            func=AF.Silu), reads=["gacc"], writes=[gtmp_k])
                        ctx.emit("V", lambda v: v.tensor_tensor(
                            out=gtmp[:], in0=gtmp[:],
                            in1=s_tok[:, tc * 1024:(tc + 1) * 1024], op=OP.mult),
                            reads=[gtmp_k, "s%d" % tc], writes=[gtmp_k])
                        ctx.emit("V", lambda v: v.tensor_tensor(
                            out=x[:, tc * 1024:(tc + 1) * 1024],
                            in0=x[:, tc * 1024:(tc + 1) * 1024], in1=gtmp[:],
                            op=OP.add), reads=[gtmp_k, "x%d" % tc],
                            writes=["x%d" % tc])
                    return
                if kind == "gu":
                    if tb == 0 and g == 0:
                        _m = ctx.emit("V", lambda v: v.nop())
                        _m.then_inc(ctx.sems["MP"], 1)
                    ng = FGROUPS[g]
                    fc0 = GOFF[g]
                    w_ = ng * 128
                    sl = cg_ct[0] % 2
                    cg_ct[0] += 1
                    gb, ub = gbufs[sl], ubufs[sl]
                    gk, uk = "gbuf%d" % sl, "ubuf%d" % sl
                    for fl in range(ng):
                        fc = fc0 + fl
                        pcg = (fc % 2) * 512
                        gsb = gsils[fc % 2]
                        gsk = "xn_f:%d" % (fc % 2)
                        for dc in range(DC):
                            ctx.emit("T", lambda t: t.matmul(
                                out=P[0][:, pcg:pcg + 512],
                                lhsT=gb[:, dc * w_ + fl * 128: dc * w_ + (fl + 1) * 128],
                                rhs=xnT[:, dc * T + tb * 512: dc * T + (tb + 1) * 512],
                                start=(dc == 0), stop=(dc == DC - 1)),
                                reads=[gk, "xnT"], writes=[PKa(0, fc % 2)],
                                accum=(dc > 0))
                        ctx.emit("S", lambda s: s.activation(
                            out=gsb, in_=P[0][:, pcg:pcg + 512], func=AF.Silu),
                            reads=[PKa(0, fc % 2)], writes=[gsk])
                        for dc in range(DC):
                            ctx.emit("T", lambda t: t.matmul(
                                out=P[1][:, pcg:pcg + 512],
                                lhsT=ub[:, dc * w_ + fl * 128: dc * w_ + (fl + 1) * 128],
                                rhs=xnT[:, dc * T + tb * 512: dc * T + (tb + 1) * 512],
                                start=(dc == 0), stop=(dc == DC - 1)),
                                reads=[uk, "xnT"], writes=[PKa(1, fc % 2)],
                                accum=(dc > 0))
                        ctx.emit("V", lambda v: v.tensor_tensor(
                            out=h_w[:, fc * 512:(fc + 1) * 512],
                            in0=P[1][:, pcg:pcg + 512], in1=gsb, op=OP.mult),
                            reads=[PKa(1, fc % 2), gsk], writes=["h_w"])
                else:
                    ng = DGROUPS[g]
                    fc0 = GOFFD[g]
                    sl = cd_ct[0] % 2
                    cd_ct[0] += 1
                    db = dbufs[sl]
                    dk = "dbuf%d" % sl
                    for fl in range(ng):
                        fc = fc0 + fl
                        for lc in range(4):
                            for h in range(2):
                                ctx.emit("T", lambda t: t.matmul(
                                    out=P[lc][:, h * 512:(h + 1) * 512],
                                    lhsT=h_w[:, fc * 512 + lc * 128:
                                             fc * 512 + (lc + 1) * 128],
                                    rhs=db[:, fl * 1024 + h * 512:
                                           fl * 1024 + h * 512 + 512],
                                    start=(fc == 0), stop=(fc == FC - 1)),
                                    reads=["h_w", dk], writes=[PKa(lc, h)],
                                    accum=(fc > 0))
                    if g == len(DGROUPS) - 1:
                        for lc in range(4):
                            tc = tb * 4 + lc
                            wecol = ew[:, tc * E + e: tc * E + e + 1]
                            if e == 0:
                                ctx.emit("V", lambda v: v.tensor_scalar(
                                    out=gacc[:, lc * 1024:(lc + 1) * 1024],
                                    in0=P[lc][:], scalar1=wecol, scalar2=None,
                                    op0=OP.mult),
                                    reads=PKf(lc) + ["ew"], writes=["gacc"])
                            else:
                                ctx.emit("V", lambda v: v.scalar_tensor_tensor(
                                    out=gacc[:, lc * 1024:(lc + 1) * 1024],
                                    in0=P[lc][:], scalar=wecol,
                                    in1=gacc[:, lc * 1024:(lc + 1) * 1024],
                                    op0=OP.mult, op1=OP.add),
                                    reads=PKf(lc) + ["ew", "gacc"], writes=["gacc"])

            pend = 0
            for i in range(len(units)):
                while pend <= i + 1 and pend < len(units):
                    issue_dma(units[pend])
                    pend += 1
                emit_compute(units[i])
            ctx.barrier()

        c2f = math.sqrt(128.0) / STD
        c3f = MU / STD
        ln_stats()
        ctx.dma("Q", wsb2[:], RAP(tensor=wscale.tensor, offset=0,
                                  ap=[[0, 128], [1, 1]]), writes=["wsb2"])
        with ExitStack() as stb:
            c_ohT = SB(stb, "c_ohT", [NB, S + 768], F32)
            ctx.dma("Q", c_ohT[:], ohmat, writes=["c_ohT"])
            relb = SB(stb, "relb", [NB, 1], F32)
            ctx.dma("Q", relb[:], rel_bias.rearrange("(b o) -> b o", o=1),
                    writes=["relb"])
            btmp = SB(stb, "btmp", [128, 128], F32)
            bt2 = SB(stb, "bt2", [128, 128], F32)

            def biasext_gen():
                ctx.emit("V", lambda v: v.memset(btmp[:], 0.0), writes=["btmp"])
                yield
                for ch in range(NBE):
                    ctx.emit("T", lambda t: t.matmul(
                        out=P[1][:, 0:1], lhsT=c_ohT[:, ch * 128:(ch + 1) * 128],
                        rhs=relb[:], start=True, stop=True),
                        reads=["c_ohT", "relb"], writes=[PKa(1, 0)])
                    yield
                    ctx.emit("V", lambda v: v.tensor_scalar(
                        out=btmp[:, ch:ch + 1], in0=P[1][:, 0:1], scalar1=c2f,
                        scalar2=-c3f, op0=OP.mult, op1=OP.add),
                        reads=[PKa(1, 0)], writes=["btmp"])
                    yield
                ctx.emit("T", lambda t: t.transpose(
                    out=P[1][:, 0:128], in_=btmp[:], identity=c_id[:]),
                    reads=["btmp", "c_id"], writes=[PKa(1, 0)])
                yield
                ctx.emit("V", lambda v: v.tensor_copy(
                    out=bt2[:], in_=P[1][:, 0:128]), reads=[PKa(1, 0)], writes=["bt2"])
                yield
                ctx.dma("Q", bias_ext.rearrange("(c p) -> c p", p=128),
                        bt2[0:NBE, :], reads=["bt2"], writes=["bias_ext"])
                yield

            def qstats_gen():
                for tc in range(TC):
                    xs = x[:, tc * 1024:(tc + 1) * 1024]
                    ctx.emit("V", lambda v: v.tensor_scalar(
                        out=lndump[:], in0=xs, scalar1=m_st[:, tc:tc + 1],
                        scalar2=r_st[:, tc:tc + 1], op0=OP.subtract, op1=OP.mult),
                        reads=["x%d" % tc, "m_st", "r_st"], writes=["lndump"])
                    yield
                    ctx.emit("V", lambda v: v.tensor_reduce(
                        out=amx_a[:, tc:tc + 1], in_=lndump[:], axis=AX.X,
                        op=OP.max, apply_absolute_value=True),
                        reads=["lndump"], writes=["amx_a"])
                    yield
                    ctx.emit("S", lambda s: s.activation(
                        out=lndump[:], in_=lndump[:], func=AF.Square,
                        accum_out=stot_a[:, tc:tc + 1]),
                        reads=["lndump"], writes=["lndump", "stot_a"])
                    yield
                ctx.emit("V", lambda v: v.tensor_scalar(
                    out=stot_a[:], in0=stot_a[:], scalar1=1.0 / 1024.0,
                    scalar2=RMS_EPS, op0=OP.mult, op1=OP.add),
                    reads=["stot_a"], writes=["stot_a"])
                yield
                ctx.emit("V", lambda v: v.reciprocal(out=stot_a[:], in_=stot_a[:]),
                         reads=["stot_a"], writes=["stot_a"])
                yield
                ctx.emit("S", lambda s: s.activation(
                    out=stot_a[:], in_=stot_a[:], func=AF.Sqrt),
                    reads=["stot_a"], writes=["stot_a"])
                yield
                ctx.emit("V", lambda v: v.tensor_tensor(
                    out=ct_a[:], in0=amx_a[:], in1=stot_a[:], op=OP.mult),
                    reads=["amx_a", "stot_a"], writes=["ct_a"])
                yield
                ctx.emit("V", lambda v: v.tensor_scalar(
                    out=ct_a[:], in0=ct_a[:], scalar1=ACT_EPS, scalar2=None,
                    op0=OP.max), reads=["ct_a"], writes=["ct_a"])
                yield
                ctx.emit("V", lambda v: v.reciprocal(out=amx_a[:], in_=ct_a[:]),
                         reads=["ct_a"], writes=["amx_a"])
                yield
                ctx.emit("V", lambda v: v.tensor_tensor(
                    out=stot_a[:], in0=stot_a[:], in1=amx_a[:], op=OP.mult),
                    reads=["stot_a", "amx_a"], writes=["stot_a"])
                yield
                ctx.emit("V", lambda v: v.tensor_scalar(
                    out=stot_a[:], in0=stot_a[:], scalar1=127.0, scalar2=None,
                    op0=OP.mult), reads=["stot_a"], writes=["stot_a"])
                yield
                ctx.emit("V", lambda v: v.tensor_scalar(
                    out=cq2_a[:], in0=ct_a[:], scalar1=wsb2[:], scalar2=None,
                    op0=OP.mult), reads=["ct_a", "wsb2"], writes=["cq2_a"])
                yield
                ctx.emit("V", lambda v: v.tensor_scalar(
                    out=cq2_a[:], in0=cq2_a[:], scalar1=1.0 / 127.0, scalar2=None,
                    op0=OP.mult), reads=["cq2_a"], writes=["cq2_a"])
                yield
                ctx.emit("V", lambda v: v.tensor_tensor(
                    out=amx_a[:], in0=r_st[:], in1=stot_a[:], op=OP.mult),
                    reads=["r_st", "stot_a"], writes=["amx_a"])
                yield
                ctx.emit("V", lambda v: v.scalar_tensor_tensor(
                    out=ct_a[:], in0=m_st[:], scalar=-1.0, in1=amx_a[:],
                    op0=OP.mult, op1=OP.mult),
                    reads=["m_st", "amx_a"], writes=["ct_a"])
                yield

            def ln2_gen():
                for tc in range(TC):
                    for _ in ln_xn_gen(tc, pingpong=True):
                        yield

            weave([ln2_gen(), biasext_gen(), qstats_gen()])
            ctx.barrier()

        with ExitStack() as st1:

            bias2_b = SB(st1, "bias2_b", [128, D], F32)
            stAtt = st1.enter_context(ExitStack())
            qf = SB(stAtt, "qf", [128, T], BF16)
            kf = SB(stAtt, "kf", [128, T], BF16)
            with ExitStack() as stq:
                qk_f = SB(stq, "qk_f", [128, T], F32)
                wqkb = SB(stq, "wqkb", [128, DC * DQK], BF16)
                ctx.dma("Q", wqkb[:].rearrange("p (dc f) -> p dc f", f=DQK),
                        qkcache.rearrange("(dc p) f -> p dc f", p=128),
                        reads=["qkcache"], writes=["wqkb"])
                bqk_t = SB(stq, "bqk_t", [128, 1], F32)
                ctx.dma("Q", bqk_t[:], bqk.rearrange("(p o) -> p o", o=1),
                        writes=["bqk_t"])
                g0 = SB(stq, "g0", [128, 1], F32)
                g1 = SB(stq, "g1", [128, 1], F32)
                b0 = SB(stq, "b0", [128, 1], F32)
                b1 = SB(stq, "b1", [128, 1], F32)
                ctx.dma("Q", g0[:], RAP(tensor=gamma.tensor, offset=0,
                                        ap=[[1, 128], [1, 1]]), writes=["g0"])
                ctx.dma("Q", g1[:], RAP(tensor=gamma.tensor, offset=128,
                                        ap=[[1, 128], [1, 1]]), writes=["g1"])
                ctx.dma("Q", b0[:], RAP(tensor=beta.tensor, offset=0,
                                        ap=[[1, 128], [1, 1]]), writes=["b0"])
                ctx.dma("Q", b1[:], RAP(tensor=beta.tensor, offset=128,
                                        ap=[[1, 128], [1, 1]]), writes=["b1"])
                bcast_row_into(bias2_b, stq, "bias2_b",
                               bias2.rearrange("(o d) -> o d", o=1), D, 3)
                cqs = 1.0 / (float(S) * STD)
                ctx.emit("V", lambda v: v.tensor_scalar(
                    out=g0[:], in0=g0[:], scalar1=cqs, scalar2=None, op0=OP.mult),
                    reads=["g0"], writes=["g0"])
                ctx.emit("V", lambda v: v.tensor_scalar(
                    out=b0[:], in0=b0[:], scalar1=cqs, scalar2=None, op0=OP.mult),
                    reads=["b0"], writes=["b0"])
                for it in range(NIT):
                    for dc in range(DC):
                        ctx.emit("T", lambda t: t.matmul(
                            out=P[0][:, 0:512],
                            lhsT=wqkb[:, dc * DQK:(dc + 1) * DQK],
                            rhs=xnT[:, dc * T + it * 512: dc * T + (it + 1) * 512],
                            start=(dc == 0), stop=(dc == DC - 1)),
                            reads=["wqkb", "xnT"], writes=[PKa(0, 0)], accum=(dc > 0))
                    ctx.emit("S", lambda s: s.activation(
                        out=qk_f[:, it * 512:(it + 1) * 512], in_=P[0][:, 0:512],
                        func=AF.Silu, bias=bqk_t[:], scale=1.0),
                        reads=[PKa(0, 0), "bqk_t"], writes=["qk_f"])
                ctx.emit("V", lambda v: v.tensor_scalar(
                    out=qf[:], in0=qk_f[:], scalar1=g0[:], scalar2=b0[:],
                    op0=OP.mult, op1=OP.add),
                    reads=["qk_f", "g0", "b0"], writes=["qf"])
                ctx.emit("V", lambda v: v.tensor_scalar(
                    out=kf[:], in0=qk_f[:], scalar1=g1[:], scalar2=b1[:],
                    op0=OP.mult, op1=OP.add),
                    reads=["qk_f", "g1", "b1"], writes=["kf"])
                ctx.barrier()

            v_tok = SB(stAtt, "v_tok", [128, TC * 1024], BF16)
            with ExitStack() as stv:
                bv_b = bcast_row(stv, "bv_b", bv.rearrange("(o d) -> o d", o=1),
                                 DV, 2)
                wv_h = SB(stv, "wv_h", [128, DC * 512], BF16)
                vtmp = SB(stv, "vtmp", [128, 512], F32)
                for h in range(2):
                    ctx.dma("Q", wv_h[:].rearrange("p (dc f) -> p dc f", f=512),
                            vcache.rearrange("(dc p) f -> p dc f", p=128)
                            [:, :, h * 512:(h + 1) * 512],
                            reads=["vcache"], writes=["wv_h"])
                    for tc in range(TC):
                        for dc in range(DC):
                            ctx.emit("T", lambda t: t.matmul(
                                out=P[h][:, 0:512],
                                lhsT=xnT[:, dc * T + tc * 128: dc * T + (tc + 1) * 128],
                                rhs=wv_h[:, dc * 512:(dc + 1) * 512],
                                start=(dc == 0), stop=(dc == DC - 1)),
                                reads=["xnT", "wv_h"], writes=[PKa(h, 0)], accum=(dc > 0))
                        ctx.emit("V", lambda v: v.scalar_tensor_tensor(
                            out=vtmp[:], in0=P[h][:, 0:512], scalar=1.0,
                            in1=bv_b[:, h * 512:(h + 1) * 512],
                            op0=OP.mult, op1=OP.add),
                            reads=[PKa(h, 0), "bv_b"], writes=["vtmp"])
                        ctx.emit("S", lambda s: s.activation(
                            out=v_tok[:, tc * 1024 + h * 512:
                                      tc * 1024 + (h + 1) * 512],
                            in_=vtmp[:], func=AF.Silu),
                            reads=["vtmp"], writes=["v_tok"])
                ctx.barrier()

            xnT_stk.close()
            wTb = SB(st1, "wTb", [128, DC * 1024], BF16)

            def w8_gen():
                w8f, w8f_k = yq, "yq"
                for oc in range(DC):
                    ctx.dma("G", w8f[:],
                            w_int8.rearrange("(oc p) d -> oc p d", p=128)[oc],
                            writes=[w8f_k])
                    yield
                    for dc in range(DC):
                        bk = dc % 2
                        pcol = bk * 512 + ((dc % 4) // 2) * 128
                        ctx.emit("T", lambda t: t.transpose(
                            out=P[3][:, pcol:pcol + 128],
                            in_=w8f[:, dc * 128:(dc + 1) * 128], identity=c_id[:]),
                            reads=[w8f_k, "c_id"], writes=[PKa(3, bk)])
                        yield
                        ctx.emit("S", lambda s: s.copy(
                            out=wTb[:, dc * 1024 + oc * 128:
                                    dc * 1024 + (oc + 1) * 128],
                            in_=P[3][:, pcol:pcol + 128]),
                            reads=[PKa(3, bk)], writes=["wTb"])
                        yield

            yq = SB(st1, "yq", [128, 1024], F32)
            ti32 = SB(st1, "ti32", [128, 1024], I32)
            xqT = SB(st1, "xqT", [128, DC * 128], BF16)
            yq1 = SB(st1, "yq1", [128, 1024], F32)
            yq_s = [(yq, "yq"), (yq1, "yq1")]
            ti_s = [(ti32, "ti32"), (ti32, "ti32")]
            dif_s = [(xn_f, "xn_f"), (lndump, "lndump")]
            xqT_s = [(xqT, "xqT"), (xqT, "xqT")]
            ctx.barrier()

            def ffn_quant(tc):
                p = tc % 2
                yqb, yq_k = yq_s[p]
                tib, ti_k = ti_s[p]
                difb, dif_k = dif_s[p]
                ctx.emit("S", lambda s: s.activation(
                    out=yqb[:], in_=x[:, tc * 1024:(tc + 1) * 1024],
                    func=AF.Identity,
                    scale=amx_a[:, tc:tc + 1], bias=ct_a[:, tc:tc + 1]),
                    reads=["x%d" % tc, "amx_a", "ct_a"], writes=[yq_k])
                yield
                ctx.emit("S", lambda s: s.copy(out=tib[:], in_=yqb[:]),
                         reads=[yq_k], writes=[ti_k])
                yield
                ctx.emit("V", lambda v: v.scalar_tensor_tensor(
                    out=difb[:], in0=tib[:], scalar=-1.0, in1=yqb[:],
                    op0=OP.mult, op1=OP.add), reads=[ti_k, yq_k], writes=[dif_k])
                yield
                ctx.emit("V", lambda v: v.tensor_scalar(
                    out=yqb[:], in0=difb[:], scalar1=-0.5, scalar2=None,
                    op0=OP.is_le), reads=[dif_k], writes=[yq_k])
                yield
                ctx.emit("V", lambda v: v.tensor_scalar(
                    out=difb[:], in0=difb[:], scalar1=0.5, scalar2=None,
                    op0=OP.is_ge), reads=[dif_k], writes=[dif_k])
                yield
                ctx.emit("V", lambda v: v.scalar_tensor_tensor(
                    out=difb[:], in0=yqb[:], scalar=-1.0, in1=difb[:],
                    op0=OP.mult, op1=OP.add), reads=[yq_k, dif_k], writes=[dif_k])
                yield
                ctx.emit("V", lambda v: v.tensor_tensor(
                    out=yqb[:], in0=tib[:], in1=difb[:], op=OP.add),
                    reads=[ti_k, dif_k], writes=[yq_k])
                yield
                ctx.emit("V", lambda v: v.tensor_scalar(
                    out=yqb[:], in0=yqb[:], scalar1=-128.0, scalar2=127.0,
                    op0=OP.max, op1=OP.min), reads=[yq_k], writes=[yq_k])
                yield
                ctx.emit("V", lambda v: v.tensor_scalar(
                    out=yqb[:], in0=yqb[:], scalar1=cq2_a[:, tc:tc + 1],
                    scalar2=None, op0=OP.mult),
                    reads=[yq_k, "cq2_a"], writes=[yq_k])
                yield

            def ffn_mm(tc):
                p = tc % 2
                yqb, yq_k = yq_s[p]
                difb, dif_k = dif_s[p]
                xqTb, xqT_k = xqT_s[p]
                g2t = difb[:, 0:512]
                g2t_k = dif_k
                for dc in range(DC):
                    bk = dc % 2
                    pcol = bk * 512 + ((dc % 4) // 2) * 128
                    ctx.emit("T", lambda t: t.transpose(
                        out=P[3][:, pcol:pcol + 128],
                        in_=yqb[:, dc * 128:(dc + 1) * 128], identity=c_id[:]),
                        reads=[yq_k, "c_id"], writes=[PKa(3, bk)])
                    yield
                    ctx.emit("S", lambda s: s.copy(
                        out=xqTb[:, dc * 128:(dc + 1) * 128],
                        in_=P[3][:, pcol:pcol + 128]),
                        reads=[PKa(3, bk)], writes=[xqT_k])
                    yield
                for h in range(2):
                    for dc in range(DC):
                        ctx.emit("T", lambda t: t.matmul(
                            out=P[3][:, h * 512:(h + 1) * 512],
                            lhsT=xqTb[:, dc * 128:(dc + 1) * 128],
                            rhs=wTb[:, dc * 1024 + h * 512:
                                    dc * 1024 + h * 512 + 512],
                            start=(dc == 0), stop=(dc == DC - 1)),
                            reads=[xqT_k, "wTb"],
                            writes=[PKa(3, h)], accum=(dc > 0))
                        yield
                    ctx.emit("V", lambda v: v.scalar_tensor_tensor(
                        out=g2t, in0=P[3][:, h * 512:(h + 1) * 512], scalar=1.0,
                        in1=bias2_b[:, h * 512:(h + 1) * 512],
                        op0=OP.mult, op1=OP.add),
                        reads=[PKa(3, h), "bias2_b"], writes=[g2t_k])
                    yield
                    ctx.emit("S", lambda s: s.activation(
                        out=g2t, in_=g2t, func=AF.Silu),
                        reads=[g2t_k], writes=[g2t_k])
                    yield
                    ctx.emit("V", lambda v: v.tensor_tensor(
                        out=g2t, in0=g2t,
                        in1=s_tok[:, tc * 1024 + h * 512:
                                  tc * 1024 + (h + 1) * 512], op=OP.mult),
                        reads=[g2t_k, "s%d" % tc], writes=[g2t_k])
                    yield
                    ctx.emit("V", lambda v: v.tensor_tensor(
                        out=x[:, tc * 1024 + h * 512: tc * 1024 + (h + 1) * 512],
                        in0=x[:, tc * 1024 + h * 512: tc * 1024 + (h + 1) * 512],
                        in1=g2t, op=OP.add),
                        reads=[g2t_k, "x%d" % tc], writes=["x%d" % tc])
                    yield

            def interleave(*gs):
                gs = [g for g in gs if g is not None]
                while gs:
                    alive = []
                    for g in gs:
                        try:
                            next(g)
                            alive.append(g)
                        except StopIteration:
                            pass
                    gs = alive
                    yield

            with ExitStack() as sts:
                c_j = SB(sts, "c_j", [128, 128], F32)
                ctx.dma("Q", c_j[:], jrev, writes=["c_j"])
                c_mk = SB(sts, "c_mk", [128, 896], BF16)
                ctx.dma("Q", c_mk[:], cmasks, writes=["c_mk"])
                strip = SB(sts, "strip", [128, TC * 512], BF16)
                t1bs = [SB(sts, "t1b%d" % i, [128, 512], F32) for i in range(2)]
                biasT = SB(sts, "biasT", [128, 512], F32)
                zt = SB(sts, "zt", [128, 512], F32)
                dent, dent_k = biasT, "biasT"
                def simav_gen(it):
                    i0 = it * 512
                    jcmax = min(TC - 1, it * 4 + 3)
                    for jc in range(jcmax + 1):
                        base = (i0 - jc * 128) + 513
                        t1b, t1b_k = t1bs[jc % 2], "t1b%d" % (jc % 2)
                        ctx.dma("S", t1b[:],
                                RAP(tensor=bias_ext.tensor, offset=base,
                                    ap=[[1, 128], [1, 512]]),
                                reads=["bias_ext"], writes=[t1b_k],
                                slot=t1b_k)
                        jp = (jc % 2) * 512
                        ctx.emit("T", lambda t: t.matmul(
                            out=P[1][:, jp:jp + 512], lhsT=c_j[:], rhs=t1b[:],
                            start=True, stop=True),
                            reads=["c_j", t1b_k], writes=[PKa(1, jc % 2)])
                        ctx.emit("S", lambda s: s.copy(
                            out=biasT[:], in_=P[1][:, jp:jp + 512]),
                            reads=[PKa(1, jc % 2)], writes=["biasT"])
                        ctx.emit("T", lambda t: t.matmul(
                            out=P[0][:, jp:jp + 512],
                            lhsT=kf[:, jc * 128:(jc + 1) * 128],
                            rhs=qf[:, it * 512:(it + 1) * 512],
                            start=True, stop=True),
                            reads=["kf", "qf"], writes=[PKa(0, jc % 2)])
                        ctx.emit("V", lambda v: v.scalar_tensor_tensor(
                            out=zt[:], in0=P[0][:, jp:jp + 512], scalar=1.0,
                            in1=biasT[:], op0=OP.mult, op1=OP.add),
                            reads=[PKa(0, jc % 2), "biasT"], writes=["zt"])
                        ctx.emit("S", lambda s: s.activation(
                            out=strip[:, jc * 512:(jc + 1) * 512], in_=zt[:],
                            func=AF.Gelu),
                            reads=["zt"], writes=["strip"])
                        ctx.emit("V", lambda v: v.scalar_tensor_tensor(
                            out=dent[:], in0=zt[:], scalar=0.0, in1=zt[:],
                            op0=OP.is_equal, op1=OP.add),
                            reads=["zt"], writes=[dent_k])
                        ctx.emit("V", lambda v: v.reciprocal(
                            out=dent[:], in_=dent[:]), reads=[dent_k], writes=[dent_k])
                        ctx.emit("V", lambda v: v.tensor_tensor(
                            out=strip[:, jc * 512:(jc + 1) * 512],
                            in0=strip[:, jc * 512:(jc + 1) * 512],
                            in1=dent[:], op=OP.mult),
                            reads=[dent_k, "strip"], writes=["strip"])
                        mq = it * 4 - jc
                        if mq < 1:
                            mi = -mq
                            ctx.emit("V", lambda v: v.tensor_tensor(
                                out=strip[:, jc * 512:(jc + 1) * 512],
                                in0=strip[:, jc * 512:(jc + 1) * 512],
                                in1=c_mk[:, (3 - mi) * 128:
                                         (3 - mi) * 128 + 512], op=OP.mult),
                                reads=["c_mk", "strip"], writes=["strip"])
                    for lc in range(4):
                        tci = it * 4 + lc
                        for jc in range(tci + 1):
                            for h in range(2):
                                ctx.emit("T", lambda t: t.matmul(
                                    out=P[2][:, h * 512:(h + 1) * 512],
                                    lhsT=strip[:, jc * 512 + lc * 128:
                                               jc * 512 + (lc + 1) * 128],
                                    rhs=v_tok[:, jc * 1024 + h * 512:
                                              jc * 1024 + h * 512 + 512],
                                    start=(jc == 0), stop=(jc == tci)),
                                    reads=["strip", "v_tok"], writes=PKf(2),
                                    accum=(jc > 0))
                                yield
                        ctx.emit("S", lambda s: s.copy(
                            out=s_tok[:, tci * 1024:(tci + 1) * 1024], in_=P[2][:]),
                            reads=PKf(2), writes=["s%d" % tci])
                        yield

                def ffn_chain(it):
                    for lc in range(4):
                        tc = it * 4 + lc
                        if tc == 0:
                            for _ in ffn_quant(0):
                                yield
                        nq = ffn_quant(tc + 1) if tc + 1 < TC else None
                        for _ in interleave(ffn_mm(tc), nq):
                            yield

                pend = []
                for it in range(NIT):
                    gens = [simav_gen(it)] + pend
                    if it == 0:
                        gens.append(w8_gen())
                    weave(gens)
                    pend = [ffn_chain(it)]
                weave(pend)
        for tc in range(TC):
            ctx.dma("Q", x_out.rearrange("(tc p) d -> p tc d", p=128)[:, tc],
                    x[:, tc * 1024:(tc + 1) * 1024],
                    reads=["x%d" % tc], writes=[], slot="xout")
        ctx.eng["Q"].wait_ge(ctx.sems["D:xout"], ctx.dmaval["D:xout"])


def _host_consts(S):
    consts = {}
    consts["c_ident"] = np.eye(128, dtype=np.float32)
    consts["c_jrev"] = np.eye(128, dtype=np.float32)[::-1].copy()
    consts["c_ones1"] = np.ones((1, 128), dtype=np.float32)
    consts["c_onesk"] = np.ones((128, 1), dtype=ml_dtypes.bfloat16)
    consts["c_iota8"] = np.broadcast_to(
        np.arange(E, dtype=np.float32), (128, E)).copy()
    consts["c_iotaB"] = consts["c_iota8"] + np.float32(BIG)
    sel2 = np.zeros((2, 128), dtype=ml_dtypes.bfloat16)
    sel2[0, 0:64] = 1.0
    sel2[1, 64:128] = 1.0
    consts["c_sel2"] = sel2
    s16 = np.zeros((16, 8 * 128), dtype=np.float32)
    for c in range(8):
        for p in range(128):
            s16[2 * c + p // 64, c * 128 + p] = 1.0
    consts["c_sel16"] = s16.astype(ml_dtypes.bfloat16)
    jj = np.arange(128)[:, None]
    ww = np.arange(896)[None, :]
    masks = (jj <= ww - 384).astype(np.float32)
    consts["c_masks"] = masks.astype(ml_dtypes.bfloat16)
    n_ext = S + 768
    oh = np.zeros((NB, n_ext), dtype=np.float32)
    max_exact = NB // 2
    for dp in range(n_ext):
        delta = dp - 640
        if delta < 0 or delta >= S:
            continue
        if delta < max_exact:
            b = delta
        else:
            n_safe = np.float32(max(delta, 1))
            val = max_exact + int(
                np.float32(np.log(n_safe / np.float32(max_exact))
                           / np.float32(math.log(MAXD / max_exact))
                           * np.float32(NB - max_exact)))
            b = min(val, NB - 1)
        oh[b, dp] = 1.0
    consts["c_oh"] = oh
    return consts


def kernel(L0_attn_x, L0_attn_Wqkv, L0_attn_Wout, L0_attn_bout,
           L0_ffn_router_w, L0_ffn_gate_proj, L0_ffn_up_proj, L0_ffn_down_proj,
           L1_attn_wqk, L1_attn_bqk, L1_attn_gamma, L1_attn_beta,
           L1_attn_wv, L1_attn_bv, L1_attn_rel_bias,
           L1_ffn_w_int8, L1_ffn_weight_scale, L1_ffn_bias):
    x = np.ascontiguousarray(np.asarray(L0_attn_x, dtype=np.float32))
    B, T, D_ = x.shape
    assert B == NCORES and D_ == D
    nc = bass.Bass("TRN2", target_bir_lowering=False)
    build_program(nc, T)
    consts = _host_consts(T)
    common = dict(
        L0_Wqkv=np.ascontiguousarray(np.asarray(L0_attn_Wqkv, np.float32)),
        L0_Wout=np.ascontiguousarray(np.asarray(L0_attn_Wout, np.float32)),
        L0_bout=np.ascontiguousarray(np.asarray(L0_attn_bout, np.float32)),
        L0_router=np.ascontiguousarray(np.asarray(L0_ffn_router_w, np.float32)),
        L0_gate=np.ascontiguousarray(np.asarray(L0_ffn_gate_proj, np.float32)),
        L0_up=np.ascontiguousarray(np.asarray(L0_ffn_up_proj, np.float32)),
        L0_down=np.ascontiguousarray(np.asarray(L0_ffn_down_proj, np.float32)),
        L1_wqk=np.ascontiguousarray(np.asarray(L1_attn_wqk, np.float32)),
        L1_bqk=np.ascontiguousarray(np.asarray(L1_attn_bqk, np.float32)),
        L1_gamma=np.ascontiguousarray(np.asarray(L1_attn_gamma, np.float32)),
        L1_beta=np.ascontiguousarray(np.asarray(L1_attn_beta, np.float32)),
        L1_wv=np.ascontiguousarray(np.asarray(L1_attn_wv, np.float32)),
        L1_bv=np.ascontiguousarray(np.asarray(L1_attn_bv, np.float32)),
        L1_rel=np.ascontiguousarray(np.asarray(L1_attn_rel_bias, np.float32)),
        L1_w8=np.ascontiguousarray(np.asarray(L1_ffn_w_int8, np.int32)),
        L1_ws=np.asarray(L1_ffn_weight_scale, np.float32).reshape(1),
        L1_bias=np.ascontiguousarray(np.asarray(L1_ffn_bias, np.float32)),
        **consts,
    )
    in_maps = [dict(common, x_in=np.ascontiguousarray(x[c])) for c in range(NCORES)]
    res = run_bass_kernel_spmd(nc, in_maps, core_ids=list(range(NCORES)))
    out = np.stack([np.asarray(res.results[c]["x_out"]) for c in range(NCORES)],
                   axis=0)
    return out.astype(np.float32)
